# version 02
# baseline (speedup 1.0000x reference)
import math
from contextlib import ExitStack

import numpy as np

import concourse.bass as bass
import concourse.mybir as mybir
from concourse.alu_op_type import AluOpType as OP
from concourse.bass_utils import run_bass_kernel_spmd

F32 = mybir.dt.float32
BF16 = mybir.dt.bfloat16
I32 = mybir.dt.int32
AF = mybir.ActivationFunctionType
AX = mybir.AxisListType.X

D = 2048
H = 16
HD = 128
S = 4096
NROLE = 4
BOUNDS = [0, 1024, 2048, 3072, 4096]
T = 1024
NQT = T // 128
FFN = 1024
E = 8
SCALE = float(1.0 / math.sqrt(HD))
EPS = 1e-5
NEG = -1e30
BIGI = 65536.0
BIGS = 1e6
VW = H * 129


def _nj(role, qb):
    return (BOUNDS[role] + qb * 512) // 512 + 1


def build_nc(phases=99, dbg=False):
    nc = bass.Bass("TRN2", target_bir_lowering=False, debug=False,
                   detect_race_conditions=False)
    ap = lambda t: t.ap()
    role_t = ap(nc.dram_tensor("role", [1, 1], I32, kind="ExternalInput"))
    x_t = ap(nc.dram_tensor("x", [S, D], F32, kind="ExternalInput"))
    xr_t = ap(nc.dram_tensor("xr", [T, D], F32, kind="ExternalInput"))
    wq_t = ap(nc.dram_tensor("wq", [D, D], F32, kind="ExternalInput"))
    wk_t = ap(nc.dram_tensor("wk", [D, D], F32, kind="ExternalInput"))
    wv_t = ap(nc.dram_tensor("wv", [D, D], F32, kind="ExternalInput"))
    wo_t = ap(nc.dram_tensor("wo", [D, D], F32, kind="ExternalInput"))
    bo_t = ap(nc.dram_tensor("bo_b", [128, D], F32, kind="ExternalInput"))
    wg_t = ap(nc.dram_tensor("wg", [E, D], F32, kind="ExternalInput"))
    w1_t = ap(nc.dram_tensor("w1", [E, FFN, D], F32, kind="ExternalInput"))
    w2_t = ap(nc.dram_tensor("w2", [E, D, FFN], F32, kind="ExternalInput"))
    w3_t = ap(nc.dram_tensor("w3", [E, FFN, D], F32, kind="ExternalInput"))
    fb_t = ap(nc.dram_tensor("fbias_b", [128, E], F32, kind="ExternalInput"))
    id_t = ap(nc.dram_tensor("ident", [128, 128], F32, kind="ExternalInput"))
    dm_t = ap(nc.dram_tensor("dmask", [128, 4 * 512], F32, kind="ExternalInput"))
    io_t = ap(nc.dram_tensor("iota8", [128, E], F32, kind="ExternalInput"))
    ib_t = ap(nc.dram_tensor("iota8b", [128, E], F32, kind="ExternalInput"))
    ep_t = ap(nc.dram_tensor("epsc", [128, 1], F32, kind="ExternalInput"))
    out_t = ap(nc.dram_tensor("out", [T, D], F32, kind="ExternalOutput"))
    SCR = "ExternalOutput" if dbg else "Internal"
    kscr = ap(nc.dram_tensor("k_scr", [8, 128, H * 512], BF16, kind=SCR))
    vscr = ap(nc.dram_tensor("v_scr", [8, 128, 4 * VW], BF16, kind=SCR))
    qscr = ap(nc.dram_tensor("q_scr", [2, 128, H * 512], BF16, kind=SCR))
    cscr = ap(nc.dram_tensor("ctx_scr", [NQT, 128, VW], F32, kind=SCR))
    yscr = ap(nc.dram_tensor("y1_scr", [NQT, 128, D], F32, kind=SCR))

    with ExitStack() as es:
        role_sb = es.enter_context(nc.sbuf_tensor("role_sb", [1, 1], I32))
        identb = es.enter_context(nc.sbuf_tensor("identb", [128, 128], BF16))
        dmask = es.enter_context(nc.sbuf_tensor("dmask_sb", [128, 4 * 512], F32))
        iota = es.enter_context(nc.sbuf_tensor("iota_sb", [128, E], F32))
        iotab = es.enter_context(nc.sbuf_tensor("iotab_sb", [128, E], F32))
        fbias = es.enter_context(nc.sbuf_tensor("fbias_sb", [128, E], F32))
        epssb = es.enter_context(nc.sbuf_tensor("eps_sb", [128, 1], F32))
        pA = es.enter_context(nc.psum_tensor("pA", [128, 512], F32))
        pB = es.enter_context(nc.psum_tensor("pB", [128, 512], F32))
        pC = es.enter_context(nc.psum_tensor("pC", [128, 512], F32))
        pD = es.enter_context(nc.psum_tensor("pD", [128, 512], F32))
        pE = es.enter_context(nc.psum_tensor("pE", [128, 512], F32))
        pF = es.enter_context(nc.psum_tensor("pF", [128, 512], F32))
        ptA = es.enter_context(nc.psum_tensor("ptA", [128, 1024], BF16))
        ptB = es.enter_context(nc.psum_tensor("ptB", [128, 1024], BF16))
        ptb = [ptA, ptB]
        SEMN = 16
        sems = [es.enter_context(nc.semaphore(f"s{i}")) for i in range(SEMN)]
        sw = [es.enter_context(nc.semaphore(f"sw{i}")) for i in range(10)]

        with nc.Block() as blk:
            @blk.sync
            def _(s):
                s.dma_start(out=role_sb[:, :], in_=role_t[:, :]).then_inc(sems[0], 16)
                s.dma_start(out=dmask[:, :], in_=dm_t[:, :]).then_inc(sems[0], 16)
                s.dma_start(out=iota[:, :], in_=io_t[:, :]).then_inc(sems[0], 16)
                s.dma_start(out=iotab[:, :], in_=ib_t[:, :]).then_inc(sems[0], 16)
                s.dma_start(out=fbias[:, :], in_=fb_t[:, :]).then_inc(sems[0], 16)
                s.dma_start(out=epssb[:, :], in_=ep_t[:, :]).then_inc(sems[0], 16)
                s.wait_ge(sems[0], 96)
                s.wait_ge(sw[0], 16)
            @blk.gpsimd
            def _(g):
                g.dma_start(out=identb[:, :], in_=id_t[:, :]).then_inc(sw[0], 16)

        regs = {}
        with nc.Block() as blk:
            def _mk(name):
                def _(eng):
                    r = eng.alloc_register("role_r")
                    eng.reg_load(r, role_sb[0:1, 0:1])
                    regs[name] = r
                return _
            blk.tensor(_mk("t")); blk.vector(_mk("v")); blk.scalar(_mk("c"))
            blk.sync(_mk("s")); blk.gpsimd(_mk("g"))

        def clear_block():
            with nc.Block() as blk:
                @blk.sync
                def _(s):
                    for sm in sems:
                        s.sem_clear(sm)

        clear_block()
        if phases <= 0:
            return nc


        with ExitStack() as p1:
            wkb = p1.enter_context(nc.sbuf_tensor("wkb", [128, 16 * D], BF16))
            wvb = p1.enter_context(nc.sbuf_tensor("wvb", [128, 16 * D], BF16))
            xbf = p1.enter_context(nc.sbuf_tensor("xbf", [128, 2 * D], BF16))
            xfm = p1.enter_context(nc.sbuf_tensor("xfm", [128, 16 * 512], BF16))
            ksb = p1.enter_context(nc.sbuf_tensor("ksb", [128, H * 512], BF16))
            vsb = p1.enter_context(nc.sbuf_tensor("vsb", [128, 4 * VW], BF16))

            with nc.Block() as blk:
                @blk.gpsimd
                def _(g):
                    for cc in range(16):
                        g.dma_start(out=wkb[:, cc * D:(cc + 1) * D],
                                    in_=wk_t[cc * 128:(cc + 1) * 128, :]).then_inc(sw[1], 16)
                    for cc in range(16):
                        g.dma_start(out=wvb[:, cc * D:(cc + 1) * D],
                                    in_=wv_t[cc * 128:(cc + 1) * 128, :]).then_inc(sw[1], 16)
                @blk.sync
                def _(s):
                    s.wait_ge(sw[1], 512)
            clear_block()

            sXd, sTx, sVx, sTk, sVk, sTv, sVv, sSt = sems[0:8]

            def proj_gpsimd(g, tok0, nch, sx):
                for i in range(4 * nch):
                    if i >= 2:
                        g.wait_ge(sTx, 2 * (i - 1))
                    r0 = tok0 + i * 128
                    g.dma_start(out=xbf[:, (i % 2) * D:(i % 2 + 1) * D],
                                in_=x_t[r0:r0 + 128, :]).then_inc(sx, 16)

            def proj_x_tensor(g, c, sx):
                for ts in range(4):
                    i = 4 * c + ts
                    g.wait_ge(sx, 16 * (i + 1))
                    for g2 in range(2):
                        k = 2 * i + g2
                        if k >= 2:
                            g.wait_ge(sVx, k - 1)
                        for j in range(8):
                            cc = 8 * g2 + j
                            r = g.transpose(
                                ptb[k % 2][:, (cc % 8) * 128:(cc % 8) * 128 + 128],
                                xbf[:, (i % 2) * D + cc * 128:(i % 2) * D + cc * 128 + 128],
                                identb[:, :])
                        r.then_inc(sTx, 1)

            def proj_x_vector(g, c):
                for ts in range(4):
                    i = 4 * c + ts
                    for g2 in range(2):
                        k = 2 * i + g2
                        g.wait_ge(sTx, k + 1)
                        r = g.tensor_copy(
                            bass.AP(xfm, 8 * g2 * 512 + ts * 128,
                                    [[16 * 512, 128], [512, 8], [1, 128]]),
                            ptb[k % 2][:, 0:1024].rearrange("p (b c) -> p b c", c=128))
                        r.then_inc(sVx, 1)

            def kproj_tensor(g, c, wbuf):
                g.wait_ge(sVx, 8 * (c + 1))
                for h in range(H):
                    j = 16 * c + h
                    if j >= 2:
                        g.wait_ge(sVk, j - 1)
                    p = [pA, pB][j % 2]
                    for cc in range(16):
                        r = g.matmul(p[:, :],
                                     wbuf[:, cc * D + h * 128:cc * D + h * 128 + 128],
                                     xfm[:, cc * 512:(cc + 1) * 512],
                                     start=(cc == 0), stop=(cc == 15))
                    r.then_inc(sTk, 1)

            def kproj_vector(g, c):
                for h in range(H):
                    j = 16 * c + h
                    g.wait_ge(sTk, j + 1)
                    g.tensor_copy(ksb[:, h * 512:(h + 1) * 512],
                                  [pA, pB][j % 2][:, :]).then_inc(sVk, 1)

            def p1_tensor(g, nch):
                for c in range(nch):
                    proj_x_tensor(g, c, sw[2])
                    kproj_tensor(g, c, wkb)
                    for ts in range(4):
                        for gp in range(4):
                            j = 16 * c + 4 * ts + gp
                            if j >= 2:
                                g.wait_ge(sVv, j - 1)
                            p = [pC, pD][j % 2]
                            for cc in range(16):
                                r = g.matmul(
                                    p[:, :],
                                    xfm[:, cc * 512 + ts * 128:cc * 512 + ts * 128 + 128],
                                    wvb[:, cc * D + gp * 512:cc * D + gp * 512 + 512],
                                    start=(cc == 0), stop=(cc == 15))
                            r.then_inc(sTv, 1)

            def p1_vector(g, nch):
                for ts in range(4):
                    g.memset(bass.AP(vsb, ts * VW + 128,
                                     [[4 * VW, 128], [129, H], [1, 1]]), 1.0)
                for c in range(nch):
                    if c > 0:
                        g.wait_ge(sTk, 16 * c)
                        g.wait_ge(sTv, 16 * c)
                        g.wait_ge(sSt, 32 * c)
                    proj_x_vector(g, c)
                    kproj_vector(g, c)
                    for ts in range(4):
                        for gp in range(4):
                            j = 16 * c + 4 * ts + gp
                            g.wait_ge(sTv, j + 1)
                            r = g.tensor_copy(
                                bass.AP(vsb, ts * VW + 4 * gp * 129,
                                        [[4 * VW, 128], [129, 4], [1, 128]]),
                                [pC, pD][j % 2][:, :].rearrange("p (b c) -> p b c", c=128))
                            r.then_inc(sVv, 1)

            def p1_sync(g, nch):
                for c in range(nch):
                    g.wait_ge(sVk, 16 * (c + 1))
                    g.wait_ge(sVv, 16 * (c + 1))
                    g.dma_start(out=kscr[c], in_=ksb[:, :]).then_inc(sSt, 16)
                    g.dma_start(out=vscr[c], in_=vsb[:, :]).then_inc(sSt, 16)
                g.wait_ge(sSt, 32 * nch)

            with nc.Block() as blk:
                @blk.gpsimd
                def _(g):
                    for role in range(NROLE):
                        with g.If_eq(regs["g"], role):
                            proj_gpsimd(g, 0, 2 * (role + 1), sw[2])
                @blk.tensor
                def _(t):
                    for role in range(NROLE):
                        with t.If_eq(regs["t"], role):
                            p1_tensor(t, 2 * (role + 1))
                @blk.vector
                def _(v):
                    for role in range(NROLE):
                        with v.If_eq(regs["v"], role):
                            p1_vector(v, 2 * (role + 1))
                @blk.sync
                def _(s):
                    for role in range(NROLE):
                        with s.If_eq(regs["s"], role):
                            p1_sync(s, 2 * (role + 1))
            clear_block()
            if phases <= 1:
                return nc

            with nc.Block() as blk:
                @blk.gpsimd
                def _(g):
                    for cc in range(16):
                        g.dma_start(out=wkb[:, cc * D:(cc + 1) * D],
                                    in_=wq_t[cc * 128:(cc + 1) * 128, :]).then_inc(sw[3], 16)
                @blk.sync
                def _(s):
                    s.wait_ge(sw[3], 256)
            clear_block()

            with nc.Block() as blk:
                @blk.gpsimd
                def _(g):
                    for role in range(NROLE):
                        with g.If_eq(regs["g"], role):
                            proj_gpsimd(g, BOUNDS[role], 2, sw[4])
                @blk.tensor
                def _(t):
                    for c in range(2):
                        proj_x_tensor(t, c, sw[4])
                        kproj_tensor(t, c, wkb)
                @blk.vector
                def _(v):
                    for c in range(2):
                        if c > 0:
                            v.wait_ge(sTk, 16 * c)
                            v.wait_ge(sSt, 16 * c)
                        proj_x_vector(v, c)
                        kproj_vector(v, c)
                @blk.sync
                def _(s):
                    for c in range(2):
                        s.wait_ge(sVk, 16 * (c + 1))
                        s.dma_start(out=qscr[c], in_=ksb[:, :]).then_inc(sSt, 16)
                    s.wait_ge(sSt, 32)
            clear_block()
            if phases <= 2:
                return nc

        with ExitStack() as pw:
            y1fm = pw.enter_context(nc.sbuf_tensor("y1fm", [128, 16 * T], BF16,
                                                   side="right"))
            with ExitStack() as p2:
                qbuf = p2.enter_context(nc.sbuf_tensor("qbuf", [128, H * 512], BF16))
                kbuf = p2.enter_context(nc.sbuf_tensor("kbuf", [128, 2 * H * 512], BF16))
                vbuf = p2.enter_context(nc.sbuf_tensor("vbuf", [128, 2 * 4 * VW], BF16))
                psb = p2.enter_context(nc.sbuf_tensor("psb", [128, 2 * 512], BF16))
                ptsb = p2.enter_context(nc.sbuf_tensor("ptsb", [128, 2 * 512], BF16))
                ctx = p2.enter_context(nc.sbuf_tensor("ctx", [128, 4 * VW], F32))

                sK, sTs, sVm, sAe, sTt, sVp, sTc, sVc, sSt2 = sems[0:9]

                def p2_sync(g, role):
                    kl = 0
                    slot_tenant_end = {0: None, 1: None}
                    tiles = 0
                    for qb in range(2):
                        nj = _nj(role, qb)
                        if qb == 1:
                            g.wait_ge(sTs, tiles)
                        g.dma_start(out=qbuf[:, :], in_=qscr[qb]).then_inc(sK, 16)
                        kl += 16
                        for j in range(nj):
                            sl = j % 2
                            if slot_tenant_end[sl] is not None:
                                g.wait_ge(sTc, slot_tenant_end[sl])
                            g.dma_start(out=kbuf[:, sl * H * 512:(sl + 1) * H * 512],
                                        in_=kscr[j]).then_inc(sK, 16)
                            g.dma_start(out=vbuf[:, sl * 4 * VW:(sl + 1) * 4 * VW],
                                        in_=vscr[j]).then_inc(sK, 16)
                            kl += 32
                            slot_tenant_end[sl] = tiles + 64 * (j + 1)
                        tiles += 64 * nj
                        g.wait_ge(sVc, tiles)
                        for qt in range(4):
                            g.dma_start(out=cscr[4 * qb + qt],
                                        in_=ctx[:, qt * VW:(qt + 1) * VW]).then_inc(sSt2, 16)
                    g.wait_ge(sSt2, 128)
                    g.wait_ge(sK, kl)

                def p2_tensor(g, role):
                    u = 0
                    kl = 0
                    for qb in range(2):
                        nj = _nj(role, qb)
                        kl += 16
                        g.wait_ge(sK, kl)
                        for j in range(nj):
                            kl += 32
                            g.wait_ge(sK, kl)
                            for qt in range(4):
                                for h in range(H):
                                    if u >= 2:
                                        g.wait_ge(sAe, u - 1)
                                    ps = [pA, pB][u % 2]
                                    g.matmul(ps[:, :],
                                             qbuf[:, h * 512 + qt * 128:h * 512 + qt * 128 + 128],
                                             kbuf[:, (j % 2) * H * 512 + h * 512:(j % 2) * H * 512 + (h + 1) * 512],
                                             start=True, stop=True).then_inc(sTs, 1)
                                    g.wait_ge(sAe, u + 1)
                                    if u >= 2:
                                        g.wait_ge(sVp, u - 1)
                                    for kk in range(4):
                                        r = g.transpose(
                                            ptb[u % 2][:, kk * 128:kk * 128 + 128],
                                            psb[:, (u % 2) * 512 + kk * 128:(u % 2) * 512 + kk * 128 + 128],
                                            identb[:, :])
                                    r.then_inc(sTt, 1)
                                    g.wait_ge(sVp, u + 1)
                                    if u >= 2:
                                        g.wait_ge(sVc, u - 1)
                                    pc = [pC, pD][u % 2]
                                    vb = (j % 2) * 4 * VW
                                    for kk in range(4):
                                        r = g.matmul(
                                            pc[:, 0:129],
                                            ptsb[:, (u % 2) * 512 + kk * 128:(u % 2) * 512 + kk * 128 + 128],
                                            vbuf[:, vb + kk * VW + h * 129:vb + kk * VW + (h + 1) * 129],
                                            start=(kk == 0), stop=(kk == 3))
                                    r.then_inc(sTc, 1)
                                    u += 1

                def p2_vector(g, role):
                    u = 0
                    nm = 0
                    for qb in range(2):
                        nj = _nj(role, qb)
                        if qb == 1:
                            g.wait_ge(sSt2, 64)
                        for j in range(nj):
                            diag = (j == nj - 1)
                            for qt in range(4):
                                for h in range(H):
                                    if diag:
                                        g.wait_ge(sTs, u + 1)
                                        ps = [pA, pB][u % 2]
                                        g.tensor_tensor(ps[:, :], ps[:, :],
                                                        dmask[:, qt * 512:(qt + 1) * 512],
                                                        op=OP.add).then_inc(sVm, 1)
                                        nm += 1
                                    g.wait_ge(sTt, u + 1)
                                    g.tensor_copy(ptsb[:, (u % 2) * 512:(u % 2) * 512 + 512],
                                                  ptb[u % 2][:, 0:512]).then_inc(sVp, 1)
                                    g.wait_ge(sTc, u + 1)
                                    pc = [pC, pD][u % 2]
                                    dst = ctx[:, qt * VW + h * 129:qt * VW + (h + 1) * 129]
                                    if j == 0:
                                        g.tensor_copy(dst, pc[:, 0:129]).then_inc(sVc, 1)
                                    else:
                                        g.tensor_tensor(dst, pc[:, 0:129], dst,
                                                        op=OP.add).then_inc(sVc, 1)
                                    u += 1

                def p2_scalar(g, role):
                    u = 0
                    nm = 0
                    for qb in range(2):
                        nj = _nj(role, qb)
                        for j in range(nj):
                            diag = (j == nj - 1)
                            for qt in range(4):
                                for h in range(H):
                                    g.wait_ge(sTs, u + 1)
                                    if diag:
                                        nm += 1
                                        g.wait_ge(sVm, nm)
                                    if u >= 2:
                                        g.wait_ge(sTt, u - 1)
                                    g.activation(psb[:, (u % 2) * 512:(u % 2) * 512 + 512],
                                                 [pA, pB][u % 2][:, :], AF.Exp,
                                                 bias=0.0, scale=SCALE).then_inc(sAe, 1)
                                    u += 1

                with nc.Block() as blk:
                    @blk.sync
                    def _(s):
                        for role in range(NROLE):
                            with s.If_eq(regs["s"], role):
                                p2_sync(s, role)
                    @blk.tensor
                    def _(t):
                        for role in range(NROLE):
                            with t.If_eq(regs["t"], role):
                                p2_tensor(t, role)
                    @blk.vector
                    def _(v):
                        for role in range(NROLE):
                            with v.If_eq(regs["v"], role):
                                p2_vector(v, role)
                    @blk.scalar
                    def _(c):
                        for role in range(NROLE):
                            with c.If_eq(regs["c"], role):
                                p2_scalar(c, role)
                clear_block()
                if phases <= 3:
                    return nc

            with ExitStack() as p3:
                wob = p3.enter_context(nc.sbuf_tensor("wob", [128, 16 * D], BF16))
                y1bt = p3.enter_context(nc.sbuf_tensor("y1bt", [128, D], BF16))
                ctxin = p3.enter_context(nc.sbuf_tensor("ctxin", [128, 2 * VW], F32))
                xrb = p3.enter_context(nc.sbuf_tensor("xrb", [128, 2 * D], F32))
                bo_b = p3.enter_context(nc.sbuf_tensor("bo_sb", [128, D], F32))
                ctxnb = p3.enter_context(nc.sbuf_tensor("ctxnb", [128, D], BF16))
                ctxfm = p3.enter_context(nc.sbuf_tensor("ctxfm", [128, D], BF16))
                acc = p3.enter_context(nc.sbuf_tensor("acc3", [128, D], F32))
                sq = p3.enter_context(nc.sbuf_tensor("sq3", [128, D], F32))
                y1t = p3.enter_context(nc.sbuf_tensor("y1t", [128, D], F32))
                rl = p3.enter_context(nc.sbuf_tensor("rl3", [128, H], F32))
                st1 = p3.enter_context(nc.sbuf_tensor("st13", [128, 8], F32))

                (sK, sVn, sTf, sVf, sTo, sVs, sVa, sAq, sVr, sAr, sVy, sAy,
                 sSt3, sVz) = sems[0:14]

                def ln_vector(v, i, acc_ap, sq_ap, out_ap):
                    if i >= 1:
                        v.wait_ge(sAr, 2 * i)
                    v.reduce_sum(st1[:, 0:1], acc_ap, axis=AX).then_inc(sVa, 1)
                    v.wait_ge(sAq, i + 1)
                    v.reduce_sum(st1[:, 1:2], sq_ap, axis=AX)
                    v.tensor_scalar(st1[:, 2:3], st1[:, 0:1], 1.0 / D, None, op0=OP.mult)
                    v.tensor_scalar(st1[:, 3:4], st1[:, 1:2], 1.0 / D, None, op0=OP.mult)
                    v.tensor_copy(st1[:, 4:5], st1[:, 0:1])
                    v.scalar_tensor_tensor(st1[:, 5:6], st1[:, 2:3], st1[:, 2:3],
                                           st1[:, 3:4], op0=OP.mult,
                                           op1=OP.subtract).then_inc(sVr, 1)
                    v.wait_ge(sAr, 2 * (i + 1))
                    v.tensor_scalar(out_ap, acc_ap, st1[:, 2:3], st1[:, 6:7],
                                    op0=OP.subtract, op1=OP.mult).then_inc(sVy, 1)

                def ln_scalar(c, i, acc_ap, sq_ap):
                    c.wait_ge(sVa, i + 1)
                    c.activation(sq_ap, acc_ap, AF.Square, bias=0.0, scale=1.0).then_inc(sAq, 1)
                    c.wait_ge(sVr, i + 1)
                    c.activation(st1[:, 4:5], st1[:, 5:6], AF.Ln,
                                 bias=epssb[:, :], scale=-1.0).then_inc(sAr, 1)
                    c.wait_ge(sAr, 2 * i + 1)
                    c.activation(st1[:, 6:7], st1[:, 4:5], AF.Exp,
                                 bias=0.0, scale=-0.5).then_inc(sAr, 1)

                with nc.Block() as blk:
                    @blk.gpsimd
                    def _(g):
                        for cc in range(16):
                            g.dma_start(out=wob[:, cc * D:(cc + 1) * D],
                                        in_=wo_t[cc * 128:(cc + 1) * 128, :]).then_inc(sw[5], 16)
                        g.wait_ge(sw[5], 256)
                    @blk.sync
                    def _(s):
                        s.dma_start(out=bo_b[:, :], in_=bo_t[:, :]).then_inc(sK, 16)
                        for qt in range(NQT):
                            if qt >= 1:
                                s.wait_ge(sVy, qt)
                                s.dma_start(out=yscr[qt - 1],
                                            in_=y1t[:, :]).then_inc(sSt3, 16)
                            if qt >= 2:
                                s.wait_ge(sVn, qt - 1)
                                s.wait_ge(sVs, 5 * (qt - 1))
                            s.dma_start(out=ctxin[:, (qt % 2) * VW:(qt % 2 + 1) * VW],
                                        in_=cscr[qt]).then_inc(sK, 16)
                            s.dma_start(out=xrb[:, (qt % 2) * D:(qt % 2 + 1) * D],
                                        in_=xr_t[qt * 128:(qt + 1) * 128, :]).then_inc(sK, 16)
                        s.wait_ge(sVy, NQT)
                        s.dma_start(out=yscr[NQT - 1], in_=y1t[:, :]).then_inc(sSt3, 16)
                        s.wait_ge(sSt3, 16 * NQT)
                        s.wait_ge(sK, 16 + 32 * NQT)
                    @blk.vector
                    def _(v):
                        for qt in range(NQT):
                            v.wait_ge(sK, 16 + 32 * (qt + 1))
                            if qt >= 1:
                                v.wait_ge(sTf, 2 * qt)
                            cb = (qt % 2) * VW
                            for h in range(H):
                                v.reciprocal(rl[:, h:h + 1],
                                             ctxin[:, cb + h * 129 + 128:cb + h * 129 + 129])
                            for h in range(H):
                                r = v.tensor_scalar(ctxnb[:, h * 128:(h + 1) * 128],
                                                    ctxin[:, cb + h * 129:cb + h * 129 + 128],
                                                    rl[:, h:h + 1], None, op0=OP.mult)
                            r.then_inc(sVn, 1)
                            if qt >= 1:
                                v.wait_ge(sTo, 4 * qt)
                            for g2 in range(2):
                                k = 2 * qt + g2
                                v.wait_ge(sTf, k + 1)
                                v.tensor_copy(
                                    bass.AP(ctxfm, 8 * g2 * 128,
                                            [[D, 128], [128, 8], [1, 128]]),
                                    ptb[k % 2][:, 0:1024].rearrange("p (b c) -> p b c", c=128)
                                ).then_inc(sVf, 1)
                            for gp in range(4):
                                v.wait_ge(sTo, 4 * qt + gp + 1)
                                v.tensor_tensor(acc[:, gp * 512:(gp + 1) * 512],
                                                [pE, pF][gp % 2][:, :],
                                                bo_b[:, gp * 512:(gp + 1) * 512],
                                                op=OP.add).then_inc(sVs, 1)
                            r = v.tensor_tensor(acc[:, :], acc[:, :],
                                                xrb[:, (qt % 2) * D:(qt % 2 + 1) * D],
                                                op=OP.add)
                            r.then_inc(sVs, 1)
                            if qt >= 1:
                                v.wait_ge(sSt3, 16 * qt)
                            ln_vector(v, qt, acc[:, :], sq[:, :], y1t[:, :])
                    @blk.scalar
                    def _(c):
                        for qt in range(NQT):
                            ln_scalar(c, qt, acc[:, :], sq[:, :])
                    @blk.tensor
                    def _(t):
                        t.wait_ge(sw[5], 256)
                        for qt in range(NQT):
                            t.wait_ge(sVn, qt + 1)
                            for g2 in range(2):
                                k = 2 * qt + g2
                                if k >= 2:
                                    t.wait_ge(sVf, k - 1)
                                for j in range(8):
                                    cc = 8 * g2 + j
                                    r = t.transpose(
                                        ptb[k % 2][:, (cc % 8) * 128:(cc % 8) * 128 + 128],
                                        ctxnb[:, cc * 128:cc * 128 + 128],
                                        identb[:, :])
                                r.then_inc(sTf, 1)
                            t.wait_ge(sVf, 2 * (qt + 1))
                            for gp in range(4):
                                i = 4 * qt + gp
                                if i >= 2:
                                    t.wait_ge(sVs, 5 * ((i - 2) // 4) + ((i - 2) % 4) + 1)
                                p = [pE, pF][gp % 2]
                                for cc in range(16):
                                    r = t.matmul(p[:, :], ctxfm[:, cc * 128:cc * 128 + 128],
                                                 wob[:, cc * D + gp * 512:cc * D + gp * 512 + 512],
                                                 start=(cc == 0), stop=(cc == 15))
                                r.then_inc(sTo, 1)
                clear_block()
                if phases <= 4:
                    return nc

                sK2, sTx2, sVx2 = sems[0:3]
                with nc.Block() as blk:
                    @blk.gpsimd
                    def _(g):
                        for qt in range(NQT):
                            if qt >= 1:
                                g.wait_ge(sTx2, 2 * qt)
                            g.dma_start(out=y1bt[:, :], in_=yscr[qt]).then_inc(sw[6], 16)
                        g.wait_ge(sw[6], 16 * NQT)
                    @blk.tensor
                    def _(t):
                        for qt in range(NQT):
                            t.wait_ge(sw[6], 16 * (qt + 1))
                            for g2 in range(2):
                                k = 2 * qt + g2
                                if k >= 2:
                                    t.wait_ge(sVx2, k - 1)
                                for j in range(8):
                                    cc = 8 * g2 + j
                                    r = t.transpose(
                                        ptb[k % 2][:, (cc % 8) * 128:(cc % 8) * 128 + 128],
                                        y1bt[:, cc * 128:cc * 128 + 128],
                                        identb[:, :])
                                r.then_inc(sTx2, 1)
                    @blk.vector
                    def _(v):
                        for qt in range(NQT):
                            for g2 in range(2):
                                k = 2 * qt + g2
                                v.wait_ge(sTx2, k + 1)
                                v.tensor_copy(
                                    bass.AP(y1fm, (8 * g2) * T + qt * 128,
                                            [[16 * T, 128], [T, 8], [1, 128]]),
                                    ptb[k % 2][:, 0:1024].rearrange("p (b c) -> p b c", c=128)
                                ).then_inc(sVx2, 1)
                    @blk.sync
                    def _(s):
                        s.wait_ge(sVx2, 2 * NQT)
                clear_block()
                if phases <= 5:
                    return nc

            accum = pw.enter_context(nc.sbuf_tensor("accum", [128, NQT * D], BF16,
                                                    side="right"))
            ewbuf = pw.enter_context(nc.sbuf_tensor("ew4", [128, NQT * E], F32,
                                                    side="right"))
            with ExitStack() as p4:
                w2t = p4.enter_context(nc.sbuf_tensor("w2t", [128, 8 * D], BF16))
                w1tb = p4.enter_context(nc.sbuf_tensor("w1tb", [128, 2 * D], BF16))
                w3tb = p4.enter_context(nc.sbuf_tensor("w3tb", [128, 2 * D], BF16))
                wsl = p4.enter_context(nc.sbuf_tensor("wsl", [128, 2 * D], BF16))
                wsl2 = p4.enter_context(nc.sbuf_tensor("wsl2", [128, 2 * FFN], BF16))
                gsb = p4.enter_context(nc.sbuf_tensor("gsb", [128, 8 * T], BF16))
                h1s = p4.enter_context(nc.sbuf_tensor("h1s4", [128, 2 * 512], F32))
                rw = p4.enter_context(nc.sbuf_tensor("rw4", [128, NQT * E], F32))
                wgf = p4.enter_context(nc.sbuf_tensor("wgf4", [128, 16 * E], BF16))
                wgb = p4.enter_context(nc.sbuf_tensor("wgb4", [8, D], BF16))
                tkb = p4.enter_context(nc.sbuf_tensor("tkb4", [128, NQT * 64], F32))
                tmpo = p4.enter_context(nc.sbuf_tensor("tmpo4", [128, 2 * 512], BF16))

                (sK, sTw, sVw, sTr, sAs, sVe, sX1, sX2, sTh, sAs2, sVg, sTo2,
                 sAo, sVo) = sems[0:14]

                with nc.Block() as blk:
                    @blk.gpsimd
                    def _(g):
                        g.dma_start(out=wgb[:, :], in_=wg_t[:, :]).then_inc(sw[7], 16)
                        g.memset(accum[:, :], 0.0)
                        g.wait_ge(sw[7], 16)
                    @blk.tensor
                    def _(t):
                        t.wait_ge(sw[7], 16)
                        for cc in range(16):
                            if cc >= 2:
                                t.wait_ge(sVw, cc - 1)
                            r = t.transpose(ptb[cc % 2][:, 0:8],
                                            wgb[0:8, cc * 128:(cc + 1) * 128],
                                            identb[0:8, 0:8])
                            r.then_inc(sTw, 1)
                        t.wait_ge(sVw, 16)
                        for qt in range(NQT):
                            if qt >= 2:
                                t.wait_ge(sAs, qt - 1)
                            p = [pA, pB][qt % 2]
                            for cc in range(16):
                                r = t.matmul(p[:, 0:8],
                                             y1fm[:, cc * T + qt * 128:cc * T + qt * 128 + 128],
                                             wgf[:, cc * 8:(cc + 1) * 8],
                                             start=(cc == 0), stop=(cc == 15))
                            r.then_inc(sTr, 1)
                    @blk.scalar
                    def _(c):
                        for qt in range(NQT):
                            c.wait_ge(sTr, qt + 1)
                            c.activation(rw[:, qt * E:(qt + 1) * E],
                                         [pA, pB][qt % 2][:, 0:8],
                                         AF.Sigmoid, bias=0.0, scale=1.0).then_inc(sAs, 1)
                    @blk.vector
                    def _(v):
                        for cc in range(16):
                            v.wait_ge(sTw, cc + 1)
                            v.tensor_copy(wgf[:, cc * 8:(cc + 1) * 8],
                                          ptb[cc % 2][:, 0:8]).then_inc(sVw, 1)
                        v.wait_ge(sAs, NQT)
                        C = 64
                        def col(q, a, b=None):
                            b = a + 1 if b is None else b
                            return tkb[:, q * C + a:q * C + b]
                        rwq = lambda q: rw[:, q * E:(q + 1) * E]
                        for q in range(NQT):
                            v.tensor_tensor(col(q, 0, 8), rwq(q), fbias[:, :], op=OP.add)
                        for q in range(NQT):
                            v.reduce_max(col(q, 48), col(q, 0, 8), axis=AX)
                        for q in range(NQT):
                            v.tensor_scalar(col(q, 8, 16), col(q, 0, 8), col(q, 48), None, op0=OP.is_ge)
                        for q in range(NQT):
                            v.scalar_tensor_tensor(col(q, 16, 24), col(q, 8, 16), -BIGI,
                                                   iotab[:, :], op0=OP.mult, op1=OP.add)
                        for q in range(NQT):
                            v.tensor_reduce(col(q, 49), col(q, 16, 24), axis=AX, op=OP.min)
                        for q in range(NQT):
                            v.tensor_scalar(col(q, 24, 32), iota[:, :], col(q, 49), None, op0=OP.is_equal)
                        for q in range(NQT):
                            v.tensor_tensor(col(q, 16, 24), col(q, 24, 32), rwq(q), op=OP.mult)
                        for q in range(NQT):
                            v.reduce_sum(col(q, 50), col(q, 16, 24), axis=AX)
                        for q in range(NQT):
                            v.scalar_tensor_tensor(col(q, 40, 48), col(q, 24, 32), -BIGS,
                                                   col(q, 0, 8), op0=OP.mult, op1=OP.add)
                        for q in range(NQT):
                            v.reduce_max(col(q, 51), col(q, 40, 48), axis=AX)
                        for q in range(NQT):
                            v.tensor_scalar(col(q, 8, 16), col(q, 40, 48), col(q, 51), None, op0=OP.is_ge)
                        for q in range(NQT):
                            v.scalar_tensor_tensor(col(q, 16, 24), col(q, 8, 16), -BIGI,
                                                   iotab[:, :], op0=OP.mult, op1=OP.add)
                        for q in range(NQT):
                            v.tensor_reduce(col(q, 52), col(q, 16, 24), axis=AX, op=OP.min)
                        for q in range(NQT):
                            v.tensor_scalar(col(q, 32, 40), iota[:, :], col(q, 52), None, op0=OP.is_equal)
                        for q in range(NQT):
                            v.tensor_tensor(col(q, 16, 24), col(q, 32, 40), rwq(q), op=OP.mult)
                        for q in range(NQT):
                            v.reduce_sum(col(q, 53), col(q, 16, 24), axis=AX)
                        for q in range(NQT):
                            v.tensor_tensor(col(q, 54), col(q, 50), col(q, 53), op=OP.add)
                        for q in range(NQT):
                            v.reciprocal(col(q, 55), col(q, 54))
                        for q in range(NQT):
                            v.tensor_tensor(col(q, 56), col(q, 50), col(q, 55), op=OP.mult)
                        for q in range(NQT):
                            v.tensor_tensor(col(q, 57), col(q, 53), col(q, 55), op=OP.mult)
                        for q in range(NQT):
                            v.tensor_scalar(col(q, 16, 24), col(q, 24, 32), col(q, 56), None, op0=OP.mult)
                        for q in range(NQT):
                            v.scalar_tensor_tensor(ewbuf[:, q * E:(q + 1) * E],
                                                   col(q, 32, 40), col(q, 57),
                                                   col(q, 16, 24), op0=OP.mult,
                                                   op1=OP.add).then_inc(sVe, 1)
                        v.wait_ge(sVe, NQT)
                clear_block()
                if phases <= 6:
                    return nc

                (sX1, sTw, sVw, sX2, sTw2, sVw2, sTh, sAs2, sVg, sTo2,
                 sAo, sVo, sPad1, sPad2) = sems[0:14]

                def p4_gpsimd(g):
                    for e in range(E):
                        for fb in range(8):
                            for wi, wt in ((0, w1_t), (1, w3_t)):
                                w = 16 * e + 2 * fb + wi
                                if w >= 2:
                                    g.wait_ge(sTw, 2 * (w - 1))
                                g.dma_start(out=wsl[:, (w % 2) * D:(w % 2 + 1) * D],
                                            in_=wt[e, fb * 128:(fb + 1) * 128, :]).then_inc(sw[8], 16)
                        for hc in range(16):
                            w = 16 * e + hc
                            if w >= 2:
                                g.wait_ge(sTw2, w - 1)
                            g.dma_start(out=wsl2[:, (w % 2) * FFN:(w % 2 + 1) * FFN],
                                        in_=w2_t[e, hc * 128:(hc + 1) * 128, :]).then_inc(sw[9], 16)
                    g.wait_ge(sw[8], 16 * 16 * E)
                    g.wait_ge(sw[9], 16 * 16 * E)

                def p4_tensor(g):
                    for e in range(E):
                        for fb in range(8):
                            for wi in (0, 1):
                                w = 16 * e + 2 * fb + wi
                                g.wait_ge(sw[8], 16 * (w + 1))
                                if w >= 1:
                                    g.wait_ge(sVw, 2 * w)
                                if e > 0 and fb == 0 and wi == 0:
                                    g.wait_ge(sVw2, 16 * e)
                                for g2 in range(2):
                                    for j in range(8):
                                        cc = 8 * g2 + j
                                        r = g.transpose(
                                            ptb[(2 * w + g2) % 2][:, (cc % 8) * 128:(cc % 8) * 128 + 128],
                                            wsl[:, (w % 2) * D + cc * 128:(w % 2) * D + cc * 128 + 128],
                                            identb[:, :])
                                    r.then_inc(sTw, 1)
                            g.wait_ge(sVw, 2 * (16 * e + 2 * fb + 2))
                            for tb in range(2):
                                m = 32 * e + 4 * fb + 2 * tb
                                if m // 2 >= 2:
                                    g.wait_ge(sAs2, m // 2 - 1)
                                for cc in range(16):
                                    r = g.matmul(pA[:, :] if tb == 0 else pC[:, :],
                                                 w1tb[:, (fb % 2) * D + cc * 128:(fb % 2) * D + cc * 128 + 128],
                                                 y1fm[:, cc * T + tb * 512:cc * T + (tb + 1) * 512],
                                                 start=(cc == 0), stop=(cc == 15))
                                r.then_inc(sTh, 1)
                                if m // 2 >= 2:
                                    g.wait_ge(sVg, m // 2 - 1)
                                for cc in range(16):
                                    r = g.matmul(pB[:, :] if tb == 0 else pD[:, :],
                                                 w3tb[:, (fb % 2) * D + cc * 128:(fb % 2) * D + cc * 128 + 128],
                                                 y1fm[:, cc * T + tb * 512:cc * T + (tb + 1) * 512],
                                                 start=(cc == 0), stop=(cc == 15))
                                r.then_inc(sTh, 1)
                        for hc in range(16):
                            w = 16 * e + hc
                            g.wait_ge(sw[9], 16 * (w + 1))
                            if hc < 2:
                                g.wait_ge(sVw, 32 * (e + 1))
                            if w >= 2:
                                g.wait_ge(sVw2, w - 1)
                            for j in range(8):
                                r = g.transpose(
                                    ptb[w % 2][:, j * 128:j * 128 + 128],
                                    wsl2[:, (w % 2) * FFN + j * 128:(w % 2) * FFN + j * 128 + 128],
                                    identb[:, :])
                            r.then_inc(sTw2, 1)
                        g.wait_ge(sVg, 16 * (e + 1))
                        g.wait_ge(sVw2, 16 * (e + 1))
                        for qt in range(NQT):
                            for gp in range(4):
                                i = 32 * e + 4 * qt + gp
                                if i >= 2:
                                    g.wait_ge(sAo, i - 1)
                                p = [pE, pF][i % 2]
                                for fc in range(8):
                                    r = g.matmul(
                                        p[:, :],
                                        gsb[:, fc * T + qt * 128:fc * T + qt * 128 + 128],
                                        w2t[:, fc * D + gp * 512:fc * D + gp * 512 + 512],
                                        start=(fc == 0), stop=(fc == 7))
                                r.then_inc(sTo2, 1)

                def p4_scalar(g):
                    for e in range(E):
                        for fb in range(8):
                            for tb in range(2):
                                m = 32 * e + 4 * fb + 2 * tb
                                sl = (m // 2) % 2
                                g.wait_ge(sTh, m + 1)
                                if m // 2 >= 2:
                                    g.wait_ge(sVg, m // 2 - 1)
                                g.activation(h1s[:, sl * 512:(sl + 1) * 512],
                                             pA[:, :] if tb == 0 else pC[:, :],
                                             AF.Silu, bias=0.0, scale=1.0).then_inc(sAs2, 1)
                        for qt in range(NQT):
                            for gp in range(4):
                                i = 32 * e + 4 * qt + gp
                                g.wait_ge(sTo2, i + 1)
                                if i >= 2:
                                    g.wait_ge(sVo, i - 1)
                                g.activation(tmpo[:, (i % 2) * 512:(i % 2 + 1) * 512],
                                             [pE, pF][i % 2][:, :], AF.Copy,
                                             bias=0.0,
                                             scale=ewbuf[:, qt * E + e:qt * E + e + 1]).then_inc(sAo, 1)

                def p4_vector(g):
                    for e in range(E):
                        for fb in range(8):
                            for wi, dstb in ((0, w1tb), (1, w3tb)):
                                w = 16 * e + 2 * fb + wi
                                if 8 * e + fb >= 2:
                                    g.wait_ge(sTh, 4 * (8 * e + fb - 1))
                                for g2 in range(2):
                                    g.wait_ge(sTw, 2 * w + g2 + 1)
                                    g.tensor_copy(
                                        dstb[:, (fb % 2) * D + g2 * 1024:(fb % 2) * D + (g2 + 1) * 1024],
                                        ptb[(2 * w + g2) % 2][:, 0:1024]).then_inc(sVw, 1)
                            for tb in range(2):
                                m = 32 * e + 4 * fb + 2 * tb
                                sl = (m // 2) % 2
                                g.wait_ge(sAs2, m // 2 + 1)
                                g.wait_ge(sTh, m + 2)
                                g.tensor_tensor(
                                    gsb[:, fb * T + tb * 512:fb * T + (tb + 1) * 512],
                                    h1s[:, sl * 512:(sl + 1) * 512],
                                    pB[:, :] if tb == 0 else pD[:, :],
                                    op=OP.mult).then_inc(sVg, 1)
                        for hc in range(16):
                            w = 16 * e + hc
                            g.wait_ge(sTw2, w + 1)
                            if e > 0 and hc == 0:
                                g.wait_ge(sTo2, 32 * e)
                            g.tensor_copy(
                                bass.AP(w2t, hc * 128, [[8 * D, 128], [D, 8], [1, 128]]),
                                ptb[w % 2][:, 0:1024].rearrange("p (b c) -> p b c", c=128)
                            ).then_inc(sVw2, 1)
                        for qt in range(NQT):
                            for gp in range(4):
                                i = 32 * e + 4 * qt + gp
                                g.wait_ge(sAo, i + 1)
                                dst = accum[:, qt * D + gp * 512:qt * D + (gp + 1) * 512]
                                g.tensor_tensor(dst, tmpo[:, (i % 2) * 512:(i % 2 + 1) * 512],
                                                dst, op=OP.add).then_inc(sVo, 1)


                with nc.Block() as blk:
                    @blk.gpsimd
                    def _(g):
                        p4_gpsimd(g)
                    @blk.tensor
                    def _(t):
                        p4_tensor(t)
                    @blk.scalar
                    def _(c):
                        p4_scalar(c)
                    @blk.vector
                    def _(v):
                        p4_vector(v)
                clear_block()
                if phases <= 7:
                    return nc

            with ExitStack() as p5:
                y1in = p5.enter_context(nc.sbuf_tensor("y1in", [128, 2 * D], F32))
                acc5 = p5.enter_context(nc.sbuf_tensor("acc5", [128, D], F32))
                sq5 = p5.enter_context(nc.sbuf_tensor("sq5", [128, D], F32))
                y2t = p5.enter_context(nc.sbuf_tensor("y2t", [128, 2 * D], F32))
                st1 = p5.enter_context(nc.sbuf_tensor("st15", [128, 8], F32))

                sK, sVa, sAq, sVr, sAr, sVy, sAy, sSt5 = sems[0:8]

                with nc.Block() as blk:
                    @blk.sync
                    def _(s):
                        for qt in range(NQT):
                            if qt >= 1:
                                s.wait_ge(sVy, qt)
                                s.dma_start(out=out_t[(qt - 1) * 128:qt * 128, :],
                                            in_=y2t[:, ((qt - 1) % 2) * D:((qt - 1) % 2 + 1) * D]).then_inc(sSt5, 16)
                            if qt >= 2:
                                s.wait_ge(sVa, qt - 1)
                            s.dma_start(out=y1in[:, (qt % 2) * D:(qt % 2 + 1) * D],
                                        in_=yscr[qt]).then_inc(sK, 16)
                        s.wait_ge(sVy, NQT)
                        s.dma_start(out=out_t[(NQT - 1) * 128:NQT * 128, :],
                                    in_=y2t[:, ((NQT - 1) % 2) * D:((NQT - 1) % 2 + 1) * D]).then_inc(sSt5, 16)
                        s.wait_ge(sSt5, 16 * NQT)
                        s.wait_ge(sK, 16 * NQT)
                    @blk.vector
                    def _(v):
                        for qt in range(NQT):
                            v.wait_ge(sK, 16 * (qt + 1))
                            v.tensor_tensor(acc5[:, :],
                                            accum[:, qt * D:(qt + 1) * D],
                                            y1in[:, (qt % 2) * D:(qt % 2 + 1) * D],
                                            op=OP.add)
                            if qt >= 1:
                                v.wait_ge(sAr, 2 * qt)
                            v.reduce_sum(st1[:, 0:1], acc5[:, :], axis=AX).then_inc(sVa, 1)
                            v.wait_ge(sAq, qt + 1)
                            v.reduce_sum(st1[:, 1:2], sq5[:, :], axis=AX)
                            v.tensor_scalar(st1[:, 2:3], st1[:, 0:1], 1.0 / D, None, op0=OP.mult)
                            v.tensor_scalar(st1[:, 3:4], st1[:, 1:2], 1.0 / D, None, op0=OP.mult)
                            v.tensor_copy(st1[:, 4:5], st1[:, 0:1])
                            v.scalar_tensor_tensor(st1[:, 5:6], st1[:, 2:3], st1[:, 2:3],
                                                   st1[:, 3:4], op0=OP.mult,
                                                   op1=OP.subtract).then_inc(sVr, 1)
                            v.wait_ge(sAr, 2 * (qt + 1))
                            if qt >= 2:
                                v.wait_ge(sSt5, 16 * (qt - 1))
                            v.tensor_scalar(y2t[:, (qt % 2) * D:(qt % 2 + 1) * D],
                                            acc5[:, :], st1[:, 2:3], st1[:, 6:7],
                                            op0=OP.subtract, op1=OP.mult).then_inc(sVy, 1)
                    @blk.scalar
                    def _(c):
                        for qt in range(NQT):
                            c.wait_ge(sVa, qt + 1)
                            c.activation(sq5[:, :], acc5[:, :], AF.Square,
                                         bias=0.0, scale=1.0).then_inc(sAq, 1)
                            c.wait_ge(sVr, qt + 1)
                            c.activation(st1[:, 4:5], st1[:, 5:6], AF.Ln,
                                         bias=epssb[:, :], scale=-1.0).then_inc(sAr, 1)
                            c.wait_ge(sAr, 2 * qt + 1)
                            c.activation(st1[:, 6:7], st1[:, 4:5], AF.Exp,
                                         bias=0.0, scale=-0.5).then_inc(sAr, 1)
    return nc


_NC_CACHE = {}


def _get_nc():
    if "nc" not in _NC_CACHE:
        _NC_CACHE["nc"] = build_nc()
    return _NC_CACHE["nc"]


def kernel(L0_attn_x, L0_attn_Wq, L0_attn_Wk, L0_attn_Wv, L0_attn_Wo,
           L0_attn_bo, L0_ffn_Wg, L0_ffn_W1, L0_ffn_W2, L0_ffn_W3,
           L0_ffn_bias):
    x = np.ascontiguousarray(np.asarray(L0_attn_x, dtype=np.float32))
    wq = np.ascontiguousarray(np.asarray(L0_attn_Wq, dtype=np.float32))
    wk = np.ascontiguousarray(np.asarray(L0_attn_Wk, dtype=np.float32))
    wv = np.ascontiguousarray(np.asarray(L0_attn_Wv, dtype=np.float32))
    wo = np.ascontiguousarray(np.asarray(L0_attn_Wo, dtype=np.float32))
    bo = np.asarray(L0_attn_bo, dtype=np.float32)
    wg = np.ascontiguousarray(np.asarray(L0_ffn_Wg, dtype=np.float32))
    w1 = np.ascontiguousarray(np.asarray(L0_ffn_W1, dtype=np.float32))
    w2 = np.ascontiguousarray(np.asarray(L0_ffn_W2, dtype=np.float32))
    w3 = np.ascontiguousarray(np.asarray(L0_ffn_W3, dtype=np.float32))
    fb = np.asarray(L0_ffn_bias, dtype=np.float32)

    bo_b = np.ascontiguousarray(np.broadcast_to(bo[None, :], (128, D)))
    fb_b = np.ascontiguousarray(np.broadcast_to(fb[None, :], (128, E)))
    ident = np.eye(128, dtype=np.float32)
    ii = np.arange(128, dtype=np.float32)[:, None]
    jj = np.arange(512, dtype=np.float32)[None, :]
    dm = np.zeros((128, 4 * 512), np.float32)
    for p in range(4):
        delta = 128 * p
        dm[:, p * 512:(p + 1) * 512] = np.where(jj <= delta + ii, 0.0, NEG)
    iota8 = np.ascontiguousarray(
        np.broadcast_to(np.arange(E, dtype=np.float32)[None, :], (128, E)))
    iota8b = np.ascontiguousarray(iota8 + BIGI)

    nc = _get_nc()
    in_maps = []
    for core in range(8):
        seq, r = core // 4, core % 4
        in_maps.append({
            "role": np.array([[r]], dtype=np.int32),
            "x": np.ascontiguousarray(x[seq]),
            "xr": np.ascontiguousarray(x[seq, BOUNDS[r]:BOUNDS[r] + T]),
            "wq": wq, "wk": wk, "wv": wv, "wo": wo, "bo_b": bo_b,
            "wg": wg, "w1": w1, "w2": w2, "w3": w3, "fbias_b": fb_b,
            "ident": ident, "dmask": dm, "iota8": iota8, "iota8b": iota8b,
            "epsc": np.full((128, 1), EPS, dtype=np.float32),
        })
    res = run_bass_kernel_spmd(nc, in_maps, core_ids=list(range(8)))
    out = np.empty((2, S, D), dtype=np.float32)
    for core in range(8):
        seq, r = core // 4, core % 4
        out[seq, BOUNDS[r]:BOUNDS[r] + T] = res.results[core]["out"]
    return out


# version 04
# speedup vs baseline: 1.0204x; 1.0204x over previous
import math
from contextlib import ExitStack

import numpy as np

import concourse.bass as bass
import concourse.mybir as mybir
from concourse.alu_op_type import AluOpType as OP
from concourse.bass_utils import run_bass_kernel_spmd

F32 = mybir.dt.float32
BF16 = mybir.dt.bfloat16
I32 = mybir.dt.int32
AF = mybir.ActivationFunctionType
AX = mybir.AxisListType.X

D = 2048
H = 16
HD = 128
S = 4096
NROLE = 4
BOUNDS = [0, 1024, 2048, 3072, 4096]
T = 1024
NQT = T // 128
FFN = 1024
E = 8
SCALE = float(1.0 / math.sqrt(HD))
EPS = 1e-5
NEG = -1e30
BIGI = 65536.0
BIGS = 1e6
VW = H * 129


def _nj(role, qb):
    return (BOUNDS[role] + qb * 512) // 512 + 1


def build_nc(phases=99, dbg=False):
    nc = bass.Bass("TRN2", target_bir_lowering=False, debug=False,
                   detect_race_conditions=False)
    ap = lambda t: t.ap()
    role_t = ap(nc.dram_tensor("role", [1, 1], I32, kind="ExternalInput"))
    x_t = ap(nc.dram_tensor("x", [S, D], F32, kind="ExternalInput"))
    xr_t = ap(nc.dram_tensor("xr", [T, D], F32, kind="ExternalInput"))
    wq_t = ap(nc.dram_tensor("wq", [D, D], F32, kind="ExternalInput"))
    wk_t = ap(nc.dram_tensor("wk", [D, D], F32, kind="ExternalInput"))
    wv_t = ap(nc.dram_tensor("wv", [D, D], F32, kind="ExternalInput"))
    wo_t = ap(nc.dram_tensor("wo", [D, D], F32, kind="ExternalInput"))
    bo_t = ap(nc.dram_tensor("bo_b", [128, D], F32, kind="ExternalInput"))
    wg_t = ap(nc.dram_tensor("wg", [E, D], F32, kind="ExternalInput"))
    w1_t = ap(nc.dram_tensor("w1", [E, FFN, D], F32, kind="ExternalInput"))
    w2_t = ap(nc.dram_tensor("w2", [E, D, FFN], F32, kind="ExternalInput"))
    w3_t = ap(nc.dram_tensor("w3", [E, FFN, D], F32, kind="ExternalInput"))
    fb_t = ap(nc.dram_tensor("fbias_b", [128, E], F32, kind="ExternalInput"))
    id_t = ap(nc.dram_tensor("ident", [128, 128], F32, kind="ExternalInput"))
    dm_t = ap(nc.dram_tensor("dmask", [128, 4 * 512], F32, kind="ExternalInput"))
    io_t = ap(nc.dram_tensor("iota8", [128, E], F32, kind="ExternalInput"))
    ib_t = ap(nc.dram_tensor("iota8b", [128, E], F32, kind="ExternalInput"))
    ep_t = ap(nc.dram_tensor("epsc", [128, 1], F32, kind="ExternalInput"))
    out_t = ap(nc.dram_tensor("out", [T, D], F32, kind="ExternalOutput"))
    SCR = "ExternalOutput" if dbg else "Internal"
    kscr = ap(nc.dram_tensor("k_scr", [8, 128, H * 512], BF16, kind=SCR))
    vscr = ap(nc.dram_tensor("v_scr", [8, 128, 4 * VW], BF16, kind=SCR))
    qscr = ap(nc.dram_tensor("q_scr", [2, 128, H * 512], BF16, kind=SCR))
    cscr = ap(nc.dram_tensor("ctx_scr", [NQT, 128, VW], F32, kind=SCR))
    yscr = ap(nc.dram_tensor("y1_scr", [NQT, 128, D], F32, kind=SCR))

    with ExitStack() as es:
        role_sb = es.enter_context(nc.sbuf_tensor("role_sb", [1, 1], I32))
        identb = es.enter_context(nc.sbuf_tensor("identb", [128, 128], BF16))
        dmask = es.enter_context(nc.sbuf_tensor("dmask_sb", [128, 4 * 512], F32))
        iota = es.enter_context(nc.sbuf_tensor("iota_sb", [128, E], F32))
        iotab = es.enter_context(nc.sbuf_tensor("iotab_sb", [128, E], F32))
        fbias = es.enter_context(nc.sbuf_tensor("fbias_sb", [128, E], F32))
        epssb = es.enter_context(nc.sbuf_tensor("eps_sb", [128, 1], F32))
        pA = es.enter_context(nc.psum_tensor("pA", [128, 512], F32))
        pB = es.enter_context(nc.psum_tensor("pB", [128, 512], F32))
        pC = es.enter_context(nc.psum_tensor("pC", [128, 512], F32))
        pD = es.enter_context(nc.psum_tensor("pD", [128, 512], F32))
        pE = es.enter_context(nc.psum_tensor("pE", [128, 512], F32))
        pF = es.enter_context(nc.psum_tensor("pF", [128, 512], F32))
        ptA = es.enter_context(nc.psum_tensor("ptA", [128, 1024], BF16))
        ptB = es.enter_context(nc.psum_tensor("ptB", [128, 1024], BF16))
        ptb = [ptA, ptB]
        SEMN = 16
        sems = [es.enter_context(nc.semaphore(f"s{i}")) for i in range(SEMN)]
        sw = [es.enter_context(nc.semaphore(f"sw{i}")) for i in range(10)]

        with nc.Block() as blk:
            @blk.sync
            def _(s):
                s.dma_start(out=role_sb[:, :], in_=role_t[:, :]).then_inc(sems[0], 16)
                s.dma_start(out=dmask[:, :], in_=dm_t[:, :]).then_inc(sems[0], 16)
                s.dma_start(out=iota[:, :], in_=io_t[:, :]).then_inc(sems[0], 16)
                s.dma_start(out=iotab[:, :], in_=ib_t[:, :]).then_inc(sems[0], 16)
                s.dma_start(out=fbias[:, :], in_=fb_t[:, :]).then_inc(sems[0], 16)
                s.dma_start(out=epssb[:, :], in_=ep_t[:, :]).then_inc(sems[0], 16)
                s.wait_ge(sems[0], 96)
                s.wait_ge(sw[0], 16)
            @blk.gpsimd
            def _(g):
                g.dma_start(out=identb[:, :], in_=id_t[:, :]).then_inc(sw[0], 16)

        regs = {}
        with nc.Block() as blk:
            def _mk(name):
                def _(eng):
                    r = eng.alloc_register("role_r")
                    eng.reg_load(r, role_sb[0:1, 0:1])
                    regs[name] = r
                return _
            blk.tensor(_mk("t")); blk.vector(_mk("v")); blk.scalar(_mk("c"))
            blk.sync(_mk("s")); blk.gpsimd(_mk("g"))

        def clear_block():
            with nc.Block() as blk:
                @blk.sync
                def _(s):
                    for sm in sems:
                        s.sem_clear(sm)

        clear_block()
        if phases <= 0:
            return nc


        with ExitStack() as p1:
            wkb = p1.enter_context(nc.sbuf_tensor("wkb", [128, 16 * D], BF16))
            wvb = p1.enter_context(nc.sbuf_tensor("wvb", [128, 16 * D], BF16))
            xbf = p1.enter_context(nc.sbuf_tensor("xbf", [128, 2 * D], BF16))
            xfm = p1.enter_context(nc.sbuf_tensor("xfm", [128, 16 * 512], BF16))
            ksb = p1.enter_context(nc.sbuf_tensor("ksb", [128, H * 512], BF16))
            vsb = p1.enter_context(nc.sbuf_tensor("vsb", [128, 4 * VW], BF16))

            with nc.Block() as blk:
                @blk.gpsimd
                def _(g):
                    for cc in range(16):
                        g.dma_start(out=wkb[:, cc * D:(cc + 1) * D],
                                    in_=wk_t[cc * 128:(cc + 1) * 128, :]).then_inc(sw[1], 16)
                    for cc in range(16):
                        g.dma_start(out=wvb[:, cc * D:(cc + 1) * D],
                                    in_=wv_t[cc * 128:(cc + 1) * 128, :]).then_inc(sw[1], 16)
                @blk.sync
                def _(s):
                    s.wait_ge(sw[1], 512)
            clear_block()

            sXd, sTx, sVx, sTk, sVk, sTv, sVv, sSt = sems[0:8]

            def proj_gpsimd(g, tok0, nch, sx):
                for i in range(4 * nch):
                    if i >= 2:
                        g.wait_ge(sTx, 2 * (i - 1))
                    r0 = tok0 + i * 128
                    g.dma_start(out=xbf[:, (i % 2) * D:(i % 2 + 1) * D],
                                in_=x_t[r0:r0 + 128, :]).then_inc(sx, 16)

            def proj_x_tensor(g, c, sx):
                for ts in range(4):
                    i = 4 * c + ts
                    g.wait_ge(sx, 16 * (i + 1))
                    for g2 in range(2):
                        k = 2 * i + g2
                        if k >= 2:
                            g.wait_ge(sVx, k - 1)
                        for j in range(8):
                            cc = 8 * g2 + j
                            r = g.transpose(
                                ptb[k % 2][:, (cc % 8) * 128:(cc % 8) * 128 + 128],
                                xbf[:, (i % 2) * D + cc * 128:(i % 2) * D + cc * 128 + 128],
                                identb[:, :])
                        r.then_inc(sTx, 1)

            def proj_x_vector(g, c):
                for ts in range(4):
                    i = 4 * c + ts
                    for g2 in range(2):
                        k = 2 * i + g2
                        g.wait_ge(sTx, k + 1)
                        r = g.tensor_copy(
                            bass.AP(xfm, 8 * g2 * 512 + ts * 128,
                                    [[16 * 512, 128], [512, 8], [1, 128]]),
                            ptb[k % 2][:, 0:1024].rearrange("p (b c) -> p b c", c=128))
                        r.then_inc(sVx, 1)

            def kproj_tensor(g, c, wbuf):
                g.wait_ge(sVx, 8 * (c + 1))
                for h in range(H):
                    j = 16 * c + h
                    if j >= 2:
                        g.wait_ge(sVk, j - 1)
                    p = [pA, pB][j % 2]
                    for cc in range(16):
                        r = g.matmul(p[:, :],
                                     wbuf[:, cc * D + h * 128:cc * D + h * 128 + 128],
                                     xfm[:, cc * 512:(cc + 1) * 512],
                                     start=(cc == 0), stop=(cc == 15))
                    r.then_inc(sTk, 1)

            def kproj_vector(g, c):
                for h in range(H):
                    j = 16 * c + h
                    g.wait_ge(sTk, j + 1)
                    g.tensor_copy(ksb[:, h * 512:(h + 1) * 512],
                                  [pA, pB][j % 2][:, :]).then_inc(sVk, 1)

            def p1_tensor(g, nch):
                for c in range(nch):
                    proj_x_tensor(g, c, sw[2])
                    kproj_tensor(g, c, wkb)
                    for ts in range(4):
                        for gp in range(4):
                            j = 16 * c + 4 * ts + gp
                            if j >= 2:
                                g.wait_ge(sVv, j - 1)
                            p = [pC, pD][j % 2]
                            for cc in range(16):
                                r = g.matmul(
                                    p[:, :],
                                    xfm[:, cc * 512 + ts * 128:cc * 512 + ts * 128 + 128],
                                    wvb[:, cc * D + gp * 512:cc * D + gp * 512 + 512],
                                    start=(cc == 0), stop=(cc == 15))
                            r.then_inc(sTv, 1)

            def p1_vector(g, nch):
                for ts in range(4):
                    g.memset(bass.AP(vsb, ts * VW + 128,
                                     [[4 * VW, 128], [129, H], [1, 1]]), 1.0)
                for c in range(nch):
                    if c > 0:
                        g.wait_ge(sTk, 16 * c)
                        g.wait_ge(sTv, 16 * c)
                        g.wait_ge(sSt, 32 * c)
                    proj_x_vector(g, c)
                    kproj_vector(g, c)
                    for ts in range(4):
                        for gp in range(4):
                            j = 16 * c + 4 * ts + gp
                            g.wait_ge(sTv, j + 1)
                            r = g.tensor_copy(
                                bass.AP(vsb, ts * VW + 4 * gp * 129,
                                        [[4 * VW, 128], [129, 4], [1, 128]]),
                                [pC, pD][j % 2][:, :].rearrange("p (b c) -> p b c", c=128))
                            r.then_inc(sVv, 1)

            def p1_sync(g, nch):
                for c in range(nch):
                    g.wait_ge(sVk, 16 * (c + 1))
                    g.wait_ge(sVv, 16 * (c + 1))
                    g.dma_start(out=kscr[c], in_=ksb[:, :]).then_inc(sSt, 16)
                    g.dma_start(out=vscr[c], in_=vsb[:, :]).then_inc(sSt, 16)
                g.wait_ge(sSt, 32 * nch)

            with nc.Block() as blk:
                @blk.gpsimd
                def _(g):
                    for role in range(NROLE):
                        with g.If_eq(regs["g"], role):
                            proj_gpsimd(g, 0, 2 * (role + 1), sw[2])
                @blk.tensor
                def _(t):
                    for role in range(NROLE):
                        with t.If_eq(regs["t"], role):
                            p1_tensor(t, 2 * (role + 1))
                @blk.vector
                def _(v):
                    for role in range(NROLE):
                        with v.If_eq(regs["v"], role):
                            p1_vector(v, 2 * (role + 1))
                @blk.sync
                def _(s):
                    for role in range(NROLE):
                        with s.If_eq(regs["s"], role):
                            p1_sync(s, 2 * (role + 1))
            clear_block()
            if phases <= 1:
                return nc

            with nc.Block() as blk:
                @blk.gpsimd
                def _(g):
                    for cc in range(16):
                        g.dma_start(out=wkb[:, cc * D:(cc + 1) * D],
                                    in_=wq_t[cc * 128:(cc + 1) * 128, :]).then_inc(sw[3], 16)
                @blk.sync
                def _(s):
                    s.wait_ge(sw[3], 256)
            clear_block()

            with nc.Block() as blk:
                @blk.gpsimd
                def _(g):
                    for role in range(NROLE):
                        with g.If_eq(regs["g"], role):
                            proj_gpsimd(g, BOUNDS[role], 2, sw[4])
                @blk.tensor
                def _(t):
                    for c in range(2):
                        proj_x_tensor(t, c, sw[4])
                        kproj_tensor(t, c, wkb)
                @blk.vector
                def _(v):
                    for c in range(2):
                        if c > 0:
                            v.wait_ge(sTk, 16 * c)
                            v.wait_ge(sSt, 16 * c)
                        proj_x_vector(v, c)
                        kproj_vector(v, c)
                @blk.sync
                def _(s):
                    for c in range(2):
                        s.wait_ge(sVk, 16 * (c + 1))
                        s.dma_start(out=qscr[c], in_=ksb[:, :]).then_inc(sSt, 16)
                    s.wait_ge(sSt, 32)
            clear_block()
            if phases <= 2:
                return nc

        with ExitStack() as pw:
            y1fm = pw.enter_context(nc.sbuf_tensor("y1fm", [128, 16 * T], BF16,
                                                   side="right"))
            with ExitStack() as p2:
                qbuf = p2.enter_context(nc.sbuf_tensor("qbuf", [128, H * 512], BF16))
                kbuf = p2.enter_context(nc.sbuf_tensor("kbuf", [128, 2 * H * 512], BF16))
                vbuf = p2.enter_context(nc.sbuf_tensor("vbuf", [128, 2 * 4 * VW], BF16))
                psb = p2.enter_context(nc.sbuf_tensor("psb", [128, 3 * 512], BF16))
                ptsb = p2.enter_context(nc.sbuf_tensor("ptsb", [128, 3 * 512], BF16))
                ctx = p2.enter_context(nc.sbuf_tensor("ctx", [128, 4 * VW], F32))

                sK, sTs, sVm, sAe, sTt, sVp, sTc, sVc, sSt2 = sems[0:9]

                def p2_tiles(role):
                    out = []
                    u = 0
                    for qb in range(2):
                        nj = _nj(role, qb)
                        for j in range(nj):
                            for qt in range(4):
                                for h in range(H):
                                    out.append((u, qb, j, qt, h, j == nj - 1,
                                                qt == 0 and h == 0,
                                                j == 0 and qt == 0 and h == 0))
                                    u += 1
                    return out

                def p2_sync(g, role):
                    kl = 0
                    slot_tenant_end = {0: None, 1: None}
                    tiles = 0
                    for qb in range(2):
                        nj = _nj(role, qb)
                        if qb == 1:
                            g.wait_ge(sTs, tiles)
                        g.dma_start(out=qbuf[:, :], in_=qscr[qb]).then_inc(sK, 16)
                        kl += 16
                        for j in range(nj):
                            sl = j % 2
                            if slot_tenant_end[sl] is not None:
                                g.wait_ge(sTc, slot_tenant_end[sl])
                            g.dma_start(out=kbuf[:, sl * H * 512:(sl + 1) * H * 512],
                                        in_=kscr[j]).then_inc(sK, 16)
                            g.dma_start(out=vbuf[:, sl * 4 * VW:(sl + 1) * 4 * VW],
                                        in_=vscr[j]).then_inc(sK, 16)
                            kl += 32
                            slot_tenant_end[sl] = tiles + 64 * (j + 1)
                        tiles += 64 * nj
                        g.wait_ge(sVc, tiles)
                        for qt in range(4):
                            g.dma_start(out=cscr[4 * qb + qt],
                                        in_=ctx[:, qt * VW:(qt + 1) * VW]).then_inc(sSt2, 16)
                    g.wait_ge(sSt2, 128)
                    g.wait_ge(sK, kl)

                def p2_tensor(g, role):
                    tiles = p2_tiles(role)
                    n = len(tiles)
                    sc = [pA, pB, pE]
                    pv = [pC, pD, pF]

                    def emit_qk(t):
                        u, qb, j, qt, h, diag, foc, foqb = t
                        if foqb:
                            pass
                        if foc:
                            kl = 16 * (qb + 1) + 32 * (j + 1 + (qb * _nj(role, 0)))
                            g.wait_ge(sK, kl)
                        if u >= 3:
                            g.wait_ge(sAe, u - 2)
                        g.matmul(sc[u % 3][:, :],
                                 qbuf[:, h * 512 + qt * 128:h * 512 + qt * 128 + 128],
                                 kbuf[:, (j % 2) * H * 512 + h * 512:(j % 2) * H * 512 + (h + 1) * 512],
                                 start=True, stop=True).then_inc(sTs, 1)

                    qb_of = [t[1] for t in tiles]
                    emit_qk(tiles[0])
                    if n > 1 and qb_of[1] == qb_of[0]:
                        emit_qk(tiles[1])
                    for t in tiles:
                        u, qb, j, qt, h, diag, foc, foqb = t
                        if foqb and qb == 1:
                            emit_qk(tiles[u])
                            if u + 1 < n:
                                emit_qk(tiles[u + 1])
                        if u + 2 < n and qb_of[u + 2] == qb:
                            emit_qk(tiles[u + 2])
                        g.wait_ge(sAe, u + 1)
                        if u >= 2:
                            g.wait_ge(sVp, u - 1)
                        for kk in range(4):
                            r = g.transpose(ptb[u % 2][:, kk * 128:kk * 128 + 128],
                                            psb[:, (u % 3) * 512 + kk * 128:(u % 3) * 512 + kk * 128 + 128],
                                            identb[:, :])
                        r.then_inc(sTt, 1)
                        g.wait_ge(sVp, u + 1)
                        if u >= 3:
                            g.wait_ge(sVc, u - 2)
                        pc = pv[u % 3]
                        vb = (j % 2) * 4 * VW
                        for kk in range(4):
                            r = g.matmul(
                                pc[:, 0:129],
                                ptsb[:, (u % 3) * 512 + kk * 128:(u % 3) * 512 + kk * 128 + 128],
                                vbuf[:, vb + kk * VW + h * 129:vb + kk * VW + (h + 1) * 129],
                                start=(kk == 0), stop=(kk == 3))
                        r.then_inc(sTc, 1)

                def p2_vector(g, role):
                    nm = 0
                    pv = [pC, pD, pF]
                    sc = [pA, pB, pE]
                    for t in p2_tiles(role):
                        u, qb, j, qt, h, diag, foc, foqb = t
                        if foqb and qb == 1:
                            g.wait_ge(sSt2, 64)
                        if diag:
                            g.wait_ge(sTs, u + 1)
                            g.tensor_tensor(sc[u % 3][:, :], sc[u % 3][:, :],
                                            dmask[:, qt * 512:(qt + 1) * 512],
                                            op=OP.add).then_inc(sVm, 1)
                            nm += 1
                        g.wait_ge(sTt, u + 1)
                        if u >= 3:
                            g.wait_ge(sTc, u - 2)
                        g.tensor_copy(ptsb[:, (u % 3) * 512:(u % 3) * 512 + 512],
                                      ptb[u % 2][:, 0:512]).then_inc(sVp, 1)
                        g.wait_ge(sTc, u + 1)
                        dst = ctx[:, qt * VW + h * 129:qt * VW + (h + 1) * 129]
                        if j == 0:
                            g.tensor_copy(dst, pv[u % 3][:, 0:129]).then_inc(sVc, 1)
                        else:
                            g.tensor_tensor(dst, pv[u % 3][:, 0:129], dst,
                                            op=OP.add).then_inc(sVc, 1)

                def p2_scalar(g, role):
                    nm = 0
                    sc = [pA, pB, pE]
                    for t in p2_tiles(role):
                        u, qb, j, qt, h, diag, foc, foqb = t
                        g.wait_ge(sTs, u + 1)
                        if diag:
                            nm += 1
                            g.wait_ge(sVm, nm)
                        if u >= 3:
                            g.wait_ge(sTt, u - 2)
                        g.activation(psb[:, (u % 3) * 512:(u % 3) * 512 + 512],
                                     sc[u % 3][:, :], AF.Exp,
                                     bias=0.0, scale=SCALE).then_inc(sAe, 1)

                with nc.Block() as blk:
                    @blk.sync
                    def _(s):
                        for role in range(NROLE):
                            with s.If_eq(regs["s"], role):
                                p2_sync(s, role)
                    @blk.tensor
                    def _(t):
                        for role in range(NROLE):
                            with t.If_eq(regs["t"], role):
                                p2_tensor(t, role)
                    @blk.vector
                    def _(v):
                        for role in range(NROLE):
                            with v.If_eq(regs["v"], role):
                                p2_vector(v, role)
                    @blk.scalar
                    def _(c):
                        for role in range(NROLE):
                            with c.If_eq(regs["c"], role):
                                p2_scalar(c, role)
                clear_block()
                if phases <= 3:
                    return nc

            with ExitStack() as p3:
                wob = p3.enter_context(nc.sbuf_tensor("wob", [128, 16 * D], BF16))
                y1bt = p3.enter_context(nc.sbuf_tensor("y1bt", [128, D], BF16))
                ctxin = p3.enter_context(nc.sbuf_tensor("ctxin", [128, 2 * VW], F32))
                xrb = p3.enter_context(nc.sbuf_tensor("xrb", [128, 2 * D], F32))
                bo_b = p3.enter_context(nc.sbuf_tensor("bo_sb", [128, D], F32))
                ctxnb = p3.enter_context(nc.sbuf_tensor("ctxnb", [128, D], BF16))
                ctxfm = p3.enter_context(nc.sbuf_tensor("ctxfm", [128, D], BF16))
                acc = p3.enter_context(nc.sbuf_tensor("acc3", [128, D], F32))
                sq = p3.enter_context(nc.sbuf_tensor("sq3", [128, D], F32))
                y1t = p3.enter_context(nc.sbuf_tensor("y1t", [128, D], F32))
                rl = p3.enter_context(nc.sbuf_tensor("rl3", [128, H], F32))
                st1 = p3.enter_context(nc.sbuf_tensor("st13", [128, 8], F32))

                (sK, sVn, sTf, sVf, sTo, sVs, sVa, sAq, sVr, sAr, sVy, sAy,
                 sSt3, sVz) = sems[0:14]

                def ln_vector(v, i, acc_ap, sq_ap, out_ap):
                    if i >= 1:
                        v.wait_ge(sAr, 2 * i)
                    v.reduce_sum(st1[:, 0:1], acc_ap, axis=AX).then_inc(sVa, 1)
                    v.wait_ge(sAq, i + 1)
                    v.reduce_sum(st1[:, 1:2], sq_ap, axis=AX)
                    v.tensor_scalar(st1[:, 2:3], st1[:, 0:1], 1.0 / D, None, op0=OP.mult)
                    v.tensor_scalar(st1[:, 3:4], st1[:, 1:2], 1.0 / D, None, op0=OP.mult)
                    v.tensor_copy(st1[:, 4:5], st1[:, 0:1])
                    v.scalar_tensor_tensor(st1[:, 5:6], st1[:, 2:3], st1[:, 2:3],
                                           st1[:, 3:4], op0=OP.mult,
                                           op1=OP.subtract).then_inc(sVr, 1)
                    v.wait_ge(sAr, 2 * (i + 1))
                    v.tensor_scalar(out_ap, acc_ap, st1[:, 2:3], st1[:, 6:7],
                                    op0=OP.subtract, op1=OP.mult).then_inc(sVy, 1)

                def ln_scalar(c, i, acc_ap, sq_ap):
                    c.wait_ge(sVa, i + 1)
                    c.activation(sq_ap, acc_ap, AF.Square, bias=0.0, scale=1.0).then_inc(sAq, 1)
                    c.wait_ge(sVr, i + 1)
                    c.activation(st1[:, 4:5], st1[:, 5:6], AF.Ln,
                                 bias=epssb[:, :], scale=-1.0).then_inc(sAr, 1)
                    c.wait_ge(sAr, 2 * i + 1)
                    c.activation(st1[:, 6:7], st1[:, 4:5], AF.Exp,
                                 bias=0.0, scale=-0.5).then_inc(sAr, 1)

                with nc.Block() as blk:
                    @blk.gpsimd
                    def _(g):
                        for cc in range(16):
                            g.dma_start(out=wob[:, cc * D:(cc + 1) * D],
                                        in_=wo_t[cc * 128:(cc + 1) * 128, :]).then_inc(sw[5], 16)
                        g.wait_ge(sw[5], 256)
                    @blk.sync
                    def _(s):
                        s.dma_start(out=bo_b[:, :], in_=bo_t[:, :]).then_inc(sK, 16)
                        for qt in range(NQT):
                            if qt >= 1:
                                s.wait_ge(sVy, qt)
                                s.dma_start(out=yscr[qt - 1],
                                            in_=y1t[:, :]).then_inc(sSt3, 16)
                            if qt >= 2:
                                s.wait_ge(sVn, qt - 1)
                                s.wait_ge(sVs, 5 * (qt - 1))
                            s.dma_start(out=ctxin[:, (qt % 2) * VW:(qt % 2 + 1) * VW],
                                        in_=cscr[qt]).then_inc(sK, 16)
                            s.dma_start(out=xrb[:, (qt % 2) * D:(qt % 2 + 1) * D],
                                        in_=xr_t[qt * 128:(qt + 1) * 128, :]).then_inc(sK, 16)
                        s.wait_ge(sVy, NQT)
                        s.dma_start(out=yscr[NQT - 1], in_=y1t[:, :]).then_inc(sSt3, 16)
                        s.wait_ge(sSt3, 16 * NQT)
                        s.wait_ge(sK, 16 + 32 * NQT)
                    @blk.vector
                    def _(v):
                        for qt in range(NQT):
                            v.wait_ge(sK, 16 + 32 * (qt + 1))
                            if qt >= 1:
                                v.wait_ge(sTf, 2 * qt)
                            cb = (qt % 2) * VW
                            for h in range(H):
                                v.reciprocal(rl[:, h:h + 1],
                                             ctxin[:, cb + h * 129 + 128:cb + h * 129 + 129])
                            for h in range(H):
                                r = v.tensor_scalar(ctxnb[:, h * 128:(h + 1) * 128],
                                                    ctxin[:, cb + h * 129:cb + h * 129 + 128],
                                                    rl[:, h:h + 1], None, op0=OP.mult)
                            r.then_inc(sVn, 1)
                            if qt >= 1:
                                v.wait_ge(sTo, 4 * qt)
                            for g2 in range(2):
                                k = 2 * qt + g2
                                v.wait_ge(sTf, k + 1)
                                v.tensor_copy(
                                    bass.AP(ctxfm, 8 * g2 * 128,
                                            [[D, 128], [128, 8], [1, 128]]),
                                    ptb[k % 2][:, 0:1024].rearrange("p (b c) -> p b c", c=128)
                                ).then_inc(sVf, 1)
                            for gp in range(4):
                                v.wait_ge(sTo, 4 * qt + gp + 1)
                                v.tensor_tensor(acc[:, gp * 512:(gp + 1) * 512],
                                                [pE, pF][gp % 2][:, :],
                                                bo_b[:, gp * 512:(gp + 1) * 512],
                                                op=OP.add).then_inc(sVs, 1)
                            r = v.tensor_tensor(acc[:, :], acc[:, :],
                                                xrb[:, (qt % 2) * D:(qt % 2 + 1) * D],
                                                op=OP.add)
                            r.then_inc(sVs, 1)
                            if qt >= 1:
                                v.wait_ge(sSt3, 16 * qt)
                            ln_vector(v, qt, acc[:, :], sq[:, :], y1t[:, :])
                    @blk.scalar
                    def _(c):
                        for qt in range(NQT):
                            ln_scalar(c, qt, acc[:, :], sq[:, :])
                    @blk.tensor
                    def _(t):
                        t.wait_ge(sw[5], 256)
                        for qt in range(NQT):
                            t.wait_ge(sVn, qt + 1)
                            for g2 in range(2):
                                k = 2 * qt + g2
                                if k >= 2:
                                    t.wait_ge(sVf, k - 1)
                                for j in range(8):
                                    cc = 8 * g2 + j
                                    r = t.transpose(
                                        ptb[k % 2][:, (cc % 8) * 128:(cc % 8) * 128 + 128],
                                        ctxnb[:, cc * 128:cc * 128 + 128],
                                        identb[:, :])
                                r.then_inc(sTf, 1)
                            t.wait_ge(sVf, 2 * (qt + 1))
                            for gp in range(4):
                                i = 4 * qt + gp
                                if i >= 2:
                                    t.wait_ge(sVs, 5 * ((i - 2) // 4) + ((i - 2) % 4) + 1)
                                p = [pE, pF][gp % 2]
                                for cc in range(16):
                                    r = t.matmul(p[:, :], ctxfm[:, cc * 128:cc * 128 + 128],
                                                 wob[:, cc * D + gp * 512:cc * D + gp * 512 + 512],
                                                 start=(cc == 0), stop=(cc == 15))
                                r.then_inc(sTo, 1)
                clear_block()
                if phases <= 4:
                    return nc

                sK2, sTx2, sVx2 = sems[0:3]
                with nc.Block() as blk:
                    @blk.gpsimd
                    def _(g):
                        for qt in range(NQT):
                            if qt >= 1:
                                g.wait_ge(sTx2, 2 * qt)
                            g.dma_start(out=y1bt[:, :], in_=yscr[qt]).then_inc(sw[6], 16)
                        g.wait_ge(sw[6], 16 * NQT)
                    @blk.tensor
                    def _(t):
                        for qt in range(NQT):
                            t.wait_ge(sw[6], 16 * (qt + 1))
                            for g2 in range(2):
                                k = 2 * qt + g2
                                if k >= 2:
                                    t.wait_ge(sVx2, k - 1)
                                for j in range(8):
                                    cc = 8 * g2 + j
                                    r = t.transpose(
                                        ptb[k % 2][:, (cc % 8) * 128:(cc % 8) * 128 + 128],
                                        y1bt[:, cc * 128:cc * 128 + 128],
                                        identb[:, :])
                                r.then_inc(sTx2, 1)
                    @blk.vector
                    def _(v):
                        for qt in range(NQT):
                            for g2 in range(2):
                                k = 2 * qt + g2
                                v.wait_ge(sTx2, k + 1)
                                v.tensor_copy(
                                    bass.AP(y1fm, (8 * g2) * T + qt * 128,
                                            [[16 * T, 128], [T, 8], [1, 128]]),
                                    ptb[k % 2][:, 0:1024].rearrange("p (b c) -> p b c", c=128)
                                ).then_inc(sVx2, 1)
                    @blk.sync
                    def _(s):
                        s.wait_ge(sVx2, 2 * NQT)
                clear_block()
                if phases <= 5:
                    return nc

            accum = pw.enter_context(nc.sbuf_tensor("accum", [128, NQT * D], BF16,
                                                    side="right"))
            ewbuf = pw.enter_context(nc.sbuf_tensor("ew4", [128, NQT * E], F32,
                                                    side="right"))
            with ExitStack() as p4:
                w2t = p4.enter_context(nc.sbuf_tensor("w2t", [128, 8 * D], BF16))
                w1tb = p4.enter_context(nc.sbuf_tensor("w1tb", [128, 2 * D], BF16))
                w3tb = p4.enter_context(nc.sbuf_tensor("w3tb", [128, 2 * D], BF16))
                wsl = p4.enter_context(nc.sbuf_tensor("wsl", [128, 2 * D], BF16))
                wsl2 = p4.enter_context(nc.sbuf_tensor("wsl2", [128, 2 * FFN], BF16))
                gsb = p4.enter_context(nc.sbuf_tensor("gsb", [128, 8 * T], BF16))
                h1s = p4.enter_context(nc.sbuf_tensor("h1s4", [128, 2 * 512], F32))
                rw = p4.enter_context(nc.sbuf_tensor("rw4", [128, NQT * E], F32))
                wgf = p4.enter_context(nc.sbuf_tensor("wgf4", [128, 16 * E], BF16))
                wgb = p4.enter_context(nc.sbuf_tensor("wgb4", [8, D], BF16))
                tkb = p4.enter_context(nc.sbuf_tensor("tkb4", [128, NQT * 64], F32))
                tmpo = p4.enter_context(nc.sbuf_tensor("tmpo4", [128, 2 * 512], BF16))

                (sK, sTw, sVw, sTr, sAs, sVe, sX1, sX2, sTh, sAs2, sVg, sTo2,
                 sAo, sVo) = sems[0:14]

                with nc.Block() as blk:
                    @blk.gpsimd
                    def _(g):
                        g.dma_start(out=wgb[:, :], in_=wg_t[:, :]).then_inc(sw[7], 16)
                        g.memset(accum[:, :], 0.0)
                        g.wait_ge(sw[7], 16)
                    @blk.tensor
                    def _(t):
                        t.wait_ge(sw[7], 16)
                        for cc in range(16):
                            if cc >= 2:
                                t.wait_ge(sVw, cc - 1)
                            r = t.transpose(ptb[cc % 2][:, 0:8],
                                            wgb[0:8, cc * 128:(cc + 1) * 128],
                                            identb[0:8, 0:8])
                            r.then_inc(sTw, 1)
                        t.wait_ge(sVw, 16)
                        for qt in range(NQT):
                            if qt >= 2:
                                t.wait_ge(sAs, qt - 1)
                            p = [pA, pB][qt % 2]
                            for cc in range(16):
                                r = t.matmul(p[:, 0:8],
                                             y1fm[:, cc * T + qt * 128:cc * T + qt * 128 + 128],
                                             wgf[:, cc * 8:(cc + 1) * 8],
                                             start=(cc == 0), stop=(cc == 15))
                            r.then_inc(sTr, 1)
                    @blk.scalar
                    def _(c):
                        for qt in range(NQT):
                            c.wait_ge(sTr, qt + 1)
                            c.activation(rw[:, qt * E:(qt + 1) * E],
                                         [pA, pB][qt % 2][:, 0:8],
                                         AF.Sigmoid, bias=0.0, scale=1.0).then_inc(sAs, 1)
                    @blk.vector
                    def _(v):
                        for cc in range(16):
                            v.wait_ge(sTw, cc + 1)
                            v.tensor_copy(wgf[:, cc * 8:(cc + 1) * 8],
                                          ptb[cc % 2][:, 0:8]).then_inc(sVw, 1)
                        v.wait_ge(sAs, NQT)
                        C = 64
                        def col(q, a, b=None):
                            b = a + 1 if b is None else b
                            return tkb[:, q * C + a:q * C + b]
                        rwq = lambda q: rw[:, q * E:(q + 1) * E]
                        for q in range(NQT):
                            v.tensor_tensor(col(q, 0, 8), rwq(q), fbias[:, :], op=OP.add)
                        for q in range(NQT):
                            v.reduce_max(col(q, 48), col(q, 0, 8), axis=AX)
                        for q in range(NQT):
                            v.tensor_scalar(col(q, 8, 16), col(q, 0, 8), col(q, 48), None, op0=OP.is_ge)
                        for q in range(NQT):
                            v.scalar_tensor_tensor(col(q, 16, 24), col(q, 8, 16), -BIGI,
                                                   iotab[:, :], op0=OP.mult, op1=OP.add)
                        for q in range(NQT):
                            v.tensor_reduce(col(q, 49), col(q, 16, 24), axis=AX, op=OP.min)
                        for q in range(NQT):
                            v.tensor_scalar(col(q, 24, 32), iota[:, :], col(q, 49), None, op0=OP.is_equal)
                        for q in range(NQT):
                            v.tensor_tensor(col(q, 16, 24), col(q, 24, 32), rwq(q), op=OP.mult)
                        for q in range(NQT):
                            v.reduce_sum(col(q, 50), col(q, 16, 24), axis=AX)
                        for q in range(NQT):
                            v.scalar_tensor_tensor(col(q, 40, 48), col(q, 24, 32), -BIGS,
                                                   col(q, 0, 8), op0=OP.mult, op1=OP.add)
                        for q in range(NQT):
                            v.reduce_max(col(q, 51), col(q, 40, 48), axis=AX)
                        for q in range(NQT):
                            v.tensor_scalar(col(q, 8, 16), col(q, 40, 48), col(q, 51), None, op0=OP.is_ge)
                        for q in range(NQT):
                            v.scalar_tensor_tensor(col(q, 16, 24), col(q, 8, 16), -BIGI,
                                                   iotab[:, :], op0=OP.mult, op1=OP.add)
                        for q in range(NQT):
                            v.tensor_reduce(col(q, 52), col(q, 16, 24), axis=AX, op=OP.min)
                        for q in range(NQT):
                            v.tensor_scalar(col(q, 32, 40), iota[:, :], col(q, 52), None, op0=OP.is_equal)
                        for q in range(NQT):
                            v.tensor_tensor(col(q, 16, 24), col(q, 32, 40), rwq(q), op=OP.mult)
                        for q in range(NQT):
                            v.reduce_sum(col(q, 53), col(q, 16, 24), axis=AX)
                        for q in range(NQT):
                            v.tensor_tensor(col(q, 54), col(q, 50), col(q, 53), op=OP.add)
                        for q in range(NQT):
                            v.reciprocal(col(q, 55), col(q, 54))
                        for q in range(NQT):
                            v.tensor_tensor(col(q, 56), col(q, 50), col(q, 55), op=OP.mult)
                        for q in range(NQT):
                            v.tensor_tensor(col(q, 57), col(q, 53), col(q, 55), op=OP.mult)
                        for q in range(NQT):
                            v.tensor_scalar(col(q, 16, 24), col(q, 24, 32), col(q, 56), None, op0=OP.mult)
                        for q in range(NQT):
                            v.scalar_tensor_tensor(ewbuf[:, q * E:(q + 1) * E],
                                                   col(q, 32, 40), col(q, 57),
                                                   col(q, 16, 24), op0=OP.mult,
                                                   op1=OP.add).then_inc(sVe, 1)
                        v.wait_ge(sVe, NQT)
                clear_block()
                if phases <= 6:
                    return nc

                (sX1, sTw, sVw, sX2, sTw2, sVw2, sTh, sAs2, sVg, sTo2,
                 sAo, sVo, sPad1, sPad2) = sems[0:14]

                def p4_gpsimd(g):
                    for e in range(E):
                        for fb in range(8):
                            for wi, wt in ((0, w1_t), (1, w3_t)):
                                w = 16 * e + 2 * fb + wi
                                if w >= 2:
                                    g.wait_ge(sTw, 2 * (w - 1))
                                g.dma_start(out=wsl[:, (w % 2) * D:(w % 2 + 1) * D],
                                            in_=wt[e, fb * 128:(fb + 1) * 128, :]).then_inc(sw[8], 16)
                        for hc in range(16):
                            w = 16 * e + hc
                            if w >= 2:
                                g.wait_ge(sTw2, w - 1)
                            g.dma_start(out=wsl2[:, (w % 2) * FFN:(w % 2 + 1) * FFN],
                                        in_=w2_t[e, hc * 128:(hc + 1) * 128, :]).then_inc(sw[9], 16)
                    g.wait_ge(sw[8], 16 * 16 * E)
                    g.wait_ge(sw[9], 16 * 16 * E)

                def p4_tensor(g):
                    for e in range(E):
                        for fb in range(8):
                            for wi in (0, 1):
                                w = 16 * e + 2 * fb + wi
                                g.wait_ge(sw[8], 16 * (w + 1))
                                if w >= 1:
                                    g.wait_ge(sVw, 2 * w)
                                if e > 0 and fb == 0 and wi == 0:
                                    g.wait_ge(sVw2, 16 * e)
                                for g2 in range(2):
                                    for j in range(8):
                                        cc = 8 * g2 + j
                                        r = g.transpose(
                                            ptb[(2 * w + g2) % 2][:, (cc % 8) * 128:(cc % 8) * 128 + 128],
                                            wsl[:, (w % 2) * D + cc * 128:(w % 2) * D + cc * 128 + 128],
                                            identb[:, :])
                                    r.then_inc(sTw, 1)
                            g.wait_ge(sVw, 2 * (16 * e + 2 * fb + 2))
                            for tb in range(2):
                                m = 32 * e + 4 * fb + 2 * tb
                                if m // 2 >= 2:
                                    g.wait_ge(sAs2, m // 2 - 1)
                                for cc in range(16):
                                    r = g.matmul(pA[:, :] if tb == 0 else pC[:, :],
                                                 w1tb[:, (fb % 2) * D + cc * 128:(fb % 2) * D + cc * 128 + 128],
                                                 y1fm[:, cc * T + tb * 512:cc * T + (tb + 1) * 512],
                                                 start=(cc == 0), stop=(cc == 15))
                                r.then_inc(sTh, 1)
                                if m // 2 >= 2:
                                    g.wait_ge(sVg, m // 2 - 1)
                                for cc in range(16):
                                    r = g.matmul(pB[:, :] if tb == 0 else pD[:, :],
                                                 w3tb[:, (fb % 2) * D + cc * 128:(fb % 2) * D + cc * 128 + 128],
                                                 y1fm[:, cc * T + tb * 512:cc * T + (tb + 1) * 512],
                                                 start=(cc == 0), stop=(cc == 15))
                                r.then_inc(sTh, 1)
                        for hc in range(16):
                            w = 16 * e + hc
                            g.wait_ge(sw[9], 16 * (w + 1))
                            if hc < 2:
                                g.wait_ge(sVw, 32 * (e + 1))
                            if w >= 2:
                                g.wait_ge(sVw2, w - 1)
                            for j in range(8):
                                r = g.transpose(
                                    ptb[w % 2][:, j * 128:j * 128 + 128],
                                    wsl2[:, (w % 2) * FFN + j * 128:(w % 2) * FFN + j * 128 + 128],
                                    identb[:, :])
                            r.then_inc(sTw2, 1)
                        g.wait_ge(sVg, 16 * (e + 1))
                        g.wait_ge(sVw2, 16 * (e + 1))
                        for qt in range(NQT):
                            for gp in range(4):
                                i = 32 * e + 4 * qt + gp
                                if i >= 2:
                                    g.wait_ge(sAo, i - 1)
                                p = [pE, pF][i % 2]
                                for fc in range(8):
                                    r = g.matmul(
                                        p[:, :],
                                        gsb[:, fc * T + qt * 128:fc * T + qt * 128 + 128],
                                        w2t[:, fc * D + gp * 512:fc * D + gp * 512 + 512],
                                        start=(fc == 0), stop=(fc == 7))
                                r.then_inc(sTo2, 1)

                def p4_scalar(g):
                    for e in range(E):
                        for fb in range(8):
                            for tb in range(2):
                                m = 32 * e + 4 * fb + 2 * tb
                                sl = (m // 2) % 2
                                g.wait_ge(sTh, m + 1)
                                if m // 2 >= 2:
                                    g.wait_ge(sVg, m // 2 - 1)
                                g.activation(h1s[:, sl * 512:(sl + 1) * 512],
                                             pA[:, :] if tb == 0 else pC[:, :],
                                             AF.Silu, bias=0.0, scale=1.0).then_inc(sAs2, 1)
                        for qt in range(NQT):
                            for gp in range(4):
                                i = 32 * e + 4 * qt + gp
                                g.wait_ge(sTo2, i + 1)
                                if i >= 2:
                                    g.wait_ge(sVo, i - 1)
                                g.activation(tmpo[:, (i % 2) * 512:(i % 2 + 1) * 512],
                                             [pE, pF][i % 2][:, :], AF.Copy,
                                             bias=0.0,
                                             scale=ewbuf[:, qt * E + e:qt * E + e + 1]).then_inc(sAo, 1)

                def p4_vector(g):
                    for e in range(E):
                        for fb in range(8):
                            for wi, dstb in ((0, w1tb), (1, w3tb)):
                                w = 16 * e + 2 * fb + wi
                                if 8 * e + fb >= 2:
                                    g.wait_ge(sTh, 4 * (8 * e + fb - 1))
                                for g2 in range(2):
                                    g.wait_ge(sTw, 2 * w + g2 + 1)
                                    g.tensor_copy(
                                        dstb[:, (fb % 2) * D + g2 * 1024:(fb % 2) * D + (g2 + 1) * 1024],
                                        ptb[(2 * w + g2) % 2][:, 0:1024]).then_inc(sVw, 1)
                            for tb in range(2):
                                m = 32 * e + 4 * fb + 2 * tb
                                sl = (m // 2) % 2
                                g.wait_ge(sAs2, m // 2 + 1)
                                g.wait_ge(sTh, m + 2)
                                g.tensor_tensor(
                                    gsb[:, fb * T + tb * 512:fb * T + (tb + 1) * 512],
                                    h1s[:, sl * 512:(sl + 1) * 512],
                                    pB[:, :] if tb == 0 else pD[:, :],
                                    op=OP.mult).then_inc(sVg, 1)
                        for hc in range(16):
                            w = 16 * e + hc
                            g.wait_ge(sTw2, w + 1)
                            if e > 0 and hc == 0:
                                g.wait_ge(sTo2, 32 * e)
                            g.tensor_copy(
                                bass.AP(w2t, hc * 128, [[8 * D, 128], [D, 8], [1, 128]]),
                                ptb[w % 2][:, 0:1024].rearrange("p (b c) -> p b c", c=128)
                            ).then_inc(sVw2, 1)
                        for qt in range(NQT):
                            for gp in range(4):
                                i = 32 * e + 4 * qt + gp
                                g.wait_ge(sAo, i + 1)
                                dst = accum[:, qt * D + gp * 512:qt * D + (gp + 1) * 512]
                                g.tensor_tensor(dst, tmpo[:, (i % 2) * 512:(i % 2 + 1) * 512],
                                                dst, op=OP.add).then_inc(sVo, 1)


                with nc.Block() as blk:
                    @blk.gpsimd
                    def _(g):
                        p4_gpsimd(g)
                    @blk.tensor
                    def _(t):
                        p4_tensor(t)
                    @blk.scalar
                    def _(c):
                        p4_scalar(c)
                    @blk.vector
                    def _(v):
                        p4_vector(v)
                clear_block()
                if phases <= 7:
                    return nc

            with ExitStack() as p5:
                y1in = p5.enter_context(nc.sbuf_tensor("y1in", [128, 2 * D], F32))
                acc5 = p5.enter_context(nc.sbuf_tensor("acc5", [128, D], F32))
                sq5 = p5.enter_context(nc.sbuf_tensor("sq5", [128, D], F32))
                y2t = p5.enter_context(nc.sbuf_tensor("y2t", [128, 2 * D], F32))
                st1 = p5.enter_context(nc.sbuf_tensor("st15", [128, 8], F32))

                sK, sVa, sAq, sVr, sAr, sVy, sAy, sSt5 = sems[0:8]

                with nc.Block() as blk:
                    @blk.sync
                    def _(s):
                        for qt in range(NQT):
                            if qt >= 1:
                                s.wait_ge(sVy, qt)
                                s.dma_start(out=out_t[(qt - 1) * 128:qt * 128, :],
                                            in_=y2t[:, ((qt - 1) % 2) * D:((qt - 1) % 2 + 1) * D]).then_inc(sSt5, 16)
                            if qt >= 2:
                                s.wait_ge(sVa, qt - 1)
                            s.dma_start(out=y1in[:, (qt % 2) * D:(qt % 2 + 1) * D],
                                        in_=yscr[qt]).then_inc(sK, 16)
                        s.wait_ge(sVy, NQT)
                        s.dma_start(out=out_t[(NQT - 1) * 128:NQT * 128, :],
                                    in_=y2t[:, ((NQT - 1) % 2) * D:((NQT - 1) % 2 + 1) * D]).then_inc(sSt5, 16)
                        s.wait_ge(sSt5, 16 * NQT)
                        s.wait_ge(sK, 16 * NQT)
                    @blk.vector
                    def _(v):
                        for qt in range(NQT):
                            v.wait_ge(sK, 16 * (qt + 1))
                            v.tensor_tensor(acc5[:, :],
                                            accum[:, qt * D:(qt + 1) * D],
                                            y1in[:, (qt % 2) * D:(qt % 2 + 1) * D],
                                            op=OP.add)
                            if qt >= 1:
                                v.wait_ge(sAr, 2 * qt)
                            v.reduce_sum(st1[:, 0:1], acc5[:, :], axis=AX).then_inc(sVa, 1)
                            v.wait_ge(sAq, qt + 1)
                            v.reduce_sum(st1[:, 1:2], sq5[:, :], axis=AX)
                            v.tensor_scalar(st1[:, 2:3], st1[:, 0:1], 1.0 / D, None, op0=OP.mult)
                            v.tensor_scalar(st1[:, 3:4], st1[:, 1:2], 1.0 / D, None, op0=OP.mult)
                            v.tensor_copy(st1[:, 4:5], st1[:, 0:1])
                            v.scalar_tensor_tensor(st1[:, 5:6], st1[:, 2:3], st1[:, 2:3],
                                                   st1[:, 3:4], op0=OP.mult,
                                                   op1=OP.subtract).then_inc(sVr, 1)
                            v.wait_ge(sAr, 2 * (qt + 1))
                            if qt >= 2:
                                v.wait_ge(sSt5, 16 * (qt - 1))
                            v.tensor_scalar(y2t[:, (qt % 2) * D:(qt % 2 + 1) * D],
                                            acc5[:, :], st1[:, 2:3], st1[:, 6:7],
                                            op0=OP.subtract, op1=OP.mult).then_inc(sVy, 1)
                    @blk.scalar
                    def _(c):
                        for qt in range(NQT):
                            c.wait_ge(sVa, qt + 1)
                            c.activation(sq5[:, :], acc5[:, :], AF.Square,
                                         bias=0.0, scale=1.0).then_inc(sAq, 1)
                            c.wait_ge(sVr, qt + 1)
                            c.activation(st1[:, 4:5], st1[:, 5:6], AF.Ln,
                                         bias=epssb[:, :], scale=-1.0).then_inc(sAr, 1)
                            c.wait_ge(sAr, 2 * qt + 1)
                            c.activation(st1[:, 6:7], st1[:, 4:5], AF.Exp,
                                         bias=0.0, scale=-0.5).then_inc(sAr, 1)
    return nc


_NC_CACHE = {}


def _get_nc():
    if "nc" not in _NC_CACHE:
        _NC_CACHE["nc"] = build_nc()
    return _NC_CACHE["nc"]


def kernel(L0_attn_x, L0_attn_Wq, L0_attn_Wk, L0_attn_Wv, L0_attn_Wo,
           L0_attn_bo, L0_ffn_Wg, L0_ffn_W1, L0_ffn_W2, L0_ffn_W3,
           L0_ffn_bias):
    x = np.ascontiguousarray(np.asarray(L0_attn_x, dtype=np.float32))
    wq = np.ascontiguousarray(np.asarray(L0_attn_Wq, dtype=np.float32))
    wk = np.ascontiguousarray(np.asarray(L0_attn_Wk, dtype=np.float32))
    wv = np.ascontiguousarray(np.asarray(L0_attn_Wv, dtype=np.float32))
    wo = np.ascontiguousarray(np.asarray(L0_attn_Wo, dtype=np.float32))
    bo = np.asarray(L0_attn_bo, dtype=np.float32)
    wg = np.ascontiguousarray(np.asarray(L0_ffn_Wg, dtype=np.float32))
    w1 = np.ascontiguousarray(np.asarray(L0_ffn_W1, dtype=np.float32))
    w2 = np.ascontiguousarray(np.asarray(L0_ffn_W2, dtype=np.float32))
    w3 = np.ascontiguousarray(np.asarray(L0_ffn_W3, dtype=np.float32))
    fb = np.asarray(L0_ffn_bias, dtype=np.float32)

    bo_b = np.ascontiguousarray(np.broadcast_to(bo[None, :], (128, D)))
    fb_b = np.ascontiguousarray(np.broadcast_to(fb[None, :], (128, E)))
    ident = np.eye(128, dtype=np.float32)
    ii = np.arange(128, dtype=np.float32)[:, None]
    jj = np.arange(512, dtype=np.float32)[None, :]
    dm = np.zeros((128, 4 * 512), np.float32)
    for p in range(4):
        delta = 128 * p
        dm[:, p * 512:(p + 1) * 512] = np.where(jj <= delta + ii, 0.0, NEG)
    iota8 = np.ascontiguousarray(
        np.broadcast_to(np.arange(E, dtype=np.float32)[None, :], (128, E)))
    iota8b = np.ascontiguousarray(iota8 + BIGI)

    nc = _get_nc()
    in_maps = []
    for core in range(8):
        seq, r = core // 4, core % 4
        in_maps.append({
            "role": np.array([[r]], dtype=np.int32),
            "x": np.ascontiguousarray(x[seq]),
            "xr": np.ascontiguousarray(x[seq, BOUNDS[r]:BOUNDS[r] + T]),
            "wq": wq, "wk": wk, "wv": wv, "wo": wo, "bo_b": bo_b,
            "wg": wg, "w1": w1, "w2": w2, "w3": w3, "fbias_b": fb_b,
            "ident": ident, "dmask": dm, "iota8": iota8, "iota8b": iota8b,
            "epsc": np.full((128, 1), EPS, dtype=np.float32),
        })
    res = run_bass_kernel_spmd(nc, in_maps, core_ids=list(range(8)))
    out = np.empty((2, S, D), dtype=np.float32)
    for core in range(8):
        seq, r = core // 4, core % 4
        out[seq, BOUNDS[r]:BOUNDS[r] + T] = res.results[core]["out"]
    return out


# version 06
# speedup vs baseline: 1.0656x; 1.0442x over previous
import math
from contextlib import ExitStack

import numpy as np

import concourse.bass as bass
import concourse.mybir as mybir
from concourse.alu_op_type import AluOpType as OP
from concourse.bass_utils import run_bass_kernel_spmd

F32 = mybir.dt.float32
BF16 = mybir.dt.bfloat16
I32 = mybir.dt.int32
AF = mybir.ActivationFunctionType
AX = mybir.AxisListType.X

D = 2048
H = 16
HD = 128
S = 4096
NROLE = 4
BOUNDS = [0, 1024, 2048, 3072, 4096]
T = 1024
NQT = T // 128
FFN = 1024
E = 8
SCALE = float(1.0 / math.sqrt(HD))
EPS = 1e-5
NEG = -1e30
BIGI = 65536.0
BIGS = 1e6
VW = H * 129


def _nj(role, qb):
    return (BOUNDS[role] + qb * 512) // 512 + 1


def build_nc(phases=99, dbg=False):
    nc = bass.Bass("TRN2", target_bir_lowering=False, debug=False,
                   detect_race_conditions=False)
    ap = lambda t: t.ap()
    role_t = ap(nc.dram_tensor("role", [1, 1], I32, kind="ExternalInput"))
    x_t = ap(nc.dram_tensor("x", [S, D], F32, kind="ExternalInput"))
    xr_t = ap(nc.dram_tensor("xr", [T, D], F32, kind="ExternalInput"))
    wq_t = ap(nc.dram_tensor("wq", [D, D], F32, kind="ExternalInput"))
    wk_t = ap(nc.dram_tensor("wk", [D, D], F32, kind="ExternalInput"))
    wv_t = ap(nc.dram_tensor("wv", [D, D], F32, kind="ExternalInput"))
    wo_t = ap(nc.dram_tensor("wo", [D, D], F32, kind="ExternalInput"))
    bo_t = ap(nc.dram_tensor("bo_b", [128, D], F32, kind="ExternalInput"))
    wg_t = ap(nc.dram_tensor("wg", [E, D], F32, kind="ExternalInput"))
    w1_t = ap(nc.dram_tensor("w1", [E, FFN, D], F32, kind="ExternalInput"))
    w2_t = ap(nc.dram_tensor("w2", [E, D, FFN], F32, kind="ExternalInput"))
    w3_t = ap(nc.dram_tensor("w3", [E, FFN, D], F32, kind="ExternalInput"))
    fb_t = ap(nc.dram_tensor("fbias_b", [128, E], F32, kind="ExternalInput"))
    id_t = ap(nc.dram_tensor("ident", [128, 128], F32, kind="ExternalInput"))
    dm_t = ap(nc.dram_tensor("dmaskt", [128, 4 * 512], F32, kind="ExternalInput"))
    io_t = ap(nc.dram_tensor("iota8", [128, E], F32, kind="ExternalInput"))
    ib_t = ap(nc.dram_tensor("iota8b", [128, E], F32, kind="ExternalInput"))
    ep_t = ap(nc.dram_tensor("epsc", [128, 1], F32, kind="ExternalInput"))
    out_t = ap(nc.dram_tensor("out", [T, D], F32, kind="ExternalOutput"))
    SCR = "ExternalOutput" if dbg else "Internal"
    kscr = ap(nc.dram_tensor("k_scr", [8, 128, H * 512], BF16, kind=SCR))
    vscr = ap(nc.dram_tensor("v_scr", [8, 128, 4 * VW], BF16, kind=SCR))
    qscr = ap(nc.dram_tensor("q_scr", [2, 128, H * 512], BF16, kind=SCR))
    cscr = ap(nc.dram_tensor("ctx_scr", [NQT, 128, VW], F32, kind=SCR))
    yscr = ap(nc.dram_tensor("y1_scr", [NQT, 128, D], F32, kind=SCR))

    with ExitStack() as es:
        role_sb = es.enter_context(nc.sbuf_tensor("role_sb", [1, 1], I32))
        identb = es.enter_context(nc.sbuf_tensor("identb", [128, 128], BF16))
        dmaskt = es.enter_context(nc.sbuf_tensor("dmaskt_sb", [128, 4 * 512], BF16))
        iota = es.enter_context(nc.sbuf_tensor("iota_sb", [128, E], F32))
        iotab = es.enter_context(nc.sbuf_tensor("iotab_sb", [128, E], F32))
        fbias = es.enter_context(nc.sbuf_tensor("fbias_sb", [128, E], F32))
        epssb = es.enter_context(nc.sbuf_tensor("eps_sb", [128, 1], F32))
        pA = es.enter_context(nc.psum_tensor("pA", [128, 512], F32))
        pB = es.enter_context(nc.psum_tensor("pB", [128, 512], F32))
        pC = es.enter_context(nc.psum_tensor("pC", [128, 512], F32))
        pD = es.enter_context(nc.psum_tensor("pD", [128, 512], F32))
        pE = es.enter_context(nc.psum_tensor("pE", [128, 512], F32))
        pF = es.enter_context(nc.psum_tensor("pF", [128, 512], F32))
        ptA = es.enter_context(nc.psum_tensor("ptA", [128, 1024], BF16))
        ptB = es.enter_context(nc.psum_tensor("ptB", [128, 1024], BF16))
        ptb = [ptA, ptB]
        SEMN = 16
        sems = [es.enter_context(nc.semaphore(f"s{i}")) for i in range(SEMN)]
        sw = [es.enter_context(nc.semaphore(f"sw{i}")) for i in range(10)]

        with nc.Block() as blk:
            @blk.sync
            def _(s):
                s.dma_start(out=role_sb[:, :], in_=role_t[:, :]).then_inc(sems[0], 16)
                s.dma_start(out=iota[:, :], in_=io_t[:, :]).then_inc(sems[0], 16)
                s.dma_start(out=iotab[:, :], in_=ib_t[:, :]).then_inc(sems[0], 16)
                s.dma_start(out=fbias[:, :], in_=fb_t[:, :]).then_inc(sems[0], 16)
                s.dma_start(out=epssb[:, :], in_=ep_t[:, :]).then_inc(sems[0], 16)
                s.wait_ge(sems[0], 80)
                s.wait_ge(sw[0], 32)
            @blk.gpsimd
            def _(g):
                g.dma_start(out=identb[:, :], in_=id_t[:, :]).then_inc(sw[0], 16)
                g.dma_start(out=dmaskt[:, :], in_=dm_t[:, :]).then_inc(sw[0], 16)

        regs = {}
        with nc.Block() as blk:
            def _mk(name):
                def _(eng):
                    r = eng.alloc_register("role_r")
                    eng.reg_load(r, role_sb[0:1, 0:1])
                    regs[name] = r
                return _
            blk.tensor(_mk("t")); blk.vector(_mk("v")); blk.scalar(_mk("c"))
            blk.sync(_mk("s")); blk.gpsimd(_mk("g"))

        def clear_block():
            with nc.Block() as blk:
                @blk.sync
                def _(s):
                    for sm in sems:
                        s.sem_clear(sm)

        clear_block()
        if phases <= 0:
            return nc


        with ExitStack() as p1:
            wkb = p1.enter_context(nc.sbuf_tensor("wkb", [128, 16 * D], BF16))
            wvb = p1.enter_context(nc.sbuf_tensor("wvb", [128, 16 * D], BF16))
            xbf = p1.enter_context(nc.sbuf_tensor("xbf", [128, 2 * D], BF16))
            xfm = p1.enter_context(nc.sbuf_tensor("xfm", [128, 16 * 512], BF16))
            ksb = p1.enter_context(nc.sbuf_tensor("ksb", [128, H * 512], BF16))
            vsb = p1.enter_context(nc.sbuf_tensor("vsb", [128, 4 * VW], BF16))

            with nc.Block() as blk:
                @blk.gpsimd
                def _(g):
                    for cc in range(16):
                        g.dma_start(out=wkb[:, cc * D:(cc + 1) * D],
                                    in_=wk_t[cc * 128:(cc + 1) * 128, :]).then_inc(sw[1], 16)
                    for cc in range(16):
                        g.dma_start(out=wvb[:, cc * D:(cc + 1) * D],
                                    in_=wv_t[cc * 128:(cc + 1) * 128, :]).then_inc(sw[1], 16)
                @blk.sync
                def _(s):
                    s.wait_ge(sw[1], 512)
            clear_block()

            sXd, sTx, sVx, sTk, sVk, sTv, sVv, sSt = sems[0:8]

            def proj_gpsimd(g, tok0, nch, sx):
                for i in range(4 * nch):
                    if i >= 2:
                        g.wait_ge(sTx, 2 * (i - 1))
                    r0 = tok0 + i * 128
                    g.dma_start(out=xbf[:, (i % 2) * D:(i % 2 + 1) * D],
                                in_=x_t[r0:r0 + 128, :]).then_inc(sx, 16)

            def proj_x_tensor(g, c, sx):
                for ts in range(4):
                    i = 4 * c + ts
                    g.wait_ge(sx, 16 * (i + 1))
                    for g2 in range(2):
                        k = 2 * i + g2
                        if k >= 2:
                            g.wait_ge(sVx, k - 1)
                        for j in range(8):
                            cc = 8 * g2 + j
                            r = g.transpose(
                                ptb[k % 2][:, (cc % 8) * 128:(cc % 8) * 128 + 128],
                                xbf[:, (i % 2) * D + cc * 128:(i % 2) * D + cc * 128 + 128],
                                identb[:, :])
                        r.then_inc(sTx, 1)

            def proj_x_vector(g, c):
                for ts in range(4):
                    i = 4 * c + ts
                    for g2 in range(2):
                        k = 2 * i + g2
                        g.wait_ge(sTx, k + 1)
                        r = g.tensor_copy(
                            bass.AP(xfm, 8 * g2 * 512 + ts * 128,
                                    [[16 * 512, 128], [512, 8], [1, 128]]),
                            ptb[k % 2][:, 0:1024].rearrange("p (b c) -> p b c", c=128))
                        r.then_inc(sVx, 1)

            def kproj_tensor(g, c, wbuf):
                g.wait_ge(sVx, 8 * (c + 1))
                for h in range(H):
                    j = 16 * c + h
                    if j >= 2:
                        g.wait_ge(sVk, j - 1)
                    p = [pA, pB][j % 2]
                    for cc in range(16):
                        r = g.matmul(p[:, :],
                                     wbuf[:, cc * D + h * 128:cc * D + h * 128 + 128],
                                     xfm[:, cc * 512:(cc + 1) * 512],
                                     start=(cc == 0), stop=(cc == 15))
                    r.then_inc(sTk, 1)

            def kproj_vector(g, c):
                for h in range(H):
                    j = 16 * c + h
                    g.wait_ge(sTk, j + 1)
                    g.tensor_copy(ksb[:, h * 512:(h + 1) * 512],
                                  [pA, pB][j % 2][:, :]).then_inc(sVk, 1)

            def p1_tensor(g, nch):
                for c in range(nch):
                    proj_x_tensor(g, c, sw[2])
                    kproj_tensor(g, c, wkb)
                    for ts in range(4):
                        for gp in range(4):
                            j = 16 * c + 4 * ts + gp
                            if j >= 2:
                                g.wait_ge(sVv, j - 1)
                            p = [pC, pD][j % 2]
                            for cc in range(16):
                                r = g.matmul(
                                    p[:, :],
                                    xfm[:, cc * 512 + ts * 128:cc * 512 + ts * 128 + 128],
                                    wvb[:, cc * D + gp * 512:cc * D + gp * 512 + 512],
                                    start=(cc == 0), stop=(cc == 15))
                            r.then_inc(sTv, 1)

            def p1_vector(g, nch):
                for ts in range(4):
                    g.memset(bass.AP(vsb, ts * VW + 128,
                                     [[4 * VW, 128], [129, H], [1, 1]]), 1.0)
                for c in range(nch):
                    if c > 0:
                        g.wait_ge(sTk, 16 * c)
                        g.wait_ge(sTv, 16 * c)
                        g.wait_ge(sSt, 32 * c)
                    proj_x_vector(g, c)
                    kproj_vector(g, c)
                    for ts in range(4):
                        for gp in range(4):
                            j = 16 * c + 4 * ts + gp
                            g.wait_ge(sTv, j + 1)
                            r = g.tensor_copy(
                                bass.AP(vsb, ts * VW + 4 * gp * 129,
                                        [[4 * VW, 128], [129, 4], [1, 128]]),
                                [pC, pD][j % 2][:, :].rearrange("p (b c) -> p b c", c=128))
                            r.then_inc(sVv, 1)

            def p1_sync(g, nch):
                for c in range(nch):
                    g.wait_ge(sVk, 16 * (c + 1))
                    g.wait_ge(sVv, 16 * (c + 1))
                    g.dma_start(out=kscr[c], in_=ksb[:, :]).then_inc(sSt, 16)
                    g.dma_start(out=vscr[c], in_=vsb[:, :]).then_inc(sSt, 16)
                g.wait_ge(sSt, 32 * nch)

            with nc.Block() as blk:
                @blk.gpsimd
                def _(g):
                    for role in range(NROLE):
                        with g.If_eq(regs["g"], role):
                            proj_gpsimd(g, 0, 2 * (role + 1), sw[2])
                @blk.tensor
                def _(t):
                    for role in range(NROLE):
                        with t.If_eq(regs["t"], role):
                            p1_tensor(t, 2 * (role + 1))
                @blk.vector
                def _(v):
                    for role in range(NROLE):
                        with v.If_eq(regs["v"], role):
                            p1_vector(v, 2 * (role + 1))
                @blk.sync
                def _(s):
                    for role in range(NROLE):
                        with s.If_eq(regs["s"], role):
                            p1_sync(s, 2 * (role + 1))
            clear_block()
            if phases <= 1:
                return nc

            with nc.Block() as blk:
                @blk.gpsimd
                def _(g):
                    for cc in range(16):
                        g.dma_start(out=wkb[:, cc * D:(cc + 1) * D],
                                    in_=wq_t[cc * 128:(cc + 1) * 128, :]).then_inc(sw[3], 16)
                @blk.sync
                def _(s):
                    s.wait_ge(sw[3], 256)
            clear_block()

            with nc.Block() as blk:
                @blk.gpsimd
                def _(g):
                    for role in range(NROLE):
                        with g.If_eq(regs["g"], role):
                            proj_gpsimd(g, BOUNDS[role], 2, sw[4])
                @blk.tensor
                def _(t):
                    for c in range(2):
                        proj_x_tensor(t, c, sw[4])
                        kproj_tensor(t, c, wkb)
                @blk.vector
                def _(v):
                    for c in range(2):
                        if c > 0:
                            v.wait_ge(sTk, 16 * c)
                            v.wait_ge(sSt, 16 * c)
                        proj_x_vector(v, c)
                        kproj_vector(v, c)
                @blk.sync
                def _(s):
                    for c in range(2):
                        s.wait_ge(sVk, 16 * (c + 1))
                        s.dma_start(out=qscr[c], in_=ksb[:, :]).then_inc(sSt, 16)
                    s.wait_ge(sSt, 32)
            clear_block()
            if phases <= 2:
                return nc

        with ExitStack() as pw:
            y1fm = pw.enter_context(nc.sbuf_tensor("y1fm", [128, 16 * T], BF16,
                                                   side="right"))
            with ExitStack() as p2:
                qbuf = p2.enter_context(nc.sbuf_tensor("qbuf", [128, H * 512], BF16))
                kbuf = p2.enter_context(nc.sbuf_tensor("kbuf", [128, 2 * H * 512], BF16))
                vbuf = p2.enter_context(nc.sbuf_tensor("vbuf", [128, 2 * 4 * VW], BF16))
                psb = p2.enter_context(nc.sbuf_tensor("psb", [128, 3 * 512], BF16))
                ptsb = p2.enter_context(nc.sbuf_tensor("ptsb", [128, 3 * 512], BF16))
                ctx = p2.enter_context(nc.sbuf_tensor("ctx", [128, 4 * VW], F32))

                sK, sTs, sVm, sAe, sTt, sVp, sTc, sVc, sSt2 = sems[0:9]

                def p2_tiles(role):
                    out = []
                    u = 0
                    for qb in range(2):
                        nj = _nj(role, qb)
                        for j in range(nj):
                            for qt in range(4):
                                for h in range(H):
                                    out.append((u, qb, j, qt, h, j == nj - 1,
                                                qt == 0 and h == 0,
                                                j == 0 and qt == 0 and h == 0))
                                    u += 1
                    return out

                def p2_sync(g, role):
                    kl = 0
                    slot_tenant_end = {0: None, 1: None}
                    tiles = 0
                    for qb in range(2):
                        nj = _nj(role, qb)
                        if qb == 1:
                            g.wait_ge(sTs, tiles)
                        g.dma_start(out=qbuf[:, :], in_=qscr[qb]).then_inc(sK, 16)
                        kl += 16
                        for j in range(nj):
                            sl = j % 2
                            if slot_tenant_end[sl] is not None:
                                g.wait_ge(sTc, slot_tenant_end[sl])
                            g.dma_start(out=kbuf[:, sl * H * 512:(sl + 1) * H * 512],
                                        in_=kscr[j]).then_inc(sK, 16)
                            g.dma_start(out=vbuf[:, sl * 4 * VW:(sl + 1) * 4 * VW],
                                        in_=vscr[j]).then_inc(sK, 16)
                            kl += 32
                            slot_tenant_end[sl] = tiles + 64 * (j + 1)
                        tiles += 64 * nj
                        g.wait_ge(sVc, tiles)
                        for qt in range(4):
                            g.dma_start(out=cscr[4 * qb + qt],
                                        in_=ctx[:, qt * VW:(qt + 1) * VW]).then_inc(sSt2, 16)
                    g.wait_ge(sSt2, 128)
                    g.wait_ge(sK, kl)

                def p2_tensor(g, role):
                    tiles = p2_tiles(role)
                    n = len(tiles)
                    sc = [pA, pB, pE]
                    pv = [pC, pD, pF]

                    def emit_qk(t):
                        u, qb, j, qt, h, diag, foc, foqb = t
                        if foqb:
                            pass
                        if foc:
                            kl = 16 * (qb + 1) + 32 * (j + 1 + (qb * _nj(role, 0)))
                            g.wait_ge(sK, kl)
                        if u >= 3:
                            g.wait_ge(sAe, u - 2)
                        g.matmul(sc[u % 3][:, :],
                                 qbuf[:, h * 512 + qt * 128:h * 512 + qt * 128 + 128],
                                 kbuf[:, (j % 2) * H * 512 + h * 512:(j % 2) * H * 512 + (h + 1) * 512],
                                 start=True, stop=True).then_inc(sTs, 1)

                    qb_of = [t[1] for t in tiles]
                    emit_qk(tiles[0])
                    if n > 1 and qb_of[1] == qb_of[0]:
                        emit_qk(tiles[1])
                    for t in tiles:
                        u, qb, j, qt, h, diag, foc, foqb = t
                        if foqb and qb == 1:
                            emit_qk(tiles[u])
                            if u + 1 < n:
                                emit_qk(tiles[u + 1])
                        if u + 2 < n and qb_of[u + 2] == qb:
                            emit_qk(tiles[u + 2])
                        g.wait_ge(sAe, u + 1)
                        if u >= 2:
                            g.wait_ge(sVp, u - 1)
                        for kk in range(4):
                            r = g.transpose(ptb[u % 2][:, kk * 128:kk * 128 + 128],
                                            psb[:, (u % 3) * 512 + kk * 128:(u % 3) * 512 + kk * 128 + 128],
                                            identb[:, :])
                        r.then_inc(sTt, 1)
                        g.wait_ge(sVp, u + 1)
                        if u >= 3:
                            g.wait_ge(sVc, u - 2)
                        pc = pv[u % 3]
                        vb = (j % 2) * 4 * VW
                        for kk in range(4):
                            r = g.matmul(
                                pc[:, 0:129],
                                ptsb[:, (u % 3) * 512 + kk * 128:(u % 3) * 512 + kk * 128 + 128],
                                vbuf[:, vb + kk * VW + h * 129:vb + kk * VW + (h + 1) * 129],
                                start=(kk == 0), stop=(kk == 3))
                        r.then_inc(sTc, 1)

                def p2_vector(g, role):
                    pv = [pC, pD, pF]
                    for t in p2_tiles(role):
                        u, qb, j, qt, h, diag, foc, foqb = t
                        if foqb and qb == 1:
                            g.wait_ge(sSt2, 64)
                        g.wait_ge(sTt, u + 1)
                        if u >= 3:
                            g.wait_ge(sTc, u - 2)
                        if diag:
                            g.tensor_tensor(ptsb[:, (u % 3) * 512:(u % 3) * 512 + 512],
                                            ptb[u % 2][:, 0:512],
                                            dmaskt[:, qt * 512:(qt + 1) * 512],
                                            op=OP.mult).then_inc(sVp, 1)
                        else:
                            g.tensor_copy(ptsb[:, (u % 3) * 512:(u % 3) * 512 + 512],
                                          ptb[u % 2][:, 0:512]).then_inc(sVp, 1)
                        g.wait_ge(sTc, u + 1)
                        dst = ctx[:, qt * VW + h * 129:qt * VW + (h + 1) * 129]
                        if j == 0:
                            g.tensor_copy(dst, pv[u % 3][:, 0:129]).then_inc(sVc, 1)
                        else:
                            g.tensor_tensor(dst, pv[u % 3][:, 0:129], dst,
                                            op=OP.add).then_inc(sVc, 1)

                def p2_scalar(g, role):
                    sc = [pA, pB, pE]
                    for t in p2_tiles(role):
                        u, qb, j, qt, h, diag, foc, foqb = t
                        g.wait_ge(sTs, u + 1)
                        if u >= 3:
                            g.wait_ge(sTt, u - 2)
                        g.activation(psb[:, (u % 3) * 512:(u % 3) * 512 + 512],
                                     sc[u % 3][:, :], AF.Exp,
                                     bias=0.0, scale=SCALE).then_inc(sAe, 1)

                with nc.Block() as blk:
                    @blk.sync
                    def _(s):
                        for role in range(NROLE):
                            with s.If_eq(regs["s"], role):
                                p2_sync(s, role)
                    @blk.tensor
                    def _(t):
                        for role in range(NROLE):
                            with t.If_eq(regs["t"], role):
                                p2_tensor(t, role)
                    @blk.vector
                    def _(v):
                        for role in range(NROLE):
                            with v.If_eq(regs["v"], role):
                                p2_vector(v, role)
                    @blk.scalar
                    def _(c):
                        for role in range(NROLE):
                            with c.If_eq(regs["c"], role):
                                p2_scalar(c, role)
                clear_block()
                if phases <= 3:
                    return nc

            with ExitStack() as p3:
                wob = p3.enter_context(nc.sbuf_tensor("wob", [128, 16 * D], BF16))
                y1bt = p3.enter_context(nc.sbuf_tensor("y1bt", [128, D], BF16))
                ctxin = p3.enter_context(nc.sbuf_tensor("ctxin", [128, 2 * VW], F32))
                xrb = p3.enter_context(nc.sbuf_tensor("xrb", [128, 2 * D], F32))
                bo_b = p3.enter_context(nc.sbuf_tensor("bo_sb", [128, D], F32))
                ctxnb = p3.enter_context(nc.sbuf_tensor("ctxnb", [128, D], BF16))
                ctxfm = p3.enter_context(nc.sbuf_tensor("ctxfm", [128, D], BF16))
                acc = p3.enter_context(nc.sbuf_tensor("acc3", [128, D], F32))
                sq = p3.enter_context(nc.sbuf_tensor("sq3", [128, D], F32))
                y1t = p3.enter_context(nc.sbuf_tensor("y1t", [128, D], F32))
                rl = p3.enter_context(nc.sbuf_tensor("rl3", [128, H], F32))
                st1 = p3.enter_context(nc.sbuf_tensor("st13", [128, 8], F32))

                (sK, sVn, sTf, sVf, sTo, sVs, sVa, sAq, sVr, sAr, sVy, sAy,
                 sSt3, sVz) = sems[0:14]

                def ln_vector(v, i, acc_ap, sq_ap, out_ap):
                    if i >= 1:
                        v.wait_ge(sAr, 2 * i)
                    v.reduce_sum(st1[:, 0:1], acc_ap, axis=AX).then_inc(sVa, 1)
                    v.wait_ge(sAq, i + 1)
                    v.reduce_sum(st1[:, 1:2], sq_ap, axis=AX)
                    v.tensor_scalar(st1[:, 2:3], st1[:, 0:1], 1.0 / D, None, op0=OP.mult)
                    v.tensor_scalar(st1[:, 3:4], st1[:, 1:2], 1.0 / D, None, op0=OP.mult)
                    v.tensor_copy(st1[:, 4:5], st1[:, 0:1])
                    v.scalar_tensor_tensor(st1[:, 5:6], st1[:, 2:3], st1[:, 2:3],
                                           st1[:, 3:4], op0=OP.mult,
                                           op1=OP.subtract).then_inc(sVr, 1)
                    v.wait_ge(sAr, 2 * (i + 1))
                    v.tensor_scalar(out_ap, acc_ap, st1[:, 2:3], st1[:, 6:7],
                                    op0=OP.subtract, op1=OP.mult).then_inc(sVy, 1)

                def ln_scalar(c, i, acc_ap, sq_ap):
                    c.wait_ge(sVa, i + 1)
                    c.activation(sq_ap, acc_ap, AF.Square, bias=0.0, scale=1.0).then_inc(sAq, 1)
                    c.wait_ge(sVr, i + 1)
                    c.activation(st1[:, 4:5], st1[:, 5:6], AF.Ln,
                                 bias=epssb[:, :], scale=-1.0).then_inc(sAr, 1)
                    c.wait_ge(sAr, 2 * i + 1)
                    c.activation(st1[:, 6:7], st1[:, 4:5], AF.Exp,
                                 bias=0.0, scale=-0.5).then_inc(sAr, 1)

                with nc.Block() as blk:
                    @blk.gpsimd
                    def _(g):
                        for cc in range(16):
                            g.dma_start(out=wob[:, cc * D:(cc + 1) * D],
                                        in_=wo_t[cc * 128:(cc + 1) * 128, :]).then_inc(sw[5], 16)
                        g.wait_ge(sw[5], 256)
                    @blk.sync
                    def _(s):
                        s.dma_start(out=bo_b[:, :], in_=bo_t[:, :]).then_inc(sK, 16)
                        for qt in range(NQT):
                            if qt >= 1:
                                s.wait_ge(sVy, qt)
                                s.dma_start(out=yscr[qt - 1],
                                            in_=y1t[:, :]).then_inc(sSt3, 16)
                            if qt >= 2:
                                s.wait_ge(sVn, qt - 1)
                                s.wait_ge(sVs, 5 * (qt - 1))
                            s.dma_start(out=ctxin[:, (qt % 2) * VW:(qt % 2 + 1) * VW],
                                        in_=cscr[qt]).then_inc(sK, 16)
                            s.dma_start(out=xrb[:, (qt % 2) * D:(qt % 2 + 1) * D],
                                        in_=xr_t[qt * 128:(qt + 1) * 128, :]).then_inc(sK, 16)
                        s.wait_ge(sVy, NQT)
                        s.dma_start(out=yscr[NQT - 1], in_=y1t[:, :]).then_inc(sSt3, 16)
                        s.wait_ge(sSt3, 16 * NQT)
                        s.wait_ge(sK, 16 + 32 * NQT)
                    @blk.vector
                    def _(v):
                        for qt in range(NQT):
                            v.wait_ge(sK, 16 + 32 * (qt + 1))
                            if qt >= 1:
                                v.wait_ge(sTf, 2 * qt)
                            cb = (qt % 2) * VW
                            for h in range(H):
                                v.reciprocal(rl[:, h:h + 1],
                                             ctxin[:, cb + h * 129 + 128:cb + h * 129 + 129])
                            for h in range(H):
                                r = v.tensor_scalar(ctxnb[:, h * 128:(h + 1) * 128],
                                                    ctxin[:, cb + h * 129:cb + h * 129 + 128],
                                                    rl[:, h:h + 1], None, op0=OP.mult)
                            r.then_inc(sVn, 1)
                            if qt >= 1:
                                v.wait_ge(sTo, 4 * qt)
                            for g2 in range(2):
                                k = 2 * qt + g2
                                v.wait_ge(sTf, k + 1)
                                v.tensor_copy(
                                    bass.AP(ctxfm, 8 * g2 * 128,
                                            [[D, 128], [128, 8], [1, 128]]),
                                    ptb[k % 2][:, 0:1024].rearrange("p (b c) -> p b c", c=128)
                                ).then_inc(sVf, 1)
                            for gp in range(4):
                                v.wait_ge(sTo, 4 * qt + gp + 1)
                                v.tensor_tensor(acc[:, gp * 512:(gp + 1) * 512],
                                                [pE, pF][gp % 2][:, :],
                                                bo_b[:, gp * 512:(gp + 1) * 512],
                                                op=OP.add).then_inc(sVs, 1)
                            r = v.tensor_tensor(acc[:, :], acc[:, :],
                                                xrb[:, (qt % 2) * D:(qt % 2 + 1) * D],
                                                op=OP.add)
                            r.then_inc(sVs, 1)
                            if qt >= 1:
                                v.wait_ge(sSt3, 16 * qt)
                            ln_vector(v, qt, acc[:, :], sq[:, :], y1t[:, :])
                    @blk.scalar
                    def _(c):
                        for qt in range(NQT):
                            ln_scalar(c, qt, acc[:, :], sq[:, :])
                    @blk.tensor
                    def _(t):
                        t.wait_ge(sw[5], 256)
                        for qt in range(NQT):
                            t.wait_ge(sVn, qt + 1)
                            for g2 in range(2):
                                k = 2 * qt + g2
                                if k >= 2:
                                    t.wait_ge(sVf, k - 1)
                                for j in range(8):
                                    cc = 8 * g2 + j
                                    r = t.transpose(
                                        ptb[k % 2][:, (cc % 8) * 128:(cc % 8) * 128 + 128],
                                        ctxnb[:, cc * 128:cc * 128 + 128],
                                        identb[:, :])
                                r.then_inc(sTf, 1)
                            t.wait_ge(sVf, 2 * (qt + 1))
                            for gp in range(4):
                                i = 4 * qt + gp
                                if i >= 2:
                                    t.wait_ge(sVs, 5 * ((i - 2) // 4) + ((i - 2) % 4) + 1)
                                p = [pE, pF][gp % 2]
                                for cc in range(16):
                                    r = t.matmul(p[:, :], ctxfm[:, cc * 128:cc * 128 + 128],
                                                 wob[:, cc * D + gp * 512:cc * D + gp * 512 + 512],
                                                 start=(cc == 0), stop=(cc == 15))
                                r.then_inc(sTo, 1)
                clear_block()
                if phases <= 4:
                    return nc

                sK2, sTx2, sVx2 = sems[0:3]
                with nc.Block() as blk:
                    @blk.gpsimd
                    def _(g):
                        for qt in range(NQT):
                            if qt >= 1:
                                g.wait_ge(sTx2, 2 * qt)
                            g.dma_start(out=y1bt[:, :], in_=yscr[qt]).then_inc(sw[6], 16)
                        g.wait_ge(sw[6], 16 * NQT)
                    @blk.tensor
                    def _(t):
                        for qt in range(NQT):
                            t.wait_ge(sw[6], 16 * (qt + 1))
                            for g2 in range(2):
                                k = 2 * qt + g2
                                if k >= 2:
                                    t.wait_ge(sVx2, k - 1)
                                for j in range(8):
                                    cc = 8 * g2 + j
                                    r = t.transpose(
                                        ptb[k % 2][:, (cc % 8) * 128:(cc % 8) * 128 + 128],
                                        y1bt[:, cc * 128:cc * 128 + 128],
                                        identb[:, :])
                                r.then_inc(sTx2, 1)
                    @blk.vector
                    def _(v):
                        for qt in range(NQT):
                            for g2 in range(2):
                                k = 2 * qt + g2
                                v.wait_ge(sTx2, k + 1)
                                v.tensor_copy(
                                    bass.AP(y1fm, (8 * g2) * T + qt * 128,
                                            [[16 * T, 128], [T, 8], [1, 128]]),
                                    ptb[k % 2][:, 0:1024].rearrange("p (b c) -> p b c", c=128)
                                ).then_inc(sVx2, 1)
                    @blk.sync
                    def _(s):
                        s.wait_ge(sVx2, 2 * NQT)
                clear_block()
                if phases <= 5:
                    return nc

            accum = pw.enter_context(nc.sbuf_tensor("accum", [128, NQT * D], BF16,
                                                    side="right"))
            ewbuf = pw.enter_context(nc.sbuf_tensor("ew4", [128, NQT * E], F32,
                                                    side="right"))
            with ExitStack() as p4:
                w2t = p4.enter_context(nc.sbuf_tensor("w2t", [128, 8 * D], BF16))
                w1tb = p4.enter_context(nc.sbuf_tensor("w1tb", [128, 2 * D], BF16))
                w3tb = p4.enter_context(nc.sbuf_tensor("w3tb", [128, 2 * D], BF16))
                wsl = p4.enter_context(nc.sbuf_tensor("wsl", [128, 2 * D], BF16))
                wsl2 = p4.enter_context(nc.sbuf_tensor("wsl2", [128, 2 * FFN], BF16))
                gsb = p4.enter_context(nc.sbuf_tensor("gsb", [128, 8 * T], BF16))
                h1s = p4.enter_context(nc.sbuf_tensor("h1s4", [128, 2 * 512], F32))
                rw = p4.enter_context(nc.sbuf_tensor("rw4", [128, NQT * E], F32))
                wgf = p4.enter_context(nc.sbuf_tensor("wgf4", [128, 16 * E], BF16))
                wgb = p4.enter_context(nc.sbuf_tensor("wgb4", [8, D], BF16))
                tkb = p4.enter_context(nc.sbuf_tensor("tkb4", [128, NQT * 64], F32))
                tmpo = p4.enter_context(nc.sbuf_tensor("tmpo4", [128, 2 * 512], BF16))

                (sK, sTw, sVw, sTr, sAs, sVe, sX1, sX2, sTh, sAs2, sVg, sTo2,
                 sAo, sVo) = sems[0:14]

                with nc.Block() as blk:
                    @blk.gpsimd
                    def _(g):
                        g.dma_start(out=wgb[:, :], in_=wg_t[:, :]).then_inc(sw[7], 16)
                        g.memset(accum[:, :], 0.0)
                        g.wait_ge(sw[7], 16)
                    @blk.tensor
                    def _(t):
                        t.wait_ge(sw[7], 16)
                        for cc in range(16):
                            if cc >= 2:
                                t.wait_ge(sVw, cc - 1)
                            r = t.transpose(ptb[cc % 2][:, 0:8],
                                            wgb[0:8, cc * 128:(cc + 1) * 128],
                                            identb[0:8, 0:8])
                            r.then_inc(sTw, 1)
                        t.wait_ge(sVw, 16)
                        for qt in range(NQT):
                            if qt >= 2:
                                t.wait_ge(sAs, qt - 1)
                            p = [pA, pB][qt % 2]
                            for cc in range(16):
                                r = t.matmul(p[:, 0:8],
                                             y1fm[:, cc * T + qt * 128:cc * T + qt * 128 + 128],
                                             wgf[:, cc * 8:(cc + 1) * 8],
                                             start=(cc == 0), stop=(cc == 15))
                            r.then_inc(sTr, 1)
                    @blk.scalar
                    def _(c):
                        for qt in range(NQT):
                            c.wait_ge(sTr, qt + 1)
                            c.activation(rw[:, qt * E:(qt + 1) * E],
                                         [pA, pB][qt % 2][:, 0:8],
                                         AF.Sigmoid, bias=0.0, scale=1.0).then_inc(sAs, 1)
                    @blk.vector
                    def _(v):
                        for cc in range(16):
                            v.wait_ge(sTw, cc + 1)
                            v.tensor_copy(wgf[:, cc * 8:(cc + 1) * 8],
                                          ptb[cc % 2][:, 0:8]).then_inc(sVw, 1)
                        v.wait_ge(sAs, NQT)
                        C = 64
                        def col(q, a, b=None):
                            b = a + 1 if b is None else b
                            return tkb[:, q * C + a:q * C + b]
                        rwq = lambda q: rw[:, q * E:(q + 1) * E]
                        for q in range(NQT):
                            v.tensor_tensor(col(q, 0, 8), rwq(q), fbias[:, :], op=OP.add)
                        for q in range(NQT):
                            v.reduce_max(col(q, 48), col(q, 0, 8), axis=AX)
                        for q in range(NQT):
                            v.tensor_scalar(col(q, 8, 16), col(q, 0, 8), col(q, 48), None, op0=OP.is_ge)
                        for q in range(NQT):
                            v.scalar_tensor_tensor(col(q, 16, 24), col(q, 8, 16), -BIGI,
                                                   iotab[:, :], op0=OP.mult, op1=OP.add)
                        for q in range(NQT):
                            v.tensor_reduce(col(q, 49), col(q, 16, 24), axis=AX, op=OP.min)
                        for q in range(NQT):
                            v.tensor_scalar(col(q, 24, 32), iota[:, :], col(q, 49), None, op0=OP.is_equal)
                        for q in range(NQT):
                            v.tensor_tensor(col(q, 16, 24), col(q, 24, 32), rwq(q), op=OP.mult)
                        for q in range(NQT):
                            v.reduce_sum(col(q, 50), col(q, 16, 24), axis=AX)
                        for q in range(NQT):
                            v.scalar_tensor_tensor(col(q, 40, 48), col(q, 24, 32), -BIGS,
                                                   col(q, 0, 8), op0=OP.mult, op1=OP.add)
                        for q in range(NQT):
                            v.reduce_max(col(q, 51), col(q, 40, 48), axis=AX)
                        for q in range(NQT):
                            v.tensor_scalar(col(q, 8, 16), col(q, 40, 48), col(q, 51), None, op0=OP.is_ge)
                        for q in range(NQT):
                            v.scalar_tensor_tensor(col(q, 16, 24), col(q, 8, 16), -BIGI,
                                                   iotab[:, :], op0=OP.mult, op1=OP.add)
                        for q in range(NQT):
                            v.tensor_reduce(col(q, 52), col(q, 16, 24), axis=AX, op=OP.min)
                        for q in range(NQT):
                            v.tensor_scalar(col(q, 32, 40), iota[:, :], col(q, 52), None, op0=OP.is_equal)
                        for q in range(NQT):
                            v.tensor_tensor(col(q, 16, 24), col(q, 32, 40), rwq(q), op=OP.mult)
                        for q in range(NQT):
                            v.reduce_sum(col(q, 53), col(q, 16, 24), axis=AX)
                        for q in range(NQT):
                            v.tensor_tensor(col(q, 54), col(q, 50), col(q, 53), op=OP.add)
                        for q in range(NQT):
                            v.reciprocal(col(q, 55), col(q, 54))
                        for q in range(NQT):
                            v.tensor_tensor(col(q, 56), col(q, 50), col(q, 55), op=OP.mult)
                        for q in range(NQT):
                            v.tensor_tensor(col(q, 57), col(q, 53), col(q, 55), op=OP.mult)
                        for q in range(NQT):
                            v.tensor_scalar(col(q, 16, 24), col(q, 24, 32), col(q, 56), None, op0=OP.mult)
                        for q in range(NQT):
                            v.scalar_tensor_tensor(ewbuf[:, q * E:(q + 1) * E],
                                                   col(q, 32, 40), col(q, 57),
                                                   col(q, 16, 24), op0=OP.mult,
                                                   op1=OP.add).then_inc(sVe, 1)
                        v.wait_ge(sVe, NQT)
                clear_block()
                if phases <= 6:
                    return nc

                (sX1, sTw, sVw, sX2, sTw2, sVw2, sTh, sAs2, sVg, sTo2,
                 sAo, sVo, sPad1, sPad2) = sems[0:14]

                def p4_gpsimd(g):
                    for e in range(E):
                        for fb in range(8):
                            for wi, wt in ((0, w1_t), (1, w3_t)):
                                w = 16 * e + 2 * fb + wi
                                if w >= 2:
                                    g.wait_ge(sTw, 2 * (w - 1))
                                g.dma_start(out=wsl[:, (w % 2) * D:(w % 2 + 1) * D],
                                            in_=wt[e, fb * 128:(fb + 1) * 128, :]).then_inc(sw[8], 16)
                        for hc in range(16):
                            w = 16 * e + hc
                            if w >= 2:
                                g.wait_ge(sTw2, w - 1)
                            g.dma_start(out=wsl2[:, (w % 2) * FFN:(w % 2 + 1) * FFN],
                                        in_=w2_t[e, hc * 128:(hc + 1) * 128, :]).then_inc(sw[9], 16)
                    g.wait_ge(sw[8], 16 * 16 * E)
                    g.wait_ge(sw[9], 16 * 16 * E)

                def p4_tensor(g):
                    for e in range(E):
                        for fb in range(8):
                            for wi in (0, 1):
                                w = 16 * e + 2 * fb + wi
                                g.wait_ge(sw[8], 16 * (w + 1))
                                if w >= 1:
                                    g.wait_ge(sVw, 2 * w)
                                if e > 0 and fb == 0 and wi == 0:
                                    g.wait_ge(sVw2, 16 * e)
                                for g2 in range(2):
                                    for j in range(8):
                                        cc = 8 * g2 + j
                                        r = g.transpose(
                                            ptb[(2 * w + g2) % 2][:, (cc % 8) * 128:(cc % 8) * 128 + 128],
                                            wsl[:, (w % 2) * D + cc * 128:(w % 2) * D + cc * 128 + 128],
                                            identb[:, :])
                                    r.then_inc(sTw, 1)
                            g.wait_ge(sVw, 2 * (16 * e + 2 * fb + 2))
                            for tb in range(2):
                                m = 32 * e + 4 * fb + 2 * tb
                                if m // 2 >= 2:
                                    g.wait_ge(sAs2, m // 2 - 1)
                                for cc in range(16):
                                    r = g.matmul(pA[:, :] if tb == 0 else pC[:, :],
                                                 w1tb[:, (fb % 2) * D + cc * 128:(fb % 2) * D + cc * 128 + 128],
                                                 y1fm[:, cc * T + tb * 512:cc * T + (tb + 1) * 512],
                                                 start=(cc == 0), stop=(cc == 15))
                                r.then_inc(sTh, 1)
                                if m // 2 >= 2:
                                    g.wait_ge(sVg, m // 2 - 1)
                                for cc in range(16):
                                    r = g.matmul(pB[:, :] if tb == 0 else pD[:, :],
                                                 w3tb[:, (fb % 2) * D + cc * 128:(fb % 2) * D + cc * 128 + 128],
                                                 y1fm[:, cc * T + tb * 512:cc * T + (tb + 1) * 512],
                                                 start=(cc == 0), stop=(cc == 15))
                                r.then_inc(sTh, 1)
                        for hc in range(16):
                            w = 16 * e + hc
                            g.wait_ge(sw[9], 16 * (w + 1))
                            if hc < 2:
                                g.wait_ge(sVw, 32 * (e + 1))
                            if w >= 2:
                                g.wait_ge(sVw2, w - 1)
                            for j in range(8):
                                r = g.transpose(
                                    ptb[w % 2][:, j * 128:j * 128 + 128],
                                    wsl2[:, (w % 2) * FFN + j * 128:(w % 2) * FFN + j * 128 + 128],
                                    identb[:, :])
                            r.then_inc(sTw2, 1)
                        g.wait_ge(sVg, 16 * (e + 1))
                        g.wait_ge(sVw2, 16 * (e + 1))
                        for qt in range(NQT):
                            for gp in range(4):
                                i = 32 * e + 4 * qt + gp
                                if i >= 2:
                                    g.wait_ge(sAo, i - 1)
                                p = [pE, pF][i % 2]
                                for fc in range(8):
                                    r = g.matmul(
                                        p[:, :],
                                        gsb[:, fc * T + qt * 128:fc * T + qt * 128 + 128],
                                        w2t[:, fc * D + gp * 512:fc * D + gp * 512 + 512],
                                        start=(fc == 0), stop=(fc == 7))
                                r.then_inc(sTo2, 1)

                def p4_scalar(g):
                    for e in range(E):
                        for fb in range(8):
                            for tb in range(2):
                                m = 32 * e + 4 * fb + 2 * tb
                                sl = (m // 2) % 2
                                g.wait_ge(sTh, m + 1)
                                if m // 2 >= 2:
                                    g.wait_ge(sVg, m // 2 - 1)
                                g.activation(h1s[:, sl * 512:(sl + 1) * 512],
                                             pA[:, :] if tb == 0 else pC[:, :],
                                             AF.Silu, bias=0.0, scale=1.0).then_inc(sAs2, 1)
                        for qt in range(NQT):
                            for gp in range(4):
                                i = 32 * e + 4 * qt + gp
                                g.wait_ge(sTo2, i + 1)
                                if i >= 2:
                                    g.wait_ge(sVo, i - 1)
                                g.activation(tmpo[:, (i % 2) * 512:(i % 2 + 1) * 512],
                                             [pE, pF][i % 2][:, :], AF.Copy,
                                             bias=0.0,
                                             scale=ewbuf[:, qt * E + e:qt * E + e + 1]).then_inc(sAo, 1)

                def p4_vector(g):
                    for e in range(E):
                        for fb in range(8):
                            for wi, dstb in ((0, w1tb), (1, w3tb)):
                                w = 16 * e + 2 * fb + wi
                                if 8 * e + fb >= 2:
                                    g.wait_ge(sTh, 4 * (8 * e + fb - 1))
                                for g2 in range(2):
                                    g.wait_ge(sTw, 2 * w + g2 + 1)
                                    g.tensor_copy(
                                        dstb[:, (fb % 2) * D + g2 * 1024:(fb % 2) * D + (g2 + 1) * 1024],
                                        ptb[(2 * w + g2) % 2][:, 0:1024]).then_inc(sVw, 1)
                            for tb in range(2):
                                m = 32 * e + 4 * fb + 2 * tb
                                sl = (m // 2) % 2
                                g.wait_ge(sAs2, m // 2 + 1)
                                g.wait_ge(sTh, m + 2)
                                g.tensor_tensor(
                                    gsb[:, fb * T + tb * 512:fb * T + (tb + 1) * 512],
                                    h1s[:, sl * 512:(sl + 1) * 512],
                                    pB[:, :] if tb == 0 else pD[:, :],
                                    op=OP.mult).then_inc(sVg, 1)
                        for hc in range(16):
                            w = 16 * e + hc
                            g.wait_ge(sTw2, w + 1)
                            if e > 0 and hc == 0:
                                g.wait_ge(sTo2, 32 * e)
                            g.tensor_copy(
                                bass.AP(w2t, hc * 128, [[8 * D, 128], [D, 8], [1, 128]]),
                                ptb[w % 2][:, 0:1024].rearrange("p (b c) -> p b c", c=128)
                            ).then_inc(sVw2, 1)
                        for qt in range(NQT):
                            for gp in range(4):
                                i = 32 * e + 4 * qt + gp
                                g.wait_ge(sAo, i + 1)
                                dst = accum[:, qt * D + gp * 512:qt * D + (gp + 1) * 512]
                                g.tensor_tensor(dst, tmpo[:, (i % 2) * 512:(i % 2 + 1) * 512],
                                                dst, op=OP.add).then_inc(sVo, 1)


                with nc.Block() as blk:
                    @blk.gpsimd
                    def _(g):
                        p4_gpsimd(g)
                    @blk.tensor
                    def _(t):
                        p4_tensor(t)
                    @blk.scalar
                    def _(c):
                        p4_scalar(c)
                    @blk.vector
                    def _(v):
                        p4_vector(v)
                clear_block()
                if phases <= 7:
                    return nc

            with ExitStack() as p5:
                y1in = p5.enter_context(nc.sbuf_tensor("y1in", [128, 2 * D], F32))
                acc5 = p5.enter_context(nc.sbuf_tensor("acc5", [128, D], F32))
                sq5 = p5.enter_context(nc.sbuf_tensor("sq5", [128, D], F32))
                y2t = p5.enter_context(nc.sbuf_tensor("y2t", [128, 2 * D], F32))
                st1 = p5.enter_context(nc.sbuf_tensor("st15", [128, 8], F32))

                sK, sVa, sAq, sVr, sAr, sVy, sAy, sSt5 = sems[0:8]

                with nc.Block() as blk:
                    @blk.sync
                    def _(s):
                        for qt in range(NQT):
                            if qt >= 1:
                                s.wait_ge(sVy, qt)
                                s.dma_start(out=out_t[(qt - 1) * 128:qt * 128, :],
                                            in_=y2t[:, ((qt - 1) % 2) * D:((qt - 1) % 2 + 1) * D]).then_inc(sSt5, 16)
                            if qt >= 2:
                                s.wait_ge(sVa, qt - 1)
                            s.dma_start(out=y1in[:, (qt % 2) * D:(qt % 2 + 1) * D],
                                        in_=yscr[qt]).then_inc(sK, 16)
                        s.wait_ge(sVy, NQT)
                        s.dma_start(out=out_t[(NQT - 1) * 128:NQT * 128, :],
                                    in_=y2t[:, ((NQT - 1) % 2) * D:((NQT - 1) % 2 + 1) * D]).then_inc(sSt5, 16)
                        s.wait_ge(sSt5, 16 * NQT)
                        s.wait_ge(sK, 16 * NQT)
                    @blk.vector
                    def _(v):
                        for qt in range(NQT):
                            v.wait_ge(sK, 16 * (qt + 1))
                            v.tensor_tensor(acc5[:, :],
                                            accum[:, qt * D:(qt + 1) * D],
                                            y1in[:, (qt % 2) * D:(qt % 2 + 1) * D],
                                            op=OP.add)
                            if qt >= 1:
                                v.wait_ge(sAr, 2 * qt)
                            v.reduce_sum(st1[:, 0:1], acc5[:, :], axis=AX).then_inc(sVa, 1)
                            v.wait_ge(sAq, qt + 1)
                            v.reduce_sum(st1[:, 1:2], sq5[:, :], axis=AX)
                            v.tensor_scalar(st1[:, 2:3], st1[:, 0:1], 1.0 / D, None, op0=OP.mult)
                            v.tensor_scalar(st1[:, 3:4], st1[:, 1:2], 1.0 / D, None, op0=OP.mult)
                            v.tensor_copy(st1[:, 4:5], st1[:, 0:1])
                            v.scalar_tensor_tensor(st1[:, 5:6], st1[:, 2:3], st1[:, 2:3],
                                                   st1[:, 3:4], op0=OP.mult,
                                                   op1=OP.subtract).then_inc(sVr, 1)
                            v.wait_ge(sAr, 2 * (qt + 1))
                            if qt >= 2:
                                v.wait_ge(sSt5, 16 * (qt - 1))
                            v.tensor_scalar(y2t[:, (qt % 2) * D:(qt % 2 + 1) * D],
                                            acc5[:, :], st1[:, 2:3], st1[:, 6:7],
                                            op0=OP.subtract, op1=OP.mult).then_inc(sVy, 1)
                    @blk.scalar
                    def _(c):
                        for qt in range(NQT):
                            c.wait_ge(sVa, qt + 1)
                            c.activation(sq5[:, :], acc5[:, :], AF.Square,
                                         bias=0.0, scale=1.0).then_inc(sAq, 1)
                            c.wait_ge(sVr, qt + 1)
                            c.activation(st1[:, 4:5], st1[:, 5:6], AF.Ln,
                                         bias=epssb[:, :], scale=-1.0).then_inc(sAr, 1)
                            c.wait_ge(sAr, 2 * qt + 1)
                            c.activation(st1[:, 6:7], st1[:, 4:5], AF.Exp,
                                         bias=0.0, scale=-0.5).then_inc(sAr, 1)
    return nc


_NC_CACHE = {}


def _get_nc():
    if "nc" not in _NC_CACHE:
        _NC_CACHE["nc"] = build_nc()
    return _NC_CACHE["nc"]


def kernel(L0_attn_x, L0_attn_Wq, L0_attn_Wk, L0_attn_Wv, L0_attn_Wo,
           L0_attn_bo, L0_ffn_Wg, L0_ffn_W1, L0_ffn_W2, L0_ffn_W3,
           L0_ffn_bias):
    x = np.ascontiguousarray(np.asarray(L0_attn_x, dtype=np.float32))
    wq = np.ascontiguousarray(np.asarray(L0_attn_Wq, dtype=np.float32))
    wk = np.ascontiguousarray(np.asarray(L0_attn_Wk, dtype=np.float32))
    wv = np.ascontiguousarray(np.asarray(L0_attn_Wv, dtype=np.float32))
    wo = np.ascontiguousarray(np.asarray(L0_attn_Wo, dtype=np.float32))
    bo = np.asarray(L0_attn_bo, dtype=np.float32)
    wg = np.ascontiguousarray(np.asarray(L0_ffn_Wg, dtype=np.float32))
    w1 = np.ascontiguousarray(np.asarray(L0_ffn_W1, dtype=np.float32))
    w2 = np.ascontiguousarray(np.asarray(L0_ffn_W2, dtype=np.float32))
    w3 = np.ascontiguousarray(np.asarray(L0_ffn_W3, dtype=np.float32))
    fb = np.asarray(L0_ffn_bias, dtype=np.float32)

    bo_b = np.ascontiguousarray(np.broadcast_to(bo[None, :], (128, D)))
    fb_b = np.ascontiguousarray(np.broadcast_to(fb[None, :], (128, E)))
    ident = np.eye(128, dtype=np.float32)
    kk_ = np.arange(128, dtype=np.int64)[:, None]
    dm = np.zeros((128, 4 * 512), np.float32)
    for p in range(4):
        for kk in range(4):
            ql = np.arange(128, dtype=np.int64)[None, :]
            vis = (128 * kk + kk_) <= (128 * p + ql)
            dm[:, p * 512 + kk * 128:p * 512 + (kk + 1) * 128] = vis.astype(np.float32)
    iota8 = np.ascontiguousarray(
        np.broadcast_to(np.arange(E, dtype=np.float32)[None, :], (128, E)))
    iota8b = np.ascontiguousarray(iota8 + BIGI)

    nc = _get_nc()
    in_maps = []
    for core in range(8):
        seq, r = core // 4, core % 4
        in_maps.append({
            "role": np.array([[r]], dtype=np.int32),
            "x": np.ascontiguousarray(x[seq]),
            "xr": np.ascontiguousarray(x[seq, BOUNDS[r]:BOUNDS[r] + T]),
            "wq": wq, "wk": wk, "wv": wv, "wo": wo, "bo_b": bo_b,
            "wg": wg, "w1": w1, "w2": w2, "w3": w3, "fbias_b": fb_b,
            "ident": ident, "dmaskt": dm, "iota8": iota8, "iota8b": iota8b,
            "epsc": np.full((128, 1), EPS, dtype=np.float32),
        })
    res = run_bass_kernel_spmd(nc, in_maps, core_ids=list(range(8)))
    out = np.empty((2, S, D), dtype=np.float32)
    for core in range(8):
        seq, r = core // 4, core % 4
        out[seq, BOUNDS[r]:BOUNDS[r] + T] = res.results[core]["out"]
    return out


# version 07
# speedup vs baseline: 1.2500x; 1.1731x over previous
import math
from contextlib import ExitStack

import numpy as np

import concourse.bass as bass
import concourse.mybir as mybir
from concourse.alu_op_type import AluOpType as OP
from concourse.bass_utils import run_bass_kernel_spmd

F32 = mybir.dt.float32
BF16 = mybir.dt.bfloat16
I32 = mybir.dt.int32
AF = mybir.ActivationFunctionType
AX = mybir.AxisListType.X

D = 2048
H = 16
HD = 128
S = 4096
NROLE = 4
BOUNDS = [0, 1024, 2048, 3072, 4096]
T = 1024
NQT = T // 128
FFN = 1024
E = 8
SCALE = float(1.0 / math.sqrt(HD))
EPS = 1e-5
NEG = -1e30
BIGI = 65536.0
BIGS = 1e6
VW = H * 129


def _nj(role, qb):
    return (BOUNDS[role] + qb * 512) // 512 + 1


def build_nc(phases=99, dbg=False):
    nc = bass.Bass("TRN2", target_bir_lowering=False, debug=False,
                   detect_race_conditions=False)
    ap = lambda t: t.ap()
    role_t = ap(nc.dram_tensor("role", [1, 1], I32, kind="ExternalInput"))
    x_t = ap(nc.dram_tensor("x", [S, D], F32, kind="ExternalInput"))
    xr_t = ap(nc.dram_tensor("xr", [T, D], F32, kind="ExternalInput"))
    wq_t = ap(nc.dram_tensor("wq", [D, D], F32, kind="ExternalInput"))
    wk_t = ap(nc.dram_tensor("wk", [D, D], F32, kind="ExternalInput"))
    wv_t = ap(nc.dram_tensor("wv", [D, D], F32, kind="ExternalInput"))
    wo_t = ap(nc.dram_tensor("wo", [D, D], F32, kind="ExternalInput"))
    bo_t = ap(nc.dram_tensor("bo_b", [128, D], F32, kind="ExternalInput"))
    wg_t = ap(nc.dram_tensor("wg", [E, D], F32, kind="ExternalInput"))
    w1_t = ap(nc.dram_tensor("w1", [E, FFN, D], F32, kind="ExternalInput"))
    w2_t = ap(nc.dram_tensor("w2", [E, D, FFN], F32, kind="ExternalInput"))
    w3_t = ap(nc.dram_tensor("w3", [E, FFN, D], F32, kind="ExternalInput"))
    fb_t = ap(nc.dram_tensor("fbias_b", [128, E], F32, kind="ExternalInput"))
    id_t = ap(nc.dram_tensor("ident", [128, 128], F32, kind="ExternalInput"))
    dm_t = ap(nc.dram_tensor("dmaskt", [128, 4 * 512], F32, kind="ExternalInput"))
    io_t = ap(nc.dram_tensor("iota8", [128, E], F32, kind="ExternalInput"))
    ib_t = ap(nc.dram_tensor("iota8b", [128, E], F32, kind="ExternalInput"))
    ep_t = ap(nc.dram_tensor("epsc", [128, 1], F32, kind="ExternalInput"))
    out_t = ap(nc.dram_tensor("out", [T, D], F32, kind="ExternalOutput"))
    SCR = "ExternalOutput" if dbg else "Internal"
    kscr = ap(nc.dram_tensor("k_scr", [8, 128, H * 512], BF16, kind=SCR))
    vscr = ap(nc.dram_tensor("v_scr", [8, 128, 4 * VW], BF16, kind=SCR))
    qscr = ap(nc.dram_tensor("q_scr", [2, 128, H * 512], BF16, kind=SCR))
    cscr = ap(nc.dram_tensor("ctx_scr", [NQT, 128, VW], F32, kind=SCR))
    yscr = ap(nc.dram_tensor("y1_scr", [NQT, 128, D], F32, kind=SCR))

    with ExitStack() as es:
        role_sb = es.enter_context(nc.sbuf_tensor("role_sb", [1, 1], I32))
        identb = es.enter_context(nc.sbuf_tensor("identb", [128, 128], BF16))
        dmaskt = es.enter_context(nc.sbuf_tensor("dmaskt_sb", [128, 4 * 512], BF16))
        iota = es.enter_context(nc.sbuf_tensor("iota_sb", [128, E], F32))
        iotab = es.enter_context(nc.sbuf_tensor("iotab_sb", [128, E], F32))
        fbias = es.enter_context(nc.sbuf_tensor("fbias_sb", [128, E], F32))
        epssb = es.enter_context(nc.sbuf_tensor("eps_sb", [128, 1], F32))
        pA = es.enter_context(nc.psum_tensor("pA", [128, 512], F32))
        pB = es.enter_context(nc.psum_tensor("pB", [128, 512], F32))
        pC = es.enter_context(nc.psum_tensor("pC", [128, 512], F32))
        pD = es.enter_context(nc.psum_tensor("pD", [128, 512], F32))
        pE = es.enter_context(nc.psum_tensor("pE", [128, 512], F32))
        pF = es.enter_context(nc.psum_tensor("pF", [128, 512], F32))
        ptA = es.enter_context(nc.psum_tensor("ptA", [128, 1024], BF16))
        ptB = es.enter_context(nc.psum_tensor("ptB", [128, 1024], BF16))
        ptb = [ptA, ptB]
        SEMN = 16
        sems = [es.enter_context(nc.semaphore(f"s{i}")) for i in range(SEMN)]
        sw = [es.enter_context(nc.semaphore(f"sw{i}")) for i in range(10)]

        with nc.Block() as blk:
            @blk.sync
            def _(s):
                s.dma_start(out=role_sb[:, :], in_=role_t[:, :]).then_inc(sems[0], 16)
                s.dma_start(out=iota[:, :], in_=io_t[:, :]).then_inc(sems[0], 16)
                s.dma_start(out=iotab[:, :], in_=ib_t[:, :]).then_inc(sems[0], 16)
                s.dma_start(out=fbias[:, :], in_=fb_t[:, :]).then_inc(sems[0], 16)
                s.dma_start(out=epssb[:, :], in_=ep_t[:, :]).then_inc(sems[0], 16)
                s.wait_ge(sems[0], 80)
                s.wait_ge(sw[0], 32)
            @blk.gpsimd
            def _(g):
                g.dma_start(out=identb[:, :], in_=id_t[:, :]).then_inc(sw[0], 16)
                g.dma_start(out=dmaskt[:, :], in_=dm_t[:, :]).then_inc(sw[0], 16)

        regs = {}
        with nc.Block() as blk:
            def _mk(name):
                def _(eng):
                    r = eng.alloc_register("role_r")
                    eng.reg_load(r, role_sb[0:1, 0:1])
                    regs[name] = r
                return _
            blk.tensor(_mk("t")); blk.vector(_mk("v")); blk.scalar(_mk("c"))
            blk.sync(_mk("s")); blk.gpsimd(_mk("g"))

        def clear_block():
            with nc.Block() as blk:
                @blk.sync
                def _(s):
                    for sm in sems:
                        s.sem_clear(sm)

        clear_block()
        if phases <= 0:
            return nc


        with ExitStack() as p1:
            wkb = p1.enter_context(nc.sbuf_tensor("wkb", [128, 16 * D], BF16))
            wvb = p1.enter_context(nc.sbuf_tensor("wvb", [128, 16 * D], BF16))
            xbf = p1.enter_context(nc.sbuf_tensor("xbf", [128, 2 * D], BF16))
            xfm = p1.enter_context(nc.sbuf_tensor("xfm", [128, 16 * 512], BF16))
            ksb = p1.enter_context(nc.sbuf_tensor("ksb", [128, H * 512], BF16))
            vsb = p1.enter_context(nc.sbuf_tensor("vsb", [128, 4 * VW], BF16))

            with nc.Block() as blk:
                @blk.gpsimd
                def _(g):
                    for cc in range(16):
                        g.dma_start(out=wkb[:, cc * D:(cc + 1) * D],
                                    in_=wk_t[cc * 128:(cc + 1) * 128, :]).then_inc(sw[1], 16)
                    for cc in range(16):
                        g.dma_start(out=wvb[:, cc * D:(cc + 1) * D],
                                    in_=wv_t[cc * 128:(cc + 1) * 128, :]).then_inc(sw[1], 16)
                @blk.sync
                def _(s):
                    s.wait_ge(sw[1], 512)
            clear_block()

            sXd, sTx, sVx, sTk, sVk, sTv, sVv, sSt = sems[0:8]

            def proj_gpsimd(g, tok0, nch, sx):
                for i in range(4 * nch):
                    if i >= 2:
                        g.wait_ge(sTx, 2 * (i - 1))
                    r0 = tok0 + i * 128
                    g.dma_start(out=xbf[:, (i % 2) * D:(i % 2 + 1) * D],
                                in_=x_t[r0:r0 + 128, :]).then_inc(sx, 16)

            def proj_x_tensor(g, c, sx):
                for ts in range(4):
                    i = 4 * c + ts
                    g.wait_ge(sx, 16 * (i + 1))
                    for g2 in range(2):
                        k = 2 * i + g2
                        if k >= 2:
                            g.wait_ge(sVx, k - 1)
                        for j in range(8):
                            cc = 8 * g2 + j
                            r = g.transpose(
                                ptb[k % 2][:, (cc % 8) * 128:(cc % 8) * 128 + 128],
                                xbf[:, (i % 2) * D + cc * 128:(i % 2) * D + cc * 128 + 128],
                                identb[:, :])
                        r.then_inc(sTx, 1)

            def proj_x_vector(g, c):
                for ts in range(4):
                    i = 4 * c + ts
                    for g2 in range(2):
                        k = 2 * i + g2
                        g.wait_ge(sTx, k + 1)
                        r = g.tensor_copy(
                            bass.AP(xfm, 8 * g2 * 512 + ts * 128,
                                    [[16 * 512, 128], [512, 8], [1, 128]]),
                            ptb[k % 2][:, 0:1024].rearrange("p (b c) -> p b c", c=128))
                        r.then_inc(sVx, 1)

            def kproj_tensor(g, c, wbuf):
                g.wait_ge(sVx, 8 * (c + 1))
                for h in range(H):
                    j = 16 * c + h
                    if j >= 2:
                        g.wait_ge(sVk, j - 1)
                    p = [pA, pB][j % 2]
                    for cc in range(16):
                        r = g.matmul(p[:, :],
                                     wbuf[:, cc * D + h * 128:cc * D + h * 128 + 128],
                                     xfm[:, cc * 512:(cc + 1) * 512],
                                     start=(cc == 0), stop=(cc == 15))
                    r.then_inc(sTk, 1)

            def kproj_vector(g, c):
                for h in range(H):
                    j = 16 * c + h
                    g.wait_ge(sTk, j + 1)
                    g.tensor_copy(ksb[:, h * 512:(h + 1) * 512],
                                  [pA, pB][j % 2][:, :]).then_inc(sVk, 1)

            def p1_tensor(g, nch):
                for c in range(nch):
                    proj_x_tensor(g, c, sw[2])
                    kproj_tensor(g, c, wkb)
                    for ts in range(4):
                        for gp in range(4):
                            j = 16 * c + 4 * ts + gp
                            if j >= 2:
                                g.wait_ge(sVv, j - 1)
                            p = [pC, pD][j % 2]
                            for cc in range(16):
                                r = g.matmul(
                                    p[:, :],
                                    xfm[:, cc * 512 + ts * 128:cc * 512 + ts * 128 + 128],
                                    wvb[:, cc * D + gp * 512:cc * D + gp * 512 + 512],
                                    start=(cc == 0), stop=(cc == 15))
                            r.then_inc(sTv, 1)

            def p1_vector(g, nch):
                for ts in range(4):
                    g.memset(bass.AP(vsb, ts * VW + 128,
                                     [[4 * VW, 128], [129, H], [1, 1]]), 1.0)
                for c in range(nch):
                    if c > 0:
                        g.wait_ge(sTk, 16 * c)
                        g.wait_ge(sTv, 16 * c)
                        g.wait_ge(sSt, 32 * c)
                    proj_x_vector(g, c)
                    kproj_vector(g, c)
                    for ts in range(4):
                        for gp in range(4):
                            j = 16 * c + 4 * ts + gp
                            g.wait_ge(sTv, j + 1)
                            r = g.tensor_copy(
                                bass.AP(vsb, ts * VW + 4 * gp * 129,
                                        [[4 * VW, 128], [129, 4], [1, 128]]),
                                [pC, pD][j % 2][:, :].rearrange("p (b c) -> p b c", c=128))
                            r.then_inc(sVv, 1)

            def p1_sync(g, nch):
                for c in range(nch):
                    g.wait_ge(sVk, 16 * (c + 1))
                    g.wait_ge(sVv, 16 * (c + 1))
                    g.dma_start(out=kscr[c], in_=ksb[:, :]).then_inc(sSt, 16)
                    g.dma_start(out=vscr[c], in_=vsb[:, :]).then_inc(sSt, 16)
                g.wait_ge(sSt, 32 * nch)

            with nc.Block() as blk:
                @blk.gpsimd
                def _(g):
                    for role in range(NROLE):
                        with g.If_eq(regs["g"], role):
                            proj_gpsimd(g, 0, 2 * (role + 1), sw[2])
                @blk.tensor
                def _(t):
                    for role in range(NROLE):
                        with t.If_eq(regs["t"], role):
                            p1_tensor(t, 2 * (role + 1))
                @blk.vector
                def _(v):
                    for role in range(NROLE):
                        with v.If_eq(regs["v"], role):
                            p1_vector(v, 2 * (role + 1))
                @blk.sync
                def _(s):
                    for role in range(NROLE):
                        with s.If_eq(regs["s"], role):
                            p1_sync(s, 2 * (role + 1))
            clear_block()
            if phases <= 1:
                return nc

            with nc.Block() as blk:
                @blk.gpsimd
                def _(g):
                    for cc in range(16):
                        g.dma_start(out=wkb[:, cc * D:(cc + 1) * D],
                                    in_=wq_t[cc * 128:(cc + 1) * 128, :]).then_inc(sw[3], 16)
                @blk.sync
                def _(s):
                    s.wait_ge(sw[3], 256)
            clear_block()

            with nc.Block() as blk:
                @blk.gpsimd
                def _(g):
                    for role in range(NROLE):
                        with g.If_eq(regs["g"], role):
                            proj_gpsimd(g, BOUNDS[role], 2, sw[4])
                @blk.tensor
                def _(t):
                    for c in range(2):
                        proj_x_tensor(t, c, sw[4])
                        kproj_tensor(t, c, wkb)
                @blk.vector
                def _(v):
                    for c in range(2):
                        if c > 0:
                            v.wait_ge(sTk, 16 * c)
                            v.wait_ge(sSt, 16 * c)
                        proj_x_vector(v, c)
                        kproj_vector(v, c)
                @blk.sync
                def _(s):
                    for c in range(2):
                        s.wait_ge(sVk, 16 * (c + 1))
                        s.dma_start(out=qscr[c], in_=ksb[:, :]).then_inc(sSt, 16)
                    s.wait_ge(sSt, 32)
            clear_block()
            if phases <= 2:
                return nc

        with ExitStack() as pw:
            y1fm = pw.enter_context(nc.sbuf_tensor("y1fm", [128, 16 * T], BF16,
                                                   side="right"))
            with ExitStack() as p2:
                qbuf = p2.enter_context(nc.sbuf_tensor("qbuf", [128, H * 512], BF16))
                kbuf = p2.enter_context(nc.sbuf_tensor("kbuf", [128, 2 * H * 512], BF16))
                vbuf = p2.enter_context(nc.sbuf_tensor("vbuf", [128, 2 * 4 * VW], BF16))
                psb = p2.enter_context(nc.sbuf_tensor("psb", [128, 3 * 512], BF16))
                ptsb = p2.enter_context(nc.sbuf_tensor("ptsb", [128, 3 * 512], BF16))
                ctx = p2.enter_context(nc.sbuf_tensor("ctx", [128, 4 * VW], F32))

                sK, sTs, sVm, sAe, sTt, sVp, sTc, sVc, sSt2 = sems[0:9]

                def p2_tiles(role):
                    out = []
                    u = 0
                    for qb in range(2):
                        nj = _nj(role, qb)
                        for j in range(nj):
                            for qt in range(4):
                                for h in range(H):
                                    out.append((u, qb, j, qt, h, j == nj - 1,
                                                qt == 0 and h == 0,
                                                j == 0 and qt == 0 and h == 0))
                                    u += 1
                    return out

                def p2_sync(g, role):
                    kl = 0
                    slot_tenant_end = {0: None, 1: None}
                    tiles = 0
                    for qb in range(2):
                        nj = _nj(role, qb)
                        if qb == 1:
                            g.wait_ge(sTs, tiles)
                        g.dma_start(out=qbuf[:, :], in_=qscr[qb]).then_inc(sK, 16)
                        kl += 16
                        for j in range(nj):
                            sl = j % 2
                            if slot_tenant_end[sl] is not None:
                                g.wait_ge(sTc, slot_tenant_end[sl])
                            g.dma_start(out=kbuf[:, sl * H * 512:(sl + 1) * H * 512],
                                        in_=kscr[j]).then_inc(sK, 16)
                            g.dma_start(out=vbuf[:, sl * 4 * VW:(sl + 1) * 4 * VW],
                                        in_=vscr[j]).then_inc(sK, 16)
                            kl += 32
                            slot_tenant_end[sl] = tiles + 64 * (j + 1)
                        tiles += 64 * nj
                        g.wait_ge(sVc, tiles)
                        for qt in range(4):
                            g.dma_start(out=cscr[4 * qb + qt],
                                        in_=ctx[:, qt * VW:(qt + 1) * VW]).then_inc(sSt2, 16)
                    g.wait_ge(sSt2, 128)
                    g.wait_ge(sK, kl)

                def p2_tensor(g, role):
                    tiles = p2_tiles(role)
                    n = len(tiles)
                    sc = [pA, pB, pE]
                    pv = [pC, pD, pF]

                    def emit_qk(t):
                        u, qb, j, qt, h, diag, foc, foqb = t
                        if foqb:
                            pass
                        if foc:
                            kl = 16 * (qb + 1) + 32 * (j + 1 + (qb * _nj(role, 0)))
                            g.wait_ge(sK, kl)
                        if u >= 3:
                            g.wait_ge(sAe, u - 2)
                        w_ = ((qt + 1) * 128) if diag else 512
                        g.matmul(sc[u % 3][:, 0:w_],
                                 qbuf[:, h * 512 + qt * 128:h * 512 + qt * 128 + 128],
                                 kbuf[:, (j % 2) * H * 512 + h * 512:(j % 2) * H * 512 + h * 512 + w_],
                                 start=True, stop=True).then_inc(sTs, 1)

                    qb_of = [t[1] for t in tiles]
                    emit_qk(tiles[0])
                    if n > 1 and qb_of[1] == qb_of[0]:
                        emit_qk(tiles[1])
                    for t in tiles:
                        u, qb, j, qt, h, diag, foc, foqb = t
                        if foqb and qb == 1:
                            emit_qk(tiles[u])
                            if u + 1 < n:
                                emit_qk(tiles[u + 1])
                        if u + 2 < n and qb_of[u + 2] == qb:
                            emit_qk(tiles[u + 2])
                        nkk = (qt + 1) if diag else 4
                        g.wait_ge(sAe, u + 1)
                        if u >= 2:
                            g.wait_ge(sVp, u - 1)
                        for kk in range(nkk):
                            r = g.transpose(ptb[u % 2][:, kk * 128:kk * 128 + 128],
                                            psb[:, (u % 3) * 512 + kk * 128:(u % 3) * 512 + kk * 128 + 128],
                                            identb[:, :])
                        r.then_inc(sTt, 1)
                        g.wait_ge(sVp, u + 1)
                        if u >= 3:
                            g.wait_ge(sVc, u - 2)
                        pc = pv[u % 3]
                        vb = (j % 2) * 4 * VW
                        for kk in range(nkk):
                            r = g.matmul(
                                pc[:, 0:129],
                                ptsb[:, (u % 3) * 512 + kk * 128:(u % 3) * 512 + kk * 128 + 128],
                                vbuf[:, vb + kk * VW + h * 129:vb + kk * VW + (h + 1) * 129],
                                start=(kk == 0), stop=(kk == nkk - 1))
                        r.then_inc(sTc, 1)

                def p2_vector(g, role):
                    pv = [pC, pD, pF]
                    for t in p2_tiles(role):
                        u, qb, j, qt, h, diag, foc, foqb = t
                        if foqb and qb == 1:
                            g.wait_ge(sSt2, 64)
                        g.wait_ge(sTt, u + 1)
                        if u >= 3:
                            g.wait_ge(sTc, u - 2)
                        if diag:
                            w_ = (qt + 1) * 128
                            g.tensor_tensor(ptsb[:, (u % 3) * 512:(u % 3) * 512 + w_],
                                            ptb[u % 2][:, 0:w_],
                                            dmaskt[:, qt * 512:qt * 512 + w_],
                                            op=OP.mult).then_inc(sVp, 1)
                        else:
                            g.tensor_copy(ptsb[:, (u % 3) * 512:(u % 3) * 512 + 512],
                                          ptb[u % 2][:, 0:512]).then_inc(sVp, 1)
                        g.wait_ge(sTc, u + 1)
                        dst = ctx[:, qt * VW + h * 129:qt * VW + (h + 1) * 129]
                        if j == 0:
                            g.tensor_copy(dst, pv[u % 3][:, 0:129]).then_inc(sVc, 1)
                        else:
                            g.tensor_tensor(dst, pv[u % 3][:, 0:129], dst,
                                            op=OP.add).then_inc(sVc, 1)

                def p2_scalar(g, role):
                    sc = [pA, pB, pE]
                    for t in p2_tiles(role):
                        u, qb, j, qt, h, diag, foc, foqb = t
                        g.wait_ge(sTs, u + 1)
                        if u >= 3:
                            g.wait_ge(sTt, u - 2)
                        w_ = ((qt + 1) * 128) if diag else 512
                        g.activation(psb[:, (u % 3) * 512:(u % 3) * 512 + w_],
                                     sc[u % 3][:, 0:w_], AF.Exp,
                                     bias=0.0, scale=SCALE).then_inc(sAe, 1)

                with nc.Block() as blk:
                    @blk.sync
                    def _(s):
                        for role in range(NROLE):
                            with s.If_eq(regs["s"], role):
                                p2_sync(s, role)
                    @blk.tensor
                    def _(t):
                        for role in range(NROLE):
                            with t.If_eq(regs["t"], role):
                                p2_tensor(t, role)
                    @blk.vector
                    def _(v):
                        for role in range(NROLE):
                            with v.If_eq(regs["v"], role):
                                p2_vector(v, role)
                    @blk.scalar
                    def _(c):
                        for role in range(NROLE):
                            with c.If_eq(regs["c"], role):
                                p2_scalar(c, role)
                clear_block()
                if phases <= 3:
                    return nc

            with ExitStack() as p3:
                wob = p3.enter_context(nc.sbuf_tensor("wob", [128, 16 * D], BF16))
                y1bt = p3.enter_context(nc.sbuf_tensor("y1bt", [128, D], BF16))
                ctxin = p3.enter_context(nc.sbuf_tensor("ctxin", [128, 2 * VW], F32))
                xrb = p3.enter_context(nc.sbuf_tensor("xrb", [128, 2 * D], F32))
                bo_b = p3.enter_context(nc.sbuf_tensor("bo_sb", [128, D], F32))
                ctxnb = p3.enter_context(nc.sbuf_tensor("ctxnb", [128, D], BF16))
                ctxfm = p3.enter_context(nc.sbuf_tensor("ctxfm", [128, D], BF16))
                acc = p3.enter_context(nc.sbuf_tensor("acc3", [128, D], F32))
                sq = p3.enter_context(nc.sbuf_tensor("sq3", [128, D], F32))
                y1t = p3.enter_context(nc.sbuf_tensor("y1t", [128, D], F32))
                rl = p3.enter_context(nc.sbuf_tensor("rl3", [128, H], F32))
                st1 = p3.enter_context(nc.sbuf_tensor("st13", [128, 8], F32))

                (sK, sVn, sTf, sVf, sTo, sVs, sVa, sAq, sVr, sAr, sVy, sAy,
                 sSt3, sVz) = sems[0:14]

                def ln_vector(v, i, acc_ap, sq_ap, out_ap):
                    if i >= 1:
                        v.wait_ge(sAr, 2 * i)
                    v.reduce_sum(st1[:, 0:1], acc_ap, axis=AX).then_inc(sVa, 1)
                    v.wait_ge(sAq, i + 1)
                    v.reduce_sum(st1[:, 1:2], sq_ap, axis=AX)
                    v.tensor_scalar(st1[:, 2:3], st1[:, 0:1], 1.0 / D, None, op0=OP.mult)
                    v.tensor_scalar(st1[:, 3:4], st1[:, 1:2], 1.0 / D, None, op0=OP.mult)
                    v.tensor_copy(st1[:, 4:5], st1[:, 0:1])
                    v.scalar_tensor_tensor(st1[:, 5:6], st1[:, 2:3], st1[:, 2:3],
                                           st1[:, 3:4], op0=OP.mult,
                                           op1=OP.subtract).then_inc(sVr, 1)
                    v.wait_ge(sAr, 2 * (i + 1))
                    v.tensor_scalar(out_ap, acc_ap, st1[:, 2:3], st1[:, 6:7],
                                    op0=OP.subtract, op1=OP.mult).then_inc(sVy, 1)

                def ln_scalar(c, i, acc_ap, sq_ap):
                    c.wait_ge(sVa, i + 1)
                    c.activation(sq_ap, acc_ap, AF.Square, bias=0.0, scale=1.0).then_inc(sAq, 1)
                    c.wait_ge(sVr, i + 1)
                    c.activation(st1[:, 4:5], st1[:, 5:6], AF.Ln,
                                 bias=epssb[:, :], scale=-1.0).then_inc(sAr, 1)
                    c.wait_ge(sAr, 2 * i + 1)
                    c.activation(st1[:, 6:7], st1[:, 4:5], AF.Exp,
                                 bias=0.0, scale=-0.5).then_inc(sAr, 1)

                with nc.Block() as blk:
                    @blk.gpsimd
                    def _(g):
                        for cc in range(16):
                            g.dma_start(out=wob[:, cc * D:(cc + 1) * D],
                                        in_=wo_t[cc * 128:(cc + 1) * 128, :]).then_inc(sw[5], 16)
                        g.wait_ge(sw[5], 256)
                    @blk.sync
                    def _(s):
                        s.dma_start(out=bo_b[:, :], in_=bo_t[:, :]).then_inc(sK, 16)
                        for qt in range(NQT):
                            if qt >= 1:
                                s.wait_ge(sVy, qt)
                                s.dma_start(out=yscr[qt - 1],
                                            in_=y1t[:, :]).then_inc(sSt3, 16)
                            if qt >= 2:
                                s.wait_ge(sVn, qt - 1)
                                s.wait_ge(sVs, 5 * (qt - 1))
                            s.dma_start(out=ctxin[:, (qt % 2) * VW:(qt % 2 + 1) * VW],
                                        in_=cscr[qt]).then_inc(sK, 16)
                            s.dma_start(out=xrb[:, (qt % 2) * D:(qt % 2 + 1) * D],
                                        in_=xr_t[qt * 128:(qt + 1) * 128, :]).then_inc(sK, 16)
                        s.wait_ge(sVy, NQT)
                        s.dma_start(out=yscr[NQT - 1], in_=y1t[:, :]).then_inc(sSt3, 16)
                        s.wait_ge(sSt3, 16 * NQT)
                        s.wait_ge(sK, 16 + 32 * NQT)
                    @blk.vector
                    def _(v):
                        for qt in range(NQT):
                            v.wait_ge(sK, 16 + 32 * (qt + 1))
                            if qt >= 1:
                                v.wait_ge(sTf, 2 * qt)
                            cb = (qt % 2) * VW
                            for h in range(H):
                                v.reciprocal(rl[:, h:h + 1],
                                             ctxin[:, cb + h * 129 + 128:cb + h * 129 + 129])
                            for h in range(H):
                                r = v.tensor_scalar(ctxnb[:, h * 128:(h + 1) * 128],
                                                    ctxin[:, cb + h * 129:cb + h * 129 + 128],
                                                    rl[:, h:h + 1], None, op0=OP.mult)
                            r.then_inc(sVn, 1)
                            if qt >= 1:
                                v.wait_ge(sTo, 4 * qt)
                            for g2 in range(2):
                                k = 2 * qt + g2
                                v.wait_ge(sTf, k + 1)
                                v.tensor_copy(
                                    bass.AP(ctxfm, 8 * g2 * 128,
                                            [[D, 128], [128, 8], [1, 128]]),
                                    ptb[k % 2][:, 0:1024].rearrange("p (b c) -> p b c", c=128)
                                ).then_inc(sVf, 1)
                            for gp in range(4):
                                v.wait_ge(sTo, 4 * qt + gp + 1)
                                v.tensor_tensor(acc[:, gp * 512:(gp + 1) * 512],
                                                [pE, pF][gp % 2][:, :],
                                                bo_b[:, gp * 512:(gp + 1) * 512],
                                                op=OP.add).then_inc(sVs, 1)
                            r = v.tensor_tensor(acc[:, :], acc[:, :],
                                                xrb[:, (qt % 2) * D:(qt % 2 + 1) * D],
                                                op=OP.add)
                            r.then_inc(sVs, 1)
                            if qt >= 1:
                                v.wait_ge(sSt3, 16 * qt)
                            ln_vector(v, qt, acc[:, :], sq[:, :], y1t[:, :])
                    @blk.scalar
                    def _(c):
                        for qt in range(NQT):
                            ln_scalar(c, qt, acc[:, :], sq[:, :])
                    @blk.tensor
                    def _(t):
                        t.wait_ge(sw[5], 256)
                        for qt in range(NQT):
                            t.wait_ge(sVn, qt + 1)
                            for g2 in range(2):
                                k = 2 * qt + g2
                                if k >= 2:
                                    t.wait_ge(sVf, k - 1)
                                for j in range(8):
                                    cc = 8 * g2 + j
                                    r = t.transpose(
                                        ptb[k % 2][:, (cc % 8) * 128:(cc % 8) * 128 + 128],
                                        ctxnb[:, cc * 128:cc * 128 + 128],
                                        identb[:, :])
                                r.then_inc(sTf, 1)
                            t.wait_ge(sVf, 2 * (qt + 1))
                            for gp in range(4):
                                i = 4 * qt + gp
                                if i >= 2:
                                    t.wait_ge(sVs, 5 * ((i - 2) // 4) + ((i - 2) % 4) + 1)
                                p = [pE, pF][gp % 2]
                                for cc in range(16):
                                    r = t.matmul(p[:, :], ctxfm[:, cc * 128:cc * 128 + 128],
                                                 wob[:, cc * D + gp * 512:cc * D + gp * 512 + 512],
                                                 start=(cc == 0), stop=(cc == 15))
                                r.then_inc(sTo, 1)
                clear_block()
                if phases <= 4:
                    return nc

                sK2, sTx2, sVx2 = sems[0:3]
                with nc.Block() as blk:
                    @blk.gpsimd
                    def _(g):
                        for qt in range(NQT):
                            if qt >= 1:
                                g.wait_ge(sTx2, 2 * qt)
                            g.dma_start(out=y1bt[:, :], in_=yscr[qt]).then_inc(sw[6], 16)
                        g.wait_ge(sw[6], 16 * NQT)
                    @blk.tensor
                    def _(t):
                        for qt in range(NQT):
                            t.wait_ge(sw[6], 16 * (qt + 1))
                            for g2 in range(2):
                                k = 2 * qt + g2
                                if k >= 2:
                                    t.wait_ge(sVx2, k - 1)
                                for j in range(8):
                                    cc = 8 * g2 + j
                                    r = t.transpose(
                                        ptb[k % 2][:, (cc % 8) * 128:(cc % 8) * 128 + 128],
                                        y1bt[:, cc * 128:cc * 128 + 128],
                                        identb[:, :])
                                r.then_inc(sTx2, 1)
                    @blk.vector
                    def _(v):
                        for qt in range(NQT):
                            for g2 in range(2):
                                k = 2 * qt + g2
                                v.wait_ge(sTx2, k + 1)
                                v.tensor_copy(
                                    bass.AP(y1fm, (8 * g2) * T + qt * 128,
                                            [[16 * T, 128], [T, 8], [1, 128]]),
                                    ptb[k % 2][:, 0:1024].rearrange("p (b c) -> p b c", c=128)
                                ).then_inc(sVx2, 1)
                    @blk.sync
                    def _(s):
                        s.wait_ge(sVx2, 2 * NQT)
                clear_block()
                if phases <= 5:
                    return nc

            accum = pw.enter_context(nc.sbuf_tensor("accum", [128, NQT * D], BF16,
                                                    side="right"))
            ewbuf = pw.enter_context(nc.sbuf_tensor("ew4", [128, NQT * E], F32,
                                                    side="right"))
            with ExitStack() as p4:
                w2t = p4.enter_context(nc.sbuf_tensor("w2t", [128, 8 * D], BF16))
                w1tb = p4.enter_context(nc.sbuf_tensor("w1tb", [128, 2 * D], BF16))
                w3tb = p4.enter_context(nc.sbuf_tensor("w3tb", [128, 2 * D], BF16))
                wsl = p4.enter_context(nc.sbuf_tensor("wsl", [128, 2 * D], BF16))
                wsl2 = p4.enter_context(nc.sbuf_tensor("wsl2", [128, 2 * FFN], BF16))
                gsb = p4.enter_context(nc.sbuf_tensor("gsb", [128, 8 * T], BF16))
                h1s = p4.enter_context(nc.sbuf_tensor("h1s4", [128, 2 * 512], F32))
                rw = p4.enter_context(nc.sbuf_tensor("rw4", [128, NQT * E], F32))
                wgf = p4.enter_context(nc.sbuf_tensor("wgf4", [128, 16 * E], BF16))
                wgb = p4.enter_context(nc.sbuf_tensor("wgb4", [8, D], BF16))
                tkb = p4.enter_context(nc.sbuf_tensor("tkb4", [128, NQT * 64], F32))
                tmpo = p4.enter_context(nc.sbuf_tensor("tmpo4", [128, 2 * 512], BF16))

                (sK, sTw, sVw, sTr, sAs, sVe, sX1, sX2, sTh, sAs2, sVg, sTo2,
                 sAo, sVo) = sems[0:14]

                with nc.Block() as blk:
                    @blk.gpsimd
                    def _(g):
                        g.dma_start(out=wgb[:, :], in_=wg_t[:, :]).then_inc(sw[7], 16)
                        g.memset(accum[:, :], 0.0)
                        g.wait_ge(sw[7], 16)
                    @blk.tensor
                    def _(t):
                        t.wait_ge(sw[7], 16)
                        for cc in range(16):
                            if cc >= 2:
                                t.wait_ge(sVw, cc - 1)
                            r = t.transpose(ptb[cc % 2][:, 0:8],
                                            wgb[0:8, cc * 128:(cc + 1) * 128],
                                            identb[0:8, 0:8])
                            r.then_inc(sTw, 1)
                        t.wait_ge(sVw, 16)
                        for qt in range(NQT):
                            if qt >= 2:
                                t.wait_ge(sAs, qt - 1)
                            p = [pA, pB][qt % 2]
                            for cc in range(16):
                                r = t.matmul(p[:, 0:8],
                                             y1fm[:, cc * T + qt * 128:cc * T + qt * 128 + 128],
                                             wgf[:, cc * 8:(cc + 1) * 8],
                                             start=(cc == 0), stop=(cc == 15))
                            r.then_inc(sTr, 1)
                    @blk.scalar
                    def _(c):
                        for qt in range(NQT):
                            c.wait_ge(sTr, qt + 1)
                            c.activation(rw[:, qt * E:(qt + 1) * E],
                                         [pA, pB][qt % 2][:, 0:8],
                                         AF.Sigmoid, bias=0.0, scale=1.0).then_inc(sAs, 1)
                    @blk.vector
                    def _(v):
                        for cc in range(16):
                            v.wait_ge(sTw, cc + 1)
                            v.tensor_copy(wgf[:, cc * 8:(cc + 1) * 8],
                                          ptb[cc % 2][:, 0:8]).then_inc(sVw, 1)
                        v.wait_ge(sAs, NQT)
                        C = 64
                        def col(q, a, b=None):
                            b = a + 1 if b is None else b
                            return tkb[:, q * C + a:q * C + b]
                        rwq = lambda q: rw[:, q * E:(q + 1) * E]
                        for q in range(NQT):
                            v.tensor_tensor(col(q, 0, 8), rwq(q), fbias[:, :], op=OP.add)
                        for q in range(NQT):
                            v.reduce_max(col(q, 48), col(q, 0, 8), axis=AX)
                        for q in range(NQT):
                            v.tensor_scalar(col(q, 8, 16), col(q, 0, 8), col(q, 48), None, op0=OP.is_ge)
                        for q in range(NQT):
                            v.scalar_tensor_tensor(col(q, 16, 24), col(q, 8, 16), -BIGI,
                                                   iotab[:, :], op0=OP.mult, op1=OP.add)
                        for q in range(NQT):
                            v.tensor_reduce(col(q, 49), col(q, 16, 24), axis=AX, op=OP.min)
                        for q in range(NQT):
                            v.tensor_scalar(col(q, 24, 32), iota[:, :], col(q, 49), None, op0=OP.is_equal)
                        for q in range(NQT):
                            v.tensor_tensor(col(q, 16, 24), col(q, 24, 32), rwq(q), op=OP.mult)
                        for q in range(NQT):
                            v.reduce_sum(col(q, 50), col(q, 16, 24), axis=AX)
                        for q in range(NQT):
                            v.scalar_tensor_tensor(col(q, 40, 48), col(q, 24, 32), -BIGS,
                                                   col(q, 0, 8), op0=OP.mult, op1=OP.add)
                        for q in range(NQT):
                            v.reduce_max(col(q, 51), col(q, 40, 48), axis=AX)
                        for q in range(NQT):
                            v.tensor_scalar(col(q, 8, 16), col(q, 40, 48), col(q, 51), None, op0=OP.is_ge)
                        for q in range(NQT):
                            v.scalar_tensor_tensor(col(q, 16, 24), col(q, 8, 16), -BIGI,
                                                   iotab[:, :], op0=OP.mult, op1=OP.add)
                        for q in range(NQT):
                            v.tensor_reduce(col(q, 52), col(q, 16, 24), axis=AX, op=OP.min)
                        for q in range(NQT):
                            v.tensor_scalar(col(q, 32, 40), iota[:, :], col(q, 52), None, op0=OP.is_equal)
                        for q in range(NQT):
                            v.tensor_tensor(col(q, 16, 24), col(q, 32, 40), rwq(q), op=OP.mult)
                        for q in range(NQT):
                            v.reduce_sum(col(q, 53), col(q, 16, 24), axis=AX)
                        for q in range(NQT):
                            v.tensor_tensor(col(q, 54), col(q, 50), col(q, 53), op=OP.add)
                        for q in range(NQT):
                            v.reciprocal(col(q, 55), col(q, 54))
                        for q in range(NQT):
                            v.tensor_tensor(col(q, 56), col(q, 50), col(q, 55), op=OP.mult)
                        for q in range(NQT):
                            v.tensor_tensor(col(q, 57), col(q, 53), col(q, 55), op=OP.mult)
                        for q in range(NQT):
                            v.tensor_scalar(col(q, 16, 24), col(q, 24, 32), col(q, 56), None, op0=OP.mult)
                        for q in range(NQT):
                            v.scalar_tensor_tensor(ewbuf[:, q * E:(q + 1) * E],
                                                   col(q, 32, 40), col(q, 57),
                                                   col(q, 16, 24), op0=OP.mult,
                                                   op1=OP.add).then_inc(sVe, 1)
                        v.wait_ge(sVe, NQT)
                clear_block()
                if phases <= 6:
                    return nc

                (sX1, sTw, sVw, sX2, sTw2, sVw2, sTh, sAs2, sVg, sTo2,
                 sAo, sVo, sPad1, sPad2) = sems[0:14]

                def _w13_gend(w):
                    e2, r = divmod(w, 16)
                    fb2, wi2 = divmod(r, 2)
                    return 48 * e2 + 6 * fb2 + 2 + 2 * wi2

                def _w2_gend(s2):
                    e2, r = divmod(s2, 16)
                    return 48 * e2 + 6 * (r // 2) + 4 + (r % 2) + 1

                def p4_gpsimd(g):
                    for e in range(E):
                        for fb in range(8):
                            for wi, wt in ((0, w1_t), (1, w3_t)):
                                w = 16 * e + 2 * fb + wi
                                if w >= 2:
                                    g.wait_ge(sTw, _w13_gend(w - 2))
                                g.dma_start(out=wsl[:, (w % 2) * D:(w % 2 + 1) * D],
                                            in_=wt[e, fb * 128:(fb + 1) * 128, :]).then_inc(sw[8], 16)
                            for k in range(2):
                                s2 = 16 * e + 2 * fb + k
                                if s2 >= 2:
                                    g.wait_ge(sTw, _w2_gend(s2 - 2))
                                g.dma_start(out=wsl2[:, (s2 % 2) * FFN:(s2 % 2 + 1) * FFN],
                                            in_=w2_t[e, (2 * fb + k) * 128:(2 * fb + k + 1) * 128, :]).then_inc(sw[9], 16)
                    g.wait_ge(sw[8], 16 * 16 * E)
                    g.wait_ge(sw[9], 16 * 16 * E)

                def p4_tensor(g):
                    for e in range(E):
                        for fb in range(8):
                            base = 48 * e + 6 * fb
                            for wi in (0, 1):
                                w = 16 * e + 2 * fb + wi
                                g.wait_ge(sw[8], 16 * (w + 1))
                                for g2 in range(2):
                                    gi = base + 2 * wi + g2
                                    if gi >= 2:
                                        g.wait_ge(sVw, gi - 1)
                                    for j in range(8):
                                        cc = 8 * g2 + j
                                        r = g.transpose(
                                            ptb[gi % 2][:, (cc % 8) * 128:(cc % 8) * 128 + 128],
                                            wsl[:, (w % 2) * D + cc * 128:(w % 2) * D + cc * 128 + 128],
                                            identb[:, :])
                                    r.then_inc(sTw, 1)
                            for k in range(2):
                                s2 = 16 * e + 2 * fb + k
                                gi = base + 4 + k
                                g.wait_ge(sw[9], 16 * (s2 + 1))
                                if gi >= 2:
                                    g.wait_ge(sVw, gi - 1)
                                for j in range(8):
                                    r = g.transpose(
                                        ptb[gi % 2][:, j * 128:j * 128 + 128],
                                        wsl2[:, (s2 % 2) * FFN + j * 128:(s2 % 2) * FFN + j * 128 + 128],
                                        identb[:, :])
                                r.then_inc(sTw, 1)
                            for tb in range(2):
                                m = 32 * e + 4 * fb + 2 * tb
                                g.wait_ge(sVw, base + 2)
                                if m // 2 >= 2:
                                    g.wait_ge(sAs2, m // 2 - 1)
                                for cc in range(16):
                                    r = g.matmul(pA[:, :] if tb == 0 else pC[:, :],
                                                 w1tb[:, (fb % 2) * D + cc * 128:(fb % 2) * D + cc * 128 + 128],
                                                 y1fm[:, cc * T + tb * 512:cc * T + (tb + 1) * 512],
                                                 start=(cc == 0), stop=(cc == 15))
                                r.then_inc(sTh, 1)
                                g.wait_ge(sVw, base + 4)
                                if m // 2 >= 2:
                                    g.wait_ge(sVg, m // 2 - 1)
                                for cc in range(16):
                                    r = g.matmul(pB[:, :] if tb == 0 else pD[:, :],
                                                 w3tb[:, (fb % 2) * D + cc * 128:(fb % 2) * D + cc * 128 + 128],
                                                 y1fm[:, cc * T + tb * 512:cc * T + (tb + 1) * 512],
                                                 start=(cc == 0), stop=(cc == 15))
                                r.then_inc(sTh, 1)
                        g.wait_ge(sVg, 16 * (e + 1))
                        g.wait_ge(sVw, 48 * (e + 1))
                        for qt in range(NQT):
                            for gp in range(4):
                                i = 32 * e + 4 * qt + gp
                                if i >= 2:
                                    g.wait_ge(sAo, i - 1)
                                p = [pE, pF][i % 2]
                                for fc in range(8):
                                    r = g.matmul(
                                        p[:, :],
                                        gsb[:, fc * T + qt * 128:fc * T + qt * 128 + 128],
                                        w2t[:, fc * D + gp * 512:fc * D + gp * 512 + 512],
                                        start=(fc == 0), stop=(fc == 7))
                                r.then_inc(sTo2, 1)

                def p4_scalar(g):
                    for e in range(E):
                        for fb in range(8):
                            for tb in range(2):
                                m = 32 * e + 4 * fb + 2 * tb
                                sl = (m // 2) % 2
                                g.wait_ge(sTh, m + 1)
                                if m // 2 >= 2:
                                    g.wait_ge(sVg, m // 2 - 1)
                                g.activation(h1s[:, sl * 512:(sl + 1) * 512],
                                             pA[:, :] if tb == 0 else pC[:, :],
                                             AF.Silu, bias=0.0, scale=1.0).then_inc(sAs2, 1)
                        for qt in range(NQT):
                            for gp in range(4):
                                i = 32 * e + 4 * qt + gp
                                g.wait_ge(sTo2, i + 1)
                                if i >= 2:
                                    g.wait_ge(sVo, i - 1)
                                g.activation(tmpo[:, (i % 2) * 512:(i % 2 + 1) * 512],
                                             [pE, pF][i % 2][:, :], AF.Copy,
                                             bias=0.0,
                                             scale=ewbuf[:, qt * E + e:qt * E + e + 1]).then_inc(sAo, 1)

                def p4_vector(g):
                    for e in range(E):
                        for fb in range(8):
                            base = 48 * e + 6 * fb
                            for wi, dstb in ((0, w1tb), (1, w3tb)):
                                w = 16 * e + 2 * fb + wi
                                if 8 * e + fb >= 2:
                                    g.wait_ge(sTh, 4 * (8 * e + fb - 1))
                                for g2 in range(2):
                                    gi = base + 2 * wi + g2
                                    g.wait_ge(sTw, gi + 1)
                                    g.tensor_copy(
                                        dstb[:, (fb % 2) * D + g2 * 1024:(fb % 2) * D + (g2 + 1) * 1024],
                                        ptb[gi % 2][:, 0:1024]).then_inc(sVw, 1)
                            for k in range(2):
                                s2 = 16 * e + 2 * fb + k
                                hc = 2 * fb + k
                                gi = base + 4 + k
                                g.wait_ge(sTw, gi + 1)
                                if e > 0 and hc == 0:
                                    g.wait_ge(sTo2, 32 * e)
                                g.tensor_copy(
                                    bass.AP(w2t, hc * 128, [[8 * D, 128], [D, 8], [1, 128]]),
                                    ptb[gi % 2][:, 0:1024].rearrange("p (b c) -> p b c", c=128)
                                ).then_inc(sVw, 1)
                            for tb in range(2):
                                m = 32 * e + 4 * fb + 2 * tb
                                sl = (m // 2) % 2
                                g.wait_ge(sAs2, m // 2 + 1)
                                g.wait_ge(sTh, m + 2)
                                g.tensor_tensor(
                                    gsb[:, fb * T + tb * 512:fb * T + (tb + 1) * 512],
                                    h1s[:, sl * 512:(sl + 1) * 512],
                                    pB[:, :] if tb == 0 else pD[:, :],
                                    op=OP.mult).then_inc(sVg, 1)
                        for qt in range(NQT):
                            for gp in range(4):
                                i = 32 * e + 4 * qt + gp
                                g.wait_ge(sAo, i + 1)
                                dst = accum[:, qt * D + gp * 512:qt * D + (gp + 1) * 512]
                                g.tensor_tensor(dst, tmpo[:, (i % 2) * 512:(i % 2 + 1) * 512],
                                                dst, op=OP.add).then_inc(sVo, 1)


                with nc.Block() as blk:
                    @blk.gpsimd
                    def _(g):
                        p4_gpsimd(g)
                    @blk.tensor
                    def _(t):
                        p4_tensor(t)
                    @blk.scalar
                    def _(c):
                        p4_scalar(c)
                    @blk.vector
                    def _(v):
                        p4_vector(v)
                clear_block()
                if phases <= 7:
                    return nc

            with ExitStack() as p5:
                y1in = p5.enter_context(nc.sbuf_tensor("y1in", [128, 2 * D], F32))
                acc5 = p5.enter_context(nc.sbuf_tensor("acc5", [128, D], F32))
                sq5 = p5.enter_context(nc.sbuf_tensor("sq5", [128, D], F32))
                y2t = p5.enter_context(nc.sbuf_tensor("y2t", [128, 2 * D], F32))
                st1 = p5.enter_context(nc.sbuf_tensor("st15", [128, 8], F32))

                sK, sVa, sAq, sVr, sAr, sVy, sAy, sSt5 = sems[0:8]

                with nc.Block() as blk:
                    @blk.sync
                    def _(s):
                        for qt in range(NQT):
                            if qt >= 1:
                                s.wait_ge(sVy, qt)
                                s.dma_start(out=out_t[(qt - 1) * 128:qt * 128, :],
                                            in_=y2t[:, ((qt - 1) % 2) * D:((qt - 1) % 2 + 1) * D]).then_inc(sSt5, 16)
                            if qt >= 2:
                                s.wait_ge(sVa, qt - 1)
                            s.dma_start(out=y1in[:, (qt % 2) * D:(qt % 2 + 1) * D],
                                        in_=yscr[qt]).then_inc(sK, 16)
                        s.wait_ge(sVy, NQT)
                        s.dma_start(out=out_t[(NQT - 1) * 128:NQT * 128, :],
                                    in_=y2t[:, ((NQT - 1) % 2) * D:((NQT - 1) % 2 + 1) * D]).then_inc(sSt5, 16)
                        s.wait_ge(sSt5, 16 * NQT)
                        s.wait_ge(sK, 16 * NQT)
                    @blk.vector
                    def _(v):
                        for qt in range(NQT):
                            v.wait_ge(sK, 16 * (qt + 1))
                            v.tensor_tensor(acc5[:, :],
                                            accum[:, qt * D:(qt + 1) * D],
                                            y1in[:, (qt % 2) * D:(qt % 2 + 1) * D],
                                            op=OP.add)
                            if qt >= 1:
                                v.wait_ge(sAr, 2 * qt)
                            v.reduce_sum(st1[:, 0:1], acc5[:, :], axis=AX).then_inc(sVa, 1)
                            v.wait_ge(sAq, qt + 1)
                            v.reduce_sum(st1[:, 1:2], sq5[:, :], axis=AX)
                            v.tensor_scalar(st1[:, 2:3], st1[:, 0:1], 1.0 / D, None, op0=OP.mult)
                            v.tensor_scalar(st1[:, 3:4], st1[:, 1:2], 1.0 / D, None, op0=OP.mult)
                            v.tensor_copy(st1[:, 4:5], st1[:, 0:1])
                            v.scalar_tensor_tensor(st1[:, 5:6], st1[:, 2:3], st1[:, 2:3],
                                                   st1[:, 3:4], op0=OP.mult,
                                                   op1=OP.subtract).then_inc(sVr, 1)
                            v.wait_ge(sAr, 2 * (qt + 1))
                            if qt >= 2:
                                v.wait_ge(sSt5, 16 * (qt - 1))
                            v.tensor_scalar(y2t[:, (qt % 2) * D:(qt % 2 + 1) * D],
                                            acc5[:, :], st1[:, 2:3], st1[:, 6:7],
                                            op0=OP.subtract, op1=OP.mult).then_inc(sVy, 1)
                    @blk.scalar
                    def _(c):
                        for qt in range(NQT):
                            c.wait_ge(sVa, qt + 1)
                            c.activation(sq5[:, :], acc5[:, :], AF.Square,
                                         bias=0.0, scale=1.0).then_inc(sAq, 1)
                            c.wait_ge(sVr, qt + 1)
                            c.activation(st1[:, 4:5], st1[:, 5:6], AF.Ln,
                                         bias=epssb[:, :], scale=-1.0).then_inc(sAr, 1)
                            c.wait_ge(sAr, 2 * qt + 1)
                            c.activation(st1[:, 6:7], st1[:, 4:5], AF.Exp,
                                         bias=0.0, scale=-0.5).then_inc(sAr, 1)
    return nc


_NC_CACHE = {}


def _get_nc():
    if "nc" not in _NC_CACHE:
        _NC_CACHE["nc"] = build_nc()
    return _NC_CACHE["nc"]


def kernel(L0_attn_x, L0_attn_Wq, L0_attn_Wk, L0_attn_Wv, L0_attn_Wo,
           L0_attn_bo, L0_ffn_Wg, L0_ffn_W1, L0_ffn_W2, L0_ffn_W3,
           L0_ffn_bias):
    x = np.ascontiguousarray(np.asarray(L0_attn_x, dtype=np.float32))
    wq = np.ascontiguousarray(np.asarray(L0_attn_Wq, dtype=np.float32))
    wk = np.ascontiguousarray(np.asarray(L0_attn_Wk, dtype=np.float32))
    wv = np.ascontiguousarray(np.asarray(L0_attn_Wv, dtype=np.float32))
    wo = np.ascontiguousarray(np.asarray(L0_attn_Wo, dtype=np.float32))
    bo = np.asarray(L0_attn_bo, dtype=np.float32)
    wg = np.ascontiguousarray(np.asarray(L0_ffn_Wg, dtype=np.float32))
    w1 = np.ascontiguousarray(np.asarray(L0_ffn_W1, dtype=np.float32))
    w2 = np.ascontiguousarray(np.asarray(L0_ffn_W2, dtype=np.float32))
    w3 = np.ascontiguousarray(np.asarray(L0_ffn_W3, dtype=np.float32))
    fb = np.asarray(L0_ffn_bias, dtype=np.float32)

    bo_b = np.ascontiguousarray(np.broadcast_to(bo[None, :], (128, D)))
    fb_b = np.ascontiguousarray(np.broadcast_to(fb[None, :], (128, E)))
    ident = np.eye(128, dtype=np.float32)
    kk_ = np.arange(128, dtype=np.int64)[:, None]
    dm = np.zeros((128, 4 * 512), np.float32)
    for p in range(4):
        for kk in range(4):
            ql = np.arange(128, dtype=np.int64)[None, :]
            vis = (128 * kk + kk_) <= (128 * p + ql)
            dm[:, p * 512 + kk * 128:p * 512 + (kk + 1) * 128] = vis.astype(np.float32)
    iota8 = np.ascontiguousarray(
        np.broadcast_to(np.arange(E, dtype=np.float32)[None, :], (128, E)))
    iota8b = np.ascontiguousarray(iota8 + BIGI)

    nc = _get_nc()
    in_maps = []
    for core in range(8):
        seq, r = core // 4, core % 4
        in_maps.append({
            "role": np.array([[r]], dtype=np.int32),
            "x": np.ascontiguousarray(x[seq]),
            "xr": np.ascontiguousarray(x[seq, BOUNDS[r]:BOUNDS[r] + T]),
            "wq": wq, "wk": wk, "wv": wv, "wo": wo, "bo_b": bo_b,
            "wg": wg, "w1": w1, "w2": w2, "w3": w3, "fbias_b": fb_b,
            "ident": ident, "dmaskt": dm, "iota8": iota8, "iota8b": iota8b,
            "epsc": np.full((128, 1), EPS, dtype=np.float32),
        })
    res = run_bass_kernel_spmd(nc, in_maps, core_ids=list(range(8)))
    out = np.empty((2, S, D), dtype=np.float32)
    for core in range(8):
        seq, r = core // 4, core % 4
        out[seq, BOUNDS[r]:BOUNDS[r] + T] = res.results[core]["out"]
    return out
